# version 01
# baseline (speedup 1.0000x reference)
import numpy as np
import concourse.bass as bass
import concourse.mybir as mybir
from concourse.bass_utils import run_bass_kernel_spmd
from contextlib import ExitStack

F32 = mybir.dt.float32
BF16 = mybir.dt.bfloat16
AF = mybir.ActivationFunctionType
ALU = mybir.AluOpType
AX = mybir.AxisListType

B, T, H = 8, 2048, 1024
NL, DI, DS, DTR, KC = 4, 2048, 16, 64, 4
E, IE, ISH = 64, 512, 1024
NE = E + 2
TC = 512
NTC = T // TC
TCF = 256
NTCF = T // TCF
EPS = 1e-5

QUEUES = ("tensor", "vector", "scalar", "gpsimd", "gpsimd_dma", "sync")
DMA_Q = ("gpsimd_dma", "sync")


class Prog:
    def __init__(self):
        self.ops = []
        self.last_writer = {}
        self.last_reader = {}

    SELF_SYNC = ("vector", "scalar", "gpsimd")

    def emit(self, eng, fn, reads=(), writes=(), tag=""):
        gi = len(self.ops)
        selfs = eng in self.SELF_SYNC
        deps = set()
        for r in reads:
            rr = r[:-1] if r.endswith("!") else r
            w = self.last_writer.get(rr)
            if w is not None and (w[0] != eng or selfs):
                deps.add(w[1])
        for r in writes:
            w = self.last_writer.get(r)
            if w is not None and (w[0] != eng or selfs):
                deps.add(w[1])
            for re, ri in self.last_reader.get(r, {}).items():
                if re != eng or selfs:
                    deps.add(ri)
        self.ops.append(dict(eng=eng, fn=fn, gi=gi, deps=deps, tag=tag))
        for r in writes:
            self.last_writer[r] = (eng, gi)
            self.last_reader[r] = {}
        for r in reads:
            rr = r[:-1] if r.endswith("!") else r
            self.last_reader.setdefault(rr, {})[eng] = gi
        return gi

    def barrier(self, tag=""):
        lasts = {}
        for rec in self.ops:
            lasts[rec["eng"]] = rec["gi"]

        def nopfn(h):
            return h.nop()

        for q in ("tensor", "vector", "scalar", "gpsimd", "sync"):
            gi = len(self.ops)
            deps = {g for e, g in lasts.items() if e != q}
            self.ops.append(dict(eng=q, fn=nopfn, gi=gi, deps=deps,
                                 tag=f"bar_{q}_{tag}"))

    def materialize(self, block, sems):
        for rec in self.ops:
            byeng = {}
            for d in rec["deps"]:
                de = self.ops[d]["eng"]
                if byeng.get(de, -1) < d:
                    byeng[de] = d
            rec["deps"] = set(byeng.values())
        needed = set()
        for rec in self.ops:
            needed.update(rec["deps"])
            if rec["eng"] in DMA_Q:
                needed.add(rec["gi"])
        inc_no = {}
        cum = {k: 0 for k in QUEUES}
        for rec in self.ops:
            if rec["gi"] in needed:
                cum[rec["eng"]] += 16 if rec["eng"] in DMA_Q else 1
                inc_no[rec["gi"]] = cum[rec["eng"]]
        streams = {k: [] for k in QUEUES}
        hw = {k: {s: 0 for s in QUEUES} for k in QUEUES}
        for rec in self.ops:
            eng = rec["eng"]
            waits = []
            for d in sorted(rec["deps"]):
                de = self.ops[d]["eng"]
                tgt = inc_no[d]
                if de == eng:
                    waits.append((de, tgt))
                elif hw[eng][de] < tgt:
                    hw[eng][de] = tgt
                    waits.append((de, tgt))
            streams[eng].append((waits, rec["fn"], inc_no.get(rec["gi"]),
                                 rec["tag"], eng))
        self.cum = cum
        self.n_ops = len(self.ops)

        def run(handle, items):
            for waits, fn, inc, tag, myeng in items:
                for (seng, val) in waits:
                    handle.wait_ge(sems[seng], val)
                instr = fn(handle)
                if inc is not None:
                    assert instr is not None, f"need inc on {tag}"
                    instr.then_inc(sems[myeng], 16 if myeng in DMA_Q else 1)

        @block.tensor
        def _(t):
            run(t, streams["tensor"])

        @block.vector
        def _(v):
            run(v, streams["vector"])

        @block.scalar
        def _(s):
            run(s, streams["scalar"])

        @block.gpsimd
        def _(g):
            merged = []
            ig = ic = 0
            for rec in self.ops:
                if rec["eng"] == "gpsimd":
                    merged.append(streams["gpsimd"][ig]); ig += 1
                elif rec["eng"] == "gpsimd_dma":
                    merged.append(streams["gpsimd_dma"][ic]); ic += 1
            run(g, merged)

        @block.sync
        def _(sy):
            run(sy, streams["sync"])
            sy.wait_ge(sems["sync"], cum["sync"])


def build_program(stop_stage=None, dbg_dump=False, n_experts=NE, n_tcf=NTCF):
    nc = bass.Bass("TRN2", target_bir_lowering=False,
                   detect_race_conditions=False)

    d_x = nc.dram_tensor("x", [H, T], F32, kind="ExternalInput").ap()
    d_wff = nc.dram_tensor("wff", [NE, 3, 128, 4096], F32, kind="ExternalInput").ap()
    d_gateT = nc.dram_tensor("gateT", [128, 512], F32, kind="ExternalInput").ap()
    d_gbias = nc.dram_tensor("gbias", [1, E], F32, kind="ExternalInput").ap()
    d_wwin = nc.dram_tensor("wwin", [NL, 128, 32768], F32, kind="ExternalInput").ap()
    d_wblk = nc.dram_tensor("wblk", [NL, 128, 16384], F32, kind="ExternalInput").ap()
    d_wxp = nc.dram_tensor("wxp", [128, NL * 2048], F32, kind="ExternalInput").ap()
    d_wdt = nc.dram_tensor("wdt", [128, NL * 2048], F32, kind="ExternalInput").ap()
    d_wout = nc.dram_tensor("woutp", [128, 8192], F32, kind="ExternalInput").ap()
    d_normw = nc.dram_tensor("normw", [128, NL * 8], F32, kind="ExternalInput").ap()
    d_convw = nc.dram_tensor("convw", [128, NL * 64], F32, kind="ExternalInput").ap()
    d_convb = nc.dram_tensor("convb", [128, NL * 16], F32, kind="ExternalInput").ap()
    d_bdt = nc.dram_tensor("bdt", [128, NL * 16], F32, kind="ExternalInput").ap()
    d_dpv = nc.dram_tensor("dpv", [128, NL * 16], F32, kind="ExternalInput").ap()
    d_alog = nc.dram_tensor("alog", [128, NL * 256], F32, kind="ExternalInput").ap()
    d_bout = nc.dram_tensor("boutc", [128, 8], F32, kind="ExternalInput").ap()
    d_ident = nc.dram_tensor("ident", [128, 128], F32, kind="ExternalInput").ap()
    d_oneh = nc.dram_tensor("oneh", [E, E], F32, kind="ExternalInput").ap()
    d_oneh16 = nc.dram_tensor("oneh16", [48, 16], F32, kind="ExternalInput").ap()

    d_out = nc.dram_tensor("out", [H, T], F32, kind="ExternalOutput").ap()
    d_dbg = (nc.dram_tensor("dbg", [128, 16384], F32, kind="ExternalOutput").ap()
             if dbg_dump else None)
    d_x1 = nc.dram_tensor("x1scr", [H, T], F32).ap()
    d_h = nc.dram_tensor("hscr", [H, T], F32).ap()
    d_comb = nc.dram_tensor("combscr", [E, T], BF16,
                            kind=("ExternalOutput" if dbg_dump else
                                  "Internal")).ap()

    stack = ExitStack()
    en = stack.enter_context

    R0 = en(nc.sbuf_tensor("R0", [128, 16384], F32))
    SCR = en(nc.sbuf_tensor("SCR", [128, 16384], F32))
    WARN = en(nc.sbuf_tensor("WARN", [128, 20480], BF16))
    ROWS = en(nc.sbuf_tensor("ROWS", [1, 2048], F32))
    ROWSB = en(nc.sbuf_tensor("ROWSB", [1, 1024], BF16))
    s_onesb = en(nc.sbuf_tensor("s_onesb", [128, 128], BF16))
    s_identb = en(nc.sbuf_tensor("s_identb", [128, 128], BF16))
    s_gbb = en(nc.sbuf_tensor("s_gbb", [1, E], BF16))
    CARRY = en(nc.sbuf_tensor("CARRY", [128, 256], F32))
    DTBC = en(nc.sbuf_tensor("DTBC", [128, 512], BF16))
    s_wxp = en(nc.sbuf_tensor("s_wxp", [128, 2048], BF16))
    s_wdt = en(nc.sbuf_tensor("s_wdt", [128, 2048], BF16))
    s_gateT = en(nc.sbuf_tensor("s_gateT", [128, 512], F32))
    s_gateTb = en(nc.sbuf_tensor("s_gateTb", [128, 512], BF16))
    s_normw = en(nc.sbuf_tensor("s_normw", [128, NL * 8], F32))
    s_convw = en(nc.sbuf_tensor("s_convw", [128, NL * 64], F32))
    s_convb = en(nc.sbuf_tensor("s_convb", [128, NL * 16], F32))
    s_bdt = en(nc.sbuf_tensor("s_bdt", [128, NL * 16], F32))
    s_dpv = en(nc.sbuf_tensor("s_dpv", [128, NL * 16], F32))
    s_alog = en(nc.sbuf_tensor("s_alog", [128, 256], F32))
    s_A = en(nc.sbuf_tensor("s_A", [128, 256], F32))
    s_bout = en(nc.sbuf_tensor("s_bout", [128, 8], F32))
    s_ident = en(nc.sbuf_tensor("s_ident", [128, 128], F32))
    s_oneh = en(nc.sbuf_tensor("s_oneh", [E, E], BF16))
    s_oneh16 = en(nc.sbuf_tensor("s_oneh16", [48, 16], BF16))
    s_ones = en(nc.sbuf_tensor("s_ones", [128, 128], F32))
    s_gb = en(nc.sbuf_tensor("s_gb", [1, E], F32))
    s_biasb = en(nc.sbuf_tensor("s_biasb", [128, E], F32))
    XHALO = en(nc.sbuf_tensor("XHALO", [128, 48], F32))
    s_const = en(nc.sbuf_tensor("s_const", [128, 8], F32))

    pA = en(nc.psum_tensor("pA", [128, 1024], F32))
    pB = en(nc.psum_tensor("pB", [128, 1024], F32))
    pC = en(nc.psum_tensor("pC", [128, 2048], F32))

    sems = {q: en(nc.semaphore(f"s_{q}")) for q in QUEUES}
    block = en(nc.Block())
    P = Prog()

    R0bf = R0[:, :].bitcast(BF16)
    SCRbf = SCR[:, :].bitcast(BF16)

    def r0x(hc, t0, t1):
        return R0[:, hc * 2048 + t0: hc * 2048 + t1]

    def war(s):
        return WARN[:, s * 4096:(s + 1) * 4096]

    def xnf_t(hc):
        return SCR[:, hc * 256:(hc + 1) * 256]

    def sg_t(ic):
        return SCR[:, 2048 + ic * 256: 2048 + (ic + 1) * 256]

    def uc_t(ic):
        return SCR[:, 3072 + ic * 256: 3072 + (ic + 1) * 256]

    def sh_t(ic):
        return SCRbf[:, 8192 + ic * 256: 8192 + (ic + 1) * 256]

    def combping(par):
        return SCRbf[:, 9216 + par * 2048: 9216 + (par + 1) * 2048]

    RSB = 6656

    def rs(c0, c1):
        return SCR[:, RSB + c0: RSB + c1]

    def xnbf_t(hc, t0, t1):
        return SCRbf[:, 16384 + hc * 2048 + t0: 16384 + hc * 2048 + t1]

    def zs_t(dc):
        return R0bf[:, dc * 512:(dc + 1) * 512]

    def xc_t(dc):
        return R0bf[:, 8192 + dc * 512: 8192 + (dc + 1) * 512]

    def crep_t(n):
        return R0bf[:, 16384 + n * 512: 16384 + (n + 1) * 512]

    def brep_t(n):
        return R0bf[:, 24576 + n * 512: 24576 + (n + 1) * 512]

    def hch(hc):
        return SCR[:, hc * 512:(hc + 1) * 512]

    def xnm_t(hc):
        return SCRbf[:, 8192 + hc * 512: 8192 + (hc + 1) * 512]

    def mp(i, w=512):
        return SCR[:, 6144 + i * 512: 6144 + i * 512 + w]

    def xpb(par):
        c = 13312 + par * 520
        return SCR[:, c: c + 520]

    def ncol(t, c):
        return t[:, c:c + 1]

    P.emit("sync", lambda h: h.dma_start(
        R0.rearrange("p (hc t) -> p hc t", hc=8),
        d_x.rearrange("(hc p) t -> p hc t", p=128)),
        writes=[f"R0.{hc}" for hc in range(8)], tag="ldx")
    for nm, dst, src in [("gateT", s_gateT, d_gateT), ("normw", s_normw, d_normw),
                         ("convw", s_convw, d_convw), ("convb", s_convb, d_convb),
                         ("bdt", s_bdt, d_bdt), ("dpv", s_dpv, d_dpv),
                         ("bout", s_bout, d_bout), ("ident", s_ident, d_ident),
                         ("gb", s_gb, d_gbias)]:
        P.emit("sync", lambda h, dst=dst, src=src: h.dma_start(dst[:, :], src),
               writes=[nm], tag=f"ld_{nm}")
    P.emit("gpsimd_dma", lambda h: h.dma_start(s_oneh[:, :], d_oneh),
           writes=["oneh"])
    P.emit("gpsimd_dma", lambda h: h.dma_start(s_gateTb[:, :], d_gateT),
           writes=["gateTb"])
    P.emit("gpsimd_dma", lambda h: h.dma_start(s_oneh16[:, :], d_oneh16),
           writes=["oneh16"])
    P.emit("vector", lambda h: h.memset(s_ones[:, :], 1.0), writes=["ones"])
    P.emit("vector", lambda h: h.memset(s_onesb[:, :], 1.0), writes=["ones"])
    P.emit("gpsimd_dma", lambda h: h.dma_start(s_identb[:, :], d_ident),
           writes=["identb"])
    P.emit("gpsimd_dma", lambda h: h.dma_start(s_gbb[:, :], d_gbias),
           writes=["gbb"])
    _CONSTS = [EPS, 1.0, 1.0 / H, -0.5, -1.0, 1e-8]
    for ci, cv in enumerate(_CONSTS):
        P.emit("vector", lambda h, ci=ci, cv=cv: h.memset(
            s_const[:, ci:ci + 1], cv), writes=["constap"])
        nc.const_aps.register(F32, cv, s_const[:, ci:ci + 1])
    P.emit("scalar", lambda h: h.copy(ROWS[0:1, 0:1], s_const[0:1, 1:2]),
           reads=["constap"], writes=["row_t"], tag="constfence_s")
    P.emit("vector", lambda h: h.tensor_copy(ROWS[0:1, 1:2], s_const[0:1, 1:2]),
           reads=["constap"], writes=["row_t"], tag="constfence_v")
    P.emit("tensor", lambda h: h.matmul(pA[:, 0:E], s_ones[0:1, 0:128],
                                        s_gb[0:1, :], start=True, stop=True),
           reads=["ones", "gb"], writes=["pA.b0"])
    P.emit("vector", lambda h: h.tensor_copy(s_biasb[:, :], pA[:, 0:E]),
           reads=["pA.b0"], writes=["biasb"])
    P.emit("vector", lambda h: h.memset(SCR[:, :], 0.0), writes=["SCRDUMP"],
           tag="scrinit")
    P.emit("vector", lambda h: h.memset(CARRY[:, :], 0.0), writes=["carryinit"])
    P.emit("vector", lambda h: h.memset(DTBC[:, :], 0.0), writes=["dtbcinit"])
    P.barrier("postinit")

    def layernorm(dst_fn, dst_regions, src_fn, src_regions, W, scratch,
                  scr_regions, also_bf=None, bf_regions=None):
        s0, s1, s2 = scratch
        r0, r1, r2 = scr_regions
        xb = s0.bitcast(BF16)[:, 0:W]
        sqb = s1.bitcast(BF16)[:, 0:W]
        for hc in range(8):
            sap = src_fn(hc)
            P.emit("vector", lambda h, sap=sap, xb=xb: h.tensor_copy(xb, sap),
                   reads=[src_regions[hc]], writes=[r0])
            P.emit("vector", lambda h, sap=sap, sqb=sqb: h.tensor_mul(
                sqb, sap, sap),
                reads=[src_regions[hc]], writes=[r1])
            P.emit("tensor", lambda h, hc=hc, sqb=sqb: h.matmul(
                pB[0:1, 0:W], s_onesb[:, 0:1], sqb,
                start=(hc == 0), stop=(hc == 7)),
                reads=[r1, "ones"], writes=["pB.b0"])
            P.emit("tensor", lambda h, hc=hc, xb=xb: h.matmul(
                pA[0:1, 0:W], s_onesb[:, 0:1], xb,
                start=(hc == 0), stop=(hc == 7)),
                reads=[r0, "ones"], writes=["pA.b0"])
        P.emit("scalar", lambda h: h.activation(
            ROWS[0:1, 0:W], pA[0:1, 0:W], AF.Copy, scale=1.0 / H),
            reads=["pA.b0"], writes=["row_m"])

        P.emit("scalar", lambda h: h.activation(
            ROWS[0:1, 512:512 + W], pB[0:1, 0:W], AF.Copy, scale=1.0 / H),
            reads=["pB.b0"], writes=["row_sq"])
        P.emit("vector", lambda h: h.tensor_mul(
            ROWS[0:1, 1024:1024 + W], ROWS[0:1, 0:W], ROWS[0:1, 0:W]),
            reads=["row_m"], writes=["row_t"])
        P.emit("vector", lambda h: h.tensor_sub(
            ROWS[0:1, 1024:1024 + W], ROWS[0:1, 512:512 + W],
            ROWS[0:1, 1024:1024 + W]),
            reads=["row_sq", "row_t"], writes=["row_t"])
        P.emit("scalar", lambda h: h.activation(
            ROWS[0:1, 1024:1024 + W], ROWS[0:1, 1024:1024 + W], AF.Ln,
            bias=EPS),
            reads=["row_t"], writes=["row_t"])
        P.emit("scalar", lambda h: h.activation(
            ROWS[0:1, 1536:1536 + W], ROWS[0:1, 1024:1024 + W], AF.Exp,
            scale=-0.5),
            reads=["row_t"], writes=["row_r"])
        P.emit("tensor", lambda h: h.matmul(
            pB[:, 0:W], s_ones[0:1, 0:128], ROWS[0:1, 0:W],
            start=True, stop=True),
            reads=["row_m", "ones"], writes=["pB.b0"])
        P.emit("tensor", lambda h: h.matmul(
            pB[:, 512:512 + W], s_ones[0:1, 0:128], ROWS[0:1, 1536:1536 + W],
            start=True, stop=True),
            reads=["row_r", "ones"], writes=["pB.b1"])
        for hc in range(8):
            sap = src_fn(hc)
            dap = dst_fn(hc)
            P.emit("vector", lambda h, sap=sap: h.tensor_sub(
                s2, sap, pB[:, 0:W]),
                reads=[src_regions[hc], "pB.b0"], writes=[r2])
            P.emit("vector", lambda h, dap=dap: h.tensor_mul(
                dap, s2, pB[:, 512:512 + W]),
                reads=[r2, "pB.b1"], writes=[dst_regions[hc]])
            if also_bf is not None:
                bap = also_bf(hc)
                P.emit("vector", lambda h, bap=bap, dap=dap: h.tensor_copy(
                    bap, dap),
                    reads=[dst_regions[hc]], writes=[bf_regions[hc]])

    def ffn_prologue(call):
        for tcf in range(n_tcf):
            t0, t1 = tcf * TCF, (tcf + 1) * TCF
            layernorm(
                xnf_t, [f"xnf{hc}" for hc in range(8)],
                lambda hc: r0x(hc, t0, t1), [f"R0.{hc}" for hc in range(8)],
                TCF,
                (sg_t(0), sg_t(1), sg_t(2)), ("sg0", "sg1", "sg2"),
                also_bf=lambda hc: xnbf_t(hc, t0, t1),
                bf_regions=[f"xnb.{hc}" for hc in range(8)])
            for hc in range(8):
                P.emit("tensor", lambda h, hc=hc: h.matmul(
                    pA[0:E, 0:TCF], s_gateT[:, hc * E:(hc + 1) * E],
                    xnf_t(hc), start=(hc == 0), stop=(hc == 7)),
                    reads=[f"xnf{hc}", "gateT"], writes=["pA.b0"], tag="gate")
            P.emit("scalar", lambda h: h.activation(
                rs(0, TCF)[0:E, :], pA[0:E, 0:TCF], AF.Copy),
                reads=["pA.b0"], writes=["lg"])
            for i in range(2):
                P.emit("tensor", lambda h, i=i: h.matmul(
                    pA[:, 256 + i * 64: 256 + (i + 1) * 64],
                    rs(0, TCF)[0:E, i * 128:(i + 1) * 128],
                    s_ident[0:64, 0:64], start=(i == 0), stop=True,
                    skip_group_check=True),
                    reads=["lg", "ident"], writes=["pA.b0"], tag="ltr")
            for i in range(2):
                c0 = 256 + i * 150
                sco = rs(c0, c0 + 64)
                g1 = rs(c0 + 64, c0 + 96)
                g2 = rs(c0 + 96, c0 + 112)
                g3 = rs(c0 + 112, c0 + 120)
                m8g = rs(c0 + 120, c0 + 128)
                gm = rs(c0 + 128, c0 + 136)
                t8 = rs(c0 + 136, c0 + 144)
                ssum = rs(c0 + 144, c0 + 145)
                sinv = rs(c0 + 145, c0 + 146)
                ms = rs(556 + i * 64, 620 + i * 64)
                sel = rs(684 + i * 64, 748 + i * 64)
                cmb = rs(812 + i * 32, 844 + i * 32).bitcast(BF16)
                P.emit("scalar", lambda h, i=i, sco=sco: h.activation(
                    sco, pA[:, 256 + i * 64: 256 + (i + 1) * 64], AF.Sigmoid),
                    reads=["pA.b0"], writes=[f"sco{i}"])
                P.emit("vector", lambda h, sco=sco: h.tensor_add(
                    sco, sco, s_biasb[:, :]),
                    reads=[f"sco{i}", "biasb"], writes=[f"sco{i}"])
                P.emit("vector", lambda h, sco=sco, g1=g1: h.tensor_tensor(
                    g1.rearrange("p (g q) -> p g q", g=8),
                    sco.rearrange("p (g q) -> p g q", g=8)[:, :, 0:4],
                    sco.rearrange("p (g q) -> p g q", g=8)[:, :, 4:8], ALU.max),
                    reads=[f"sco{i}"], writes=[f"g1{i}"])
                P.emit("vector", lambda h, g1=g1, g2=g2: h.tensor_tensor(
                    g2.rearrange("p (g q) -> p g q", g=8),
                    g1.rearrange("p (g q) -> p g q", g=8)[:, :, 0:2],
                    g1.rearrange("p (g q) -> p g q", g=8)[:, :, 2:4], ALU.max),
                    reads=[f"g1{i}"], writes=[f"g2{i}"])
                P.emit("vector", lambda h, g2=g2, g3=g3: h.tensor_tensor(
                    g3, g2[:, 0:16:2], g2[:, 1:16:2], ALU.max),
                    reads=[f"g2{i}"], writes=[f"g3{i}"])
                P.emit("vector", lambda h, g3=g3, m8g=m8g: h.max(m8g, g3),
                       reads=[f"g3{i}"], writes=[f"m8g{i}"])
                P.emit("vector", lambda h, g3=g3, m8g=m8g, gm=gm: h.tensor_scalar(
                    gm, g3, m8g[:, 3:4], None, ALU.is_ge),
                    reads=[f"g3{i}", f"m8g{i}!"], writes=[f"gm{i}"])
                P.emit("vector", lambda h, sco=sco, gm=gm, ms=ms: h.tensor_tensor(
                    ms.rearrange("p (g q) -> p g q", g=8),
                    sco.rearrange("p (g q) -> p g q", g=8),
                    gm.rearrange("p (g o) -> p g o", g=8).broadcast_to(
                        (128, 8, 8)), ALU.mult),
                    reads=[f"sco{i}", f"gm{i}"], writes=[f"ms{i}"])
                P.emit("vector", lambda h, ms=ms, t8=t8: h.max(t8, ms),
                       reads=[f"ms{i}"], writes=[f"t8{i}"])
                P.emit("vector", lambda h, t8=t8, ssum=ssum: h.tensor_reduce(
                    ssum, t8[:, 0:6], AX.X, ALU.add),
                    reads=[f"t8{i}"], writes=[f"ssum{i}"])
                P.emit("vector", lambda h, ssum=ssum, sinv=sinv:
                       h.tensor_scalar_add(sinv, ssum, 1e-8),
                       reads=[f"ssum{i}"], writes=[f"sinv{i}"])
                P.emit("vector", lambda h, sinv=sinv: h.reciprocal(sinv, sinv),
                       reads=[f"sinv{i}"], writes=[f"sinv{i}"])
                P.emit("vector", lambda h, ms=ms, t8=t8, sel=sel: h.tensor_scalar(
                    sel, ms, t8[:, 5:6], None, ALU.is_ge),
                    reads=[f"ms{i}", f"t8{i}!"], writes=[f"sel{i}"])
                P.emit("vector", lambda h, ms=ms, sel=sel: h.tensor_mul(
                    sel, sel, ms),
                    reads=[f"ms{i}", f"sel{i}"], writes=[f"sel{i}"])
                P.emit("vector", lambda h, sel=sel, sinv=sinv, cmb=cmb:
                       h.tensor_scalar(cmb, sel, sinv[:, 0:1], None, ALU.mult),
                       reads=[f"sel{i}", f"sinv{i}!"], writes=[f"cmb{i}"])
            ce = rs(940, 1068).bitcast(BF16)
            for i in range(2):
                P.emit("tensor", lambda h, i=i: h.matmul(
                    pA[0:64, 512 + i * 128: 512 + (i + 1) * 128],
                    rs(812 + i * 32, 844 + i * 32).bitcast(BF16),
                    s_identb[:, :],
                    start=(i == 0), stop=True, skip_group_check=True),
                    reads=[f"cmb{i}", "identb"], writes=["pA.b1"], tag="ctr")
                P.emit("vector", lambda h, i=i, ce=ce: h.tensor_copy(
                    ce[0:64, i * 128:(i + 1) * 128],
                    pA[0:64, 512 + i * 128: 512 + (i + 1) * 128]),
                    reads=["pA.b1"], writes=[f"ce{i}"])
            P.emit("sync", lambda h, t0=t0, t1=t1, ce=ce: h.dma_start(
                d_comb[:, t0:t1], ce[0:64, :]),
                reads=["ce0", "ce1"],
                writes=[f"combd{call}.{tcf}"], tag=f"combsp{call}.{tcf}")
            if d_dbg is not None:
                P.emit("sync", lambda h, tcf=tcf: h.dma_start(
                    d_dbg[:, tcf * 64:(tcf + 1) * 64].bitcast(BF16),
                    rs(812, 876).bitcast(BF16)),
                    reads=["cmb0", "cmb1"], writes=["dbgspill"],
                    tag=f"dbgsp{call}.{tcf}")
                P.emit("sync", lambda h, tcf=tcf: h.dma_start(
                    d_dbg[:, 1024 + tcf * 64:1024 + (tcf + 1) * 64],
                    rs(256, 320)),
                    reads=["sco0"], writes=["dbgspill"], tag="dbgsco0")
                P.emit("sync", lambda h, tcf=tcf: h.dma_start(
                    d_dbg[:, 2048 + tcf * 64:2048 + (tcf + 1) * 64],
                    rs(406, 470)),
                    reads=["sco1"], writes=["dbgspill"], tag="dbgsco1")
                P.emit("sync", lambda h, tcf=tcf: h.dma_start(
                    d_dbg[:, 3072 + tcf * 256:3072 + (tcf + 1) * 256],
                    rs(0, 256)),
                    reads=["lg"], writes=["dbgspill"], tag="dbglg")
                P.emit("sync", lambda h, tcf=tcf: h.dma_start(
                    d_dbg[:, 5120 + tcf * 256:5120 + (tcf + 1) * 256],
                    rs(256, 512)),
                    reads=["g30", "m8g0", "gm0", "t80", "ssum0", "sinv0"],
                    writes=["dbgspill"], tag="dbgblk0")
                P.emit("sync", lambda h, tcf=tcf: h.dma_start(
                    d_dbg[:, 7168 + tcf * 64:7168 + (tcf + 1) * 64],
                    rs(556, 620)),
                    reads=["ms0"], writes=["dbgspill"], tag="dbgms0")

    def ffn_sweep(call):
        for eni in range(n_experts):
            par = eni % 2
            is_shared = eni < 2
            eidx = eni - 2
            P.emit("gpsimd_dma", lambda h, eni=eni, par=par: h.dma_start(
                war(par), d_wff[eni, 0]),
                writes=[f"war{par}"], tag=f"dmaeg{call}.{eni}")
            P.emit("gpsimd_dma", lambda h, eni=eni, par=par: h.dma_start(
                war(2 + par), d_wff[eni, 1]),
                writes=[f"war{2 + par}"], tag=f"dmaeu{call}.{eni}")
            P.emit("gpsimd_dma", lambda h, eni=eni: h.dma_start(
                war(4), d_wff[eni, 2]),
                writes=["war4"], tag=f"dmaed{call}.{eni}")
            if not is_shared:
                P.emit("gpsimd_dma", lambda h, eidx=eidx, par=par: h.dma_start(
                    combping(par)[:, :],
                    d_comb[eidx:eidx + 1, :].broadcast_to((128, T))),
                    reads=[f"combd{call}.{t}" for t in range(NTCF)],
                    writes=[f"combL{par}"], tag=f"dmacomb{call}.{eidx}")
            for tcb in range(NTCF):
                t0, t1 = tcb * TCF, (tcb + 1) * TCF
                for ic in range(4):
                    for hc in range(8):
                        P.emit("tensor", lambda h, ic=ic, hc=hc, par=par,
                               t0=t0, t1=t1: h.matmul(
                                   pA[:, ic * 256:(ic + 1) * 256],
                                   war(par)[:, hc * 512 + ic * 128:
                                            hc * 512 + (ic + 1) * 128],
                                   xnbf_t(hc, t0, t1),
                                   start=(hc == 0 and ic % 2 == 0),
                                   stop=(hc == 7 and ic % 2 == 1),
                                   skip_group_check=True),
                               reads=[f"war{par}", f"xnb.{hc}"],
                               writes=[f"pA.b{ic // 2}"],
                               tag=f"g{call}.{eni}.{tcb}")
                for ic in range(4):
                    for hc in range(8):
                        P.emit("tensor", lambda h, ic=ic, hc=hc, par=par,
                               t0=t0, t1=t1: h.matmul(
                                   pB[:, ic * 256:(ic + 1) * 256],
                                   war(2 + par)[:, hc * 512 + ic * 128:
                                                hc * 512 + (ic + 1) * 128],
                                   xnbf_t(hc, t0, t1),
                                   start=(hc == 0 and ic % 2 == 0),
                                   stop=(hc == 7 and ic % 2 == 1),
                                   skip_group_check=True),
                               reads=[f"war{2 + par}", f"xnb.{hc}"],
                               writes=[f"pB.b{ic // 2}"],
                               tag=f"u{call}.{eni}.{tcb}")
                for ic in range(4):
                    P.emit("scalar", lambda h, ic=ic: h.activation(
                        sg_t(ic), pA[:, ic * 256:(ic + 1) * 256], AF.Silu),
                        reads=[f"pA.b{ic // 2}"], writes=[f"sg{ic}"])
                    if is_shared:
                        P.emit("vector", lambda h, ic=ic: h.tensor_copy(
                            uc_t(ic), pB[:, ic * 256:(ic + 1) * 256]),
                            reads=[f"pB.b{ic // 2}"], writes=[f"uc{ic}"])
                    else:
                        P.emit("vector", lambda h, ic=ic, par=par, t0=t0,
                               t1=t1: h.tensor_tensor(
                                   uc_t(ic), pB[:, ic * 256:(ic + 1) * 256],
                                   combping(par)[:, t0:t1], ALU.mult),
                               reads=[f"pB.b{ic // 2}", f"combL{par}"],
                               writes=[f"uc{ic}"])
                    P.emit("gpsimd", lambda h, ic=ic: h.tensor_tensor(
                        sh_t(ic), sg_t(ic), uc_t(ic), ALU.mult),
                        reads=[f"sg{ic}", f"uc{ic}"], writes=[f"sh{ic}"])
                for hc in range(8):
                    for ic in range(4):
                        P.emit("tensor", lambda h, hc=hc, ic=ic: h.matmul(
                            pC[:, hc * 256:(hc + 1) * 256],
                            war(4)[:, ic * 1024 + hc * 128:
                                   ic * 1024 + (hc + 1) * 128],
                            sh_t(ic),
                            start=(ic == 0 and hc % 2 == 0),
                            stop=(ic == 3 and hc % 2 == 1),
                            skip_group_check=True),
                            reads=["war4", f"sh{ic}"],
                            writes=[f"pC.b{hc // 2}"],
                            tag=f"ed{call}.{eni}.{tcb}")
                P.emit("vector", lambda h, t0=t0: h.scalar_tensor_tensor(
                    R0.rearrange("p (hc t) -> p hc t", hc=8)[:, :, t0:t0 + TCF],
                    pC[:, :].rearrange("p (hc t) -> p hc t", hc=8), 0.5,
                    R0.rearrange("p (hc t) -> p hc t", hc=8)[:, :, t0:t0 + TCF],
                    ALU.mult, ALU.add),
                    reads=[f"pC.b{b}" for b in range(4)] +
                          [f"R0.{hc}" for hc in range(8)],
                    writes=[f"R0.{hc}" for hc in range(8)],
                    tag=f"facc{call}.{eni}.{tcb}")

    def mamba_entry():
        for tcq in range(NTC):
            t0, t1 = tcq * TC, (tcq + 1) * TC
            layernorm(
                hch, [f"hch{hc}" for hc in range(8)],
                lambda hc: r0x(hc, t0, t1), [f"R0.{hc}" for hc in range(8)],
                TC, (mp(0), mp(1), mp(2)), ("mp0", "mp1", "mp2"))
            P.emit("sync", lambda h, t0=t0, t1=t1: h.dma_start(
                d_h[:, t0:t1].rearrange("(hc p) t -> p hc t", p=128),
                SCR[:, 0:4096].rearrange("p (hc t) -> p hc t", hc=8)),
                reads=[f"hch{hc}" for hc in range(8)],
                writes=[f"hd{tcq}"], tag=f"hsave{tcq}")
            P.emit("sync", lambda h, t0=t0, t1=t1: h.dma_start(
                d_x1[:, t0:t1].rearrange("(hc p) t -> p hc t", p=128),
                R0.rearrange("p (hc t) -> p hc t", hc=8)[:, :, t0:t1]),
                reads=[f"R0.{hc}" for hc in range(8)],
                writes=[f"x1d{tcq}"], tag=f"x1save{tcq}")

    def mamba_layer(l):
        P.emit("gpsimd_dma", lambda h, l=l: h.dma_start(
            s_wxp[:, :], d_wxp[:, l * 2048:(l + 1) * 2048]),
            writes=["wxp"], tag=f"dwxp{l}")
        P.emit("gpsimd_dma", lambda h, l=l: h.dma_start(
            s_wdt[:, :], d_wdt[:, l * 2048:(l + 1) * 2048]),
            writes=["wdt"], tag=f"dwdt{l}")
        P.emit("gpsimd_dma", lambda h, l=l: h.dma_start(
            s_alog[:, :], d_alog[:, l * 256:(l + 1) * 256]),
            writes=["alog"], tag=f"dalog{l}")
        P.emit("scalar", lambda h: h.activation(s_A[:, :], s_alog[:, :], AF.Exp),
               reads=["alog"], writes=["A"], tag=f"expA{l}")
        P.emit("vector", lambda h: h.tensor_scalar_mul(s_A[:, :], s_A[:, :],
                                                       -1.0),
               reads=["A"], writes=["A"], tag=f"negA{l}")
        P.emit("vector", lambda h: h.memset(XHALO[:, :], 0.0),
               writes=["xhalo"], tag=f"haloz{l}")

        for tcq in range(NTC):
            t0, t1 = tcq * TC, (tcq + 1) * TC
            P.emit("sync", lambda h, t0=t0, t1=t1: h.dma_start(
                SCR[:, 0:4096].rearrange("p (hc t) -> p hc t", hc=8),
                d_h[:, t0:t1].rearrange("(hc p) t -> p hc t", p=128)),
                reads=[(f"hwb{l - 1}.{tcq}" if l > 0 else f"hd{tcq}")],
                writes=[f"hch{hc}" for hc in range(8)], tag=f"hload{l}.{tcq}")
            for hc in range(8):
                sqb = mp(hc % 2).bitcast(BF16)[:, 0:TC]
                P.emit("vector", lambda h, hc=hc, sqb=sqb: h.tensor_mul(
                    sqb, hch(hc), hch(hc)),
                    reads=[f"hch{hc}"], writes=[f"mp{hc % 2}"])
                P.emit("tensor", lambda h, hc=hc, sqb=sqb: h.matmul(
                    pB[0:1, 0:TC], s_onesb[:, 0:1], sqb,
                    start=(hc == 0), stop=(hc == 7)),
                    reads=[f"mp{hc % 2}", "ones"], writes=["pB.b0"])
            P.emit("scalar", lambda h: h.activation(
                ROWS[0:1, 0:512], pB[0:1, 0:TC], AF.Ln, scale=1.0 / H,
                bias=EPS),
                reads=["pB.b0"], writes=["row_t"])
            P.emit("scalar", lambda h: h.activation(
                ROWS[0:1, 1536:2048], ROWS[0:1, 0:512], AF.Exp, scale=-0.5),
                reads=["row_t"], writes=["row_r"])
            P.emit("tensor", lambda h: h.matmul(
                pB[:, 512:1024], s_ones[0:1, 0:128], ROWS[0:1, 1536:2048],
                start=True, stop=True),
                reads=["row_r", "ones"], writes=["pB.b1"])
            for hc in range(8):
                P.emit("vector", lambda h, hc=hc, l=l: h.scalar_tensor_tensor(
                    xnm_t(hc), hch(hc), ncol(s_normw, l * 8 + hc),
                    pB[:, 512:1024], ALU.mult, ALU.mult),
                    reads=[f"hch{hc}", "pB.b1", "normw"], writes=[f"xnm{hc}"])
            def em_win_dma(piece):
                slot = piece % 3
                P.emit("gpsimd_dma", lambda h, l=l, piece=piece, slot=slot:
                       h.dma_start(war(slot),
                                   d_wwin[l, :, piece * 4096:(piece + 1) * 4096]),
                       writes=[f"war{slot}"], tag=f"dwin{l}.{tcq}.{piece}")

            em_win_dma(0)
            em_win_dma(1)
            for oc in range(32):
                if oc % 4 == 0 and oc // 4 + 2 < 8:
                    em_win_dma(oc // 4 + 2)
                slot = (oc // 4) % 3
                colb = (oc % 4) * 1024
                preg = f"pA.b{oc % 2}"
                for hc in range(8):
                    P.emit("tensor", lambda h, oc=oc, hc=hc, slot=slot,
                           colb=colb: h.matmul(
                               pA[:, (oc % 2) * 512:(oc % 2) * 512 + 512],
                               war(slot)[:, colb + hc * 128:
                                         colb + (hc + 1) * 128],
                               xnm_t(hc), start=(hc == 0), stop=(hc == 7)),
                           reads=[f"war{slot}", f"xnm{hc}"], writes=[preg],
                           tag=f"win{l}.{tcq}.{oc}")
                if oc < 16:
                    dc = oc
                    xb = xpb(dc % 2)
                    P.emit("vector", lambda h, xb=xb, oc=oc: h.tensor_copy(
                        xb[:, 3:515],
                        pA[:, (oc % 2) * 512:(oc % 2) * 512 + 512]),
                        reads=[preg], writes=[f"xp{dc % 2}"])
                    P.emit("vector", lambda h, xb=xb, dc=dc: h.tensor_copy(
                        xb[:, 0:3], XHALO[:, dc * 3:(dc + 1) * 3]),
                        reads=["xhalo"], writes=[f"xp{dc % 2}"])
                    P.emit("vector", lambda h, xb=xb, dc=dc, l=l: h.tensor_scalar(
                        mp(4), xb[:, 0:512],
                        ncol(s_convw, l * 64 + dc * 4 + 0), None, ALU.mult),
                        reads=[f"xp{dc % 2}", "convw"], writes=["mp4"])
                    P.emit("vector", lambda h, xb=xb, dc=dc, l=l:
                           h.scalar_tensor_tensor(
                               mp(5), xb[:, 1:513],
                               ncol(s_convw, l * 64 + dc * 4 + 1),
                               mp(4), ALU.mult, ALU.add),
                           reads=[f"xp{dc % 2}", "mp4", "convw"], writes=["mp5"])
                    P.emit("vector", lambda h, xb=xb, dc=dc, l=l:
                           h.scalar_tensor_tensor(
                               mp(4), xb[:, 2:514],
                               ncol(s_convw, l * 64 + dc * 4 + 2),
                               mp(5), ALU.mult, ALU.add),
                           reads=[f"xp{dc % 2}", "mp5", "convw"], writes=["mp4"])
                    P.emit("vector", lambda h, xb=xb, dc=dc, l=l:
                           h.scalar_tensor_tensor(
                               mp(5), xb[:, 3:515],
                               ncol(s_convw, l * 64 + dc * 4 + 3),
                               mp(4), ALU.mult, ALU.add),
                           reads=[f"xp{dc % 2}", "mp4", "convw"], writes=["mp5"])
                    P.emit("vector", lambda h, xb=xb, dc=dc: h.tensor_copy(
                        XHALO[:, dc * 3:(dc + 1) * 3], xb[:, 512:515]),
                        reads=[f"xp{dc % 2}"], writes=["xhalo"],
                        tag=f"halo{l}.{tcq}.{dc}")
                    P.emit("scalar", lambda h, dc=dc, l=l: h.activation(
                        xc_t(dc), mp(5), AF.Silu,
                        bias=ncol(s_convb, l * 16 + dc)),
                        reads=["mp5", "convb"], writes=[f"xc{dc}"],
                        tag=f"xcsilu{l}.{tcq}.{dc}")
                else:
                    dc2 = oc - 16
                    P.emit("scalar", lambda h, dc2=dc2, oc=oc: h.activation(
                        zs_t(dc2), pA[:, (oc % 2) * 512:(oc % 2) * 512 + 512],
                        AF.Silu),
                        reads=[preg], writes=[f"zs{dc2}"],
                        tag=f"zsilu{l}.{tcq}.{dc2}")
            for dc in range(16):
                P.emit("tensor", lambda h, dc=dc: h.matmul(
                    pC[:, 0:512], s_wxp[:, dc * 128:(dc + 1) * 128],
                    xc_t(dc), start=(dc == 0), stop=(dc == 15)),
                    reads=["wxp", f"xc{dc}"], writes=["pC.b0"],
                    tag=f"wx{l}.{tcq}")
            P.emit("vector", lambda h: h.tensor_copy(
                DTBC[0:16, :], pC[0:16, 0:512]),
                reads=["pC.b0"], writes=["dtbcC"])
            P.emit("vector", lambda h: h.tensor_copy(
                DTBC[32:48, :], pC[32:48, 0:512]),
                reads=["pC.b0"], writes=["dtbcB"])
            P.emit("vector", lambda h: h.tensor_copy(
                DTBC[64:128, :], pC[64:128, 0:512]),
                reads=["pC.b0"], writes=["dtbcD"])
            for n in range(16):
                pb = f"pC.b{1 + n % 2}"
                off = 512 + (n % 2) * 512
                P.emit("tensor", lambda h, n=n, off=off: h.matmul(
                    pC[:, off:off + 512],
                    s_oneh16[0:16, n:n + 1].broadcast_to((16, 128)),
                    DTBC[0:16, :], start=True, stop=True),
                    reads=["dtbcC", "oneh16"], writes=[pb], tag="cbcast")
                P.emit("vector", lambda h, n=n, off=off: h.tensor_copy(
                    crep_t(n), pC[:, off:off + 512]),
                    reads=[pb], writes=[f"crep{n}"])
                P.emit("tensor", lambda h, n=n, off=off: h.matmul(
                    pC[:, off:off + 512],
                    s_oneh16[32:48, n:n + 1].broadcast_to((16, 128)),
                    DTBC[32:48, :], start=True, stop=True),
                    reads=["dtbcB", "oneh16"], writes=[pb], tag="bbcast")
                P.emit("vector", lambda h, n=n, off=off: h.tensor_copy(
                    brep_t(n), pC[:, off:off + 512]),
                    reads=[pb], writes=[f"brep{n}"])
            for dc in range(16):
                pr = dc % 2
                pDt = f"pB.b{pr}"
                sS, sE, sL = (mp(4), mp(5), mp(6)) if pr == 0 else \
                             (mp(7), mp(8), mp(9))
                rS, rE, rL = (("mp4", "mp5", "mp6") if pr == 0 else
                              ("mp7", "mp8", "mp9"))
                delta, rD = mp(pr), f"mp{pr}"
                du, rU = mp(2 + pr), f"mp{2 + pr}"
                P.emit("tensor", lambda h, dc=dc, pr=pr: h.matmul(
                    pB[:, pr * 512: pr * 512 + 512],
                    s_wdt[64:128, dc * 128:(dc + 1) * 128],
                    DTBC[64:128, :], start=True, stop=True),
                    reads=["wdt", "dtbcD"], writes=[pDt],
                    tag=f"wdt{l}.{tcq}.{dc}")
                P.emit("scalar", lambda h, pr=pr, l=l, dc=dc, sS=sS:
                       h.activation(sS, pB[:, pr * 512: pr * 512 + 512],
                                    AF.Identity, bias=ncol(s_bdt, l * 16 + dc)),
                       reads=[pDt, "bdt"], writes=[rS])
                P.emit("scalar", lambda h, pr=pr, l=l, dc=dc, sE=sE:
                       h.activation(sE, pB[:, pr * 512: pr * 512 + 512],
                                    AF.Abs, bias=ncol(s_bdt, l * 16 + dc)),
                       reads=[pDt, "bdt"], writes=[rE])
                P.emit("scalar", lambda h, sE=sE: h.activation(
                    sE, sE, AF.Exp, scale=-1.0), reads=[rE], writes=[rE])
                P.emit("scalar", lambda h, sE=sE, sL=sL: h.activation(
                    sL, sE, AF.Ln, bias=1.0), reads=[rE], writes=[rL])
                P.emit("vector", lambda h, sS=sS: h.tensor_scalar_max(
                    sS, sS, 0.0), reads=[rS], writes=[rS])
                P.emit("vector", lambda h, sS=sS, sL=sL, delta=delta:
                       h.tensor_add(delta, sS, sL),
                       reads=[rS, rL], writes=[rD])
                P.emit("vector", lambda h, delta=delta, dc=dc, du=du:
                       h.tensor_tensor(du, delta, xc_t(dc), ALU.mult),
                       reads=[rD, f"xc{dc}"], writes=[rU])
                py = f"pC.b{pr}"
                pyo = pr * 512
                P.emit("vector", lambda h, dc=dc, l=l, pyo=pyo:
                       h.tensor_scalar(pC[:, pyo:pyo + 512], xc_t(dc),
                                       ncol(s_dpv, l * 16 + dc), None,
                                       ALU.mult),
                       reads=[f"xc{dc}", "dpv"], writes=[py],
                       tag=f"yinit{l}.{tcq}.{dc}")
                for n in range(16):
                    npr = n % 2
                    dA = f"pA.b{npr}"
                    dAo = npr * 512
                    dbu, rB = mp(10 + npr), f"mp{10 + npr}"
                    ch, rC = mp(12 + npr), f"mp{12 + npr}"
                    P.emit("scalar", lambda h, n=n, dc=dc, delta=delta,
                           dAo=dAo: h.activation(
                               pA[:, dAo:dAo + 512], delta, AF.Exp,
                               scale=ncol(s_A, dc * 16 + n)),
                           reads=[rD, "A"], writes=[dA],
                           tag=f"dA{l}.{tcq}.{dc}.{n}")
                    P.emit("vector", lambda h, du=du, n=n, dbu=dbu:
                           h.tensor_tensor(dbu, du, brep_t(n), ALU.mult),
                           reads=[rU, f"brep{n}"], writes=[rB])
                    P.emit("vector", lambda h, dbu=dbu, dc=dc, n=n, tcq=tcq,
                           dAo=dAo: h.tensor_tensor_scan(
                               dbu, pA[:, dAo:dAo + 512], dbu,
                               (0.0 if tcq == 0 else
                                ncol(CARRY, dc * 16 + n)),
                               ALU.mult, ALU.add),
                           reads=([dA, rB] +
                                  ([f"carry.{dc}.{n}!"] if tcq > 0 else [])),
                           writes=[rB],
                           tag=f"scan{l}.{tcq}.{dc}.{n}")
                    P.emit("vector", lambda h, dbu=dbu, dc=dc, n=n:
                           h.tensor_copy(ncol(CARRY, dc * 16 + n),
                                         dbu[:, 511:512]),
                           reads=[rB], writes=[f"carry.{dc}.{n}"])
                    P.emit("gpsimd", lambda h, dbu=dbu, n=n, ch=ch:
                           h.tensor_tensor(ch, dbu, crep_t(n), ALU.mult),
                           reads=[rB, f"crep{n}"], writes=[rC])
                    P.emit("vector", lambda h, ch=ch, pyo=pyo:
                           h.tensor_tensor(pC[:, pyo:pyo + 512],
                                           pC[:, pyo:pyo + 512], ch, ALU.add),
                           reads=[rC, py], writes=[py],
                           tag=f"yacc{l}.{tcq}.{dc}.{n}")
                P.emit("vector", lambda h, dc=dc, pyo=pyo: h.tensor_tensor(
                    xc_t(dc), pC[:, pyo:pyo + 512], zs_t(dc), ALU.mult),
                    reads=[py, f"zs{dc}"], writes=[f"ybf{dc}", f"xc{dc}"],
                    tag=f"ygate{l}.{tcq}.{dc}")
            def em_wblk_dma(piece):
                slot = 3 + piece % 2
                P.emit("gpsimd_dma", lambda h, l=l, piece=piece, slot=slot:
                       h.dma_start(war(slot),
                                   d_wblk[l, :, piece * 4096:(piece + 1) * 4096]),
                       writes=[f"war{slot}"], tag=f"dwblk{l}.{tcq}.{piece}")

            em_wblk_dma(0)
            for hc in range(8):
                if hc % 2 == 0 and hc // 2 + 1 < 4:
                    em_wblk_dma(hc // 2 + 1)
                ppr = f"pC.b{2 + hc % 2}"
                poff = 1024 + (hc % 2) * 512
                piece = hc // 2
                slot = 3 + piece % 2
                for dc in range(16):
                    inner = (hc % 2) * 2048 + dc * 128
                    P.emit("tensor", lambda h, hc=hc, dc=dc, slot=slot,
                           inner=inner, poff=poff: h.matmul(
                               pC[:, poff:poff + 512],
                               war(slot)[:, inner:inner + 128],
                               xc_t(dc), start=(dc == 0), stop=(dc == 15)),
                           reads=[f"war{slot}", f"ybf{dc}"], writes=[ppr],
                           tag=f"wblk{l}.{tcq}.{hc}")
                P.emit("vector", lambda h, hc=hc, poff=poff: h.tensor_tensor(
                    hch(hc), hch(hc), pC[:, poff:poff + 512], ALU.add),
                    reads=[ppr, f"hch{hc}"], writes=[f"hch{hc}"],
                    tag=f"hres{l}.{tcq}.{hc}")
            P.emit("sync", lambda h, t0=t0, t1=t1: h.dma_start(
                d_h[:, t0:t1].rearrange("(hc p) t -> p hc t", p=128),
                SCR[:, 0:4096].rearrange("p (hc t) -> p hc t", hc=8)),
                reads=[f"hch{hc}" for hc in range(8)],
                writes=[f"hwb{l}.{tcq}"], tag=f"hstore{l}.{tcq}")

    def wout_phase():
        P.emit("gpsimd_dma", lambda h: h.dma_start(
            war(0), d_wout[:, 0:4096]), writes=["war0"], tag="dwo0")
        P.emit("gpsimd_dma", lambda h: h.dma_start(
            war(1), d_wout[:, 4096:8192]), writes=["war1"], tag="dwo1")
        for tcq in range(NTC):
            t0, t1 = tcq * TC, (tcq + 1) * TC
            P.emit("sync", lambda h, t0=t0, t1=t1: h.dma_start(
                SCR[:, 0:4096].rearrange("p (hc t) -> p hc t", hc=8),
                d_h[:, t0:t1].rearrange("(hc p) t -> p hc t", p=128)),
                reads=[f"hwb{NL - 1}.{tcq}"],
                writes=[f"hch{hc}" for hc in range(8)], tag=f"hload4.{tcq}")
            P.emit("sync", lambda h, t0=t0, t1=t1: h.dma_start(
                R0.rearrange("p (hc t) -> p hc t", hc=8)[:, :, t0:t1],
                d_x1[:, t0:t1].rearrange("(hc p) t -> p hc t", p=128)),
                reads=[f"x1d{tcq}"],
                writes=[f"R0.{hc}" for hc in range(8)], tag=f"x1load{tcq}")
            for hc in range(8):
                P.emit("vector", lambda h, hc=hc: h.tensor_copy(
                    xnm_t(hc), hch(hc)),
                    reads=[f"hch{hc}"], writes=[f"xnm{hc}"])
            for oc in range(8):
                pr = f"pA.b{oc % 2}"
                poff = (oc % 2) * 512
                for hc in range(8):
                    P.emit("tensor", lambda h, oc=oc, hc=hc, poff=poff:
                           h.matmul(
                               pA[:, poff:poff + 512],
                               war(oc // 4)[:, (oc % 4) * 1024 + hc * 128:
                                            (oc % 4) * 1024 + (hc + 1) * 128],
                               xnm_t(hc), start=(hc == 0), stop=(hc == 7)),
                           reads=[f"war{oc // 4}", f"xnm{hc}"], writes=[pr],
                           tag=f"wout{tcq}.{oc}")
                P.emit("scalar", lambda h, oc=oc, poff=poff: h.activation(
                    mp(10), pA[:, poff:poff + 512], AF.Identity,
                    bias=ncol(s_bout, oc)),
                    reads=[pr, "bout"], writes=["mp10"])
                P.emit("vector", lambda h, oc=oc, t0=t0, t1=t1: h.tensor_add(
                    r0x(oc, t0, t1), r0x(oc, t0, t1), mp(10)),
                    reads=["mp10", f"R0.{oc}"], writes=[f"R0.{oc}"],
                    tag=f"x2{tcq}.{oc}")

    def dbg_store():
        pass

    def final_store():
        dbg_store()
        P.emit("sync", lambda h: h.dma_start(
            d_out.rearrange("(hc p) t -> p hc t", p=128),
            R0.rearrange("p (hc t) -> p hc t", hc=8)),
            reads=[f"R0.{hc}" for hc in range(8)], writes=["outd"],
            tag="outstore")

    ffn_prologue(1)
    ffn_sweep(1)
    if stop_stage == "ffn1":
        final_store()
    else:
        P.barrier("postffn1")
        mamba_entry()
        P.barrier("postentry")
        nlayers = NL
        if stop_stage and stop_stage.startswith("mamba"):
            nlayers = int(stop_stage[5:])
        for l in range(nlayers):
            mamba_layer(l)
        if stop_stage and stop_stage.startswith("mamba"):
            for tcq in range(NTC):
                t0, t1 = tcq * TC, (tcq + 1) * TC
                P.emit("sync", lambda h, t0=t0, t1=t1:
                       h.dma_start(d_out[:, t0:t1], d_h[:, t0:t1]),
                       reads=[(f"hwb{nlayers - 1}.{tcq}" if nlayers > 0 else
                               f"hd{tcq}")],
                       writes=["outd"], tag=f"hdump{tcq}")
        else:
            P.barrier("postmamba")
            wout_phase()
            if stop_stage == "wout":
                final_store()
            else:
                P.barrier("postwout")
                ffn_prologue(2)
                ffn_sweep(2)
                final_store()

    P.materialize(block, sems)
    stack.close()
    return nc


def _pack_inputs(inputs):
    f = np.float32
    ins = {k: np.asarray(v) for k, v in inputs.items()}

    def pack_gu(w):
        return np.ascontiguousarray(
            w.reshape(8, 128, 4, 128).transpose(1, 0, 2, 3).reshape(128, 4096))

    def pack_ed(w):
        return np.ascontiguousarray(
            w.reshape(4, 128, 8, 128).transpose(1, 0, 2, 3).reshape(128, 4096))

    sgT = np.ascontiguousarray(ins["L0_ffn_sg_w"].T.astype(f))
    suT = np.ascontiguousarray(ins["L0_ffn_su_w"].T.astype(f))
    sdT = np.ascontiguousarray(ins["L0_ffn_sd_w"].T.astype(f))
    wff = np.empty((NE, 3, 128, 4096), dtype=f)
    wff[0, 0] = pack_gu(sgT[:, 0:512])
    wff[0, 1] = pack_gu(suT[:, 0:512])
    wff[0, 2] = pack_ed(sdT[0:512, :])
    wff[1, 0] = pack_gu(sgT[:, 512:1024])
    wff[1, 1] = pack_gu(suT[:, 512:1024])
    wff[1, 2] = pack_ed(sdT[512:1024, :])
    egT = ins["L0_ffn_eg_w"].transpose(0, 2, 1)
    euT = ins["L0_ffn_eu_w"].transpose(0, 2, 1)
    edT = ins["L0_ffn_ed_w"].transpose(0, 2, 1)
    for eidx in range(E):
        wff[2 + eidx, 0] = pack_gu(egT[eidx])
        wff[2 + eidx, 1] = pack_gu(euT[eidx])
        wff[2 + eidx, 2] = pack_ed(edT[eidx])

    gateT = np.ascontiguousarray(
        ins["L0_ffn_gate_w"].T.reshape(8, 128, E).transpose(1, 0, 2)
        .reshape(128, 8 * E).astype(f))
    gbias = ins["L0_ffn_gate_bias"].reshape(1, E).astype(f)

    Win = ins["L0_attn_Win"].astype(f)
    wwin = np.ascontiguousarray(
        Win.reshape(NL, 8, 128, 32, 128).transpose(0, 2, 3, 1, 4)
        .reshape(NL, 128, 32768))
    Wblk = ins["L0_attn_Wblk"].astype(f)
    wblk = np.ascontiguousarray(
        Wblk.reshape(NL, 16, 128, 8, 128).transpose(0, 2, 3, 1, 4)
        .reshape(NL, 128, 16384))
    Wx = ins["L0_attn_Wx"].astype(f)
    WxP = np.zeros((NL, DI, 128), dtype=f)
    WxP[:, :, 0:16] = Wx[:, :, 80:96]
    WxP[:, :, 32:48] = Wx[:, :, 64:80]
    WxP[:, :, 64:128] = Wx[:, :, 0:64]
    wxp = np.ascontiguousarray(
        WxP.reshape(NL, 16, 128, 128).transpose(2, 0, 1, 3)
        .reshape(128, NL * 2048))
    Wdt = ins["L0_attn_Wdt"].astype(f)
    wdt = np.zeros((128, NL * 2048), dtype=f)
    for l in range(NL):
        wdt[64:128, l * 2048:(l + 1) * 2048] = Wdt[l]
    Wout = ins["L0_attn_Wout"].astype(f)
    woutp = np.ascontiguousarray(
        Wout.reshape(8, 128, 8, 128).transpose(1, 2, 0, 3).reshape(128, 8192))

    def cols(a):
        return np.ascontiguousarray(
            a.reshape(NL, 16, 128).transpose(2, 0, 1).reshape(128, NL * 16)
            .astype(f))

    normw = np.ascontiguousarray(
        ins["L0_attn_norm_w"].reshape(NL, 8, 128).transpose(2, 0, 1)
        .reshape(128, NL * 8).astype(f))
    convw = np.ascontiguousarray(
        ins["L0_attn_conv_w"].reshape(NL, 16, 128, KC).transpose(2, 0, 1, 3)
        .reshape(128, NL * 64).astype(f))
    convb = cols(ins["L0_attn_conv_b"])
    bdt = cols(ins["L0_attn_bdt"])
    dpv = cols(ins["L0_attn_Dp"])
    alog = np.ascontiguousarray(
        ins["L0_attn_A_log"].reshape(NL, 16, 128, DS).transpose(2, 0, 1, 3)
        .reshape(128, NL * 256).astype(f))
    boutc = np.ascontiguousarray(
        ins["L0_attn_bout"].reshape(8, 128).T.astype(f))

    ident = np.eye(128, dtype=f)
    oneh = np.eye(E, dtype=f)
    oneh16 = np.zeros((48, 16), dtype=f)
    oneh16[0:16] = np.eye(16)
    oneh16[32:48] = np.eye(16)

    common = dict(wff=wff, gateT=gateT, gbias=gbias, wwin=wwin, wblk=wblk,
                  wxp=wxp, wdt=wdt, woutp=woutp, normw=normw, convw=convw,
                  convb=convb, bdt=bdt, dpv=dpv, alog=alog, boutc=boutc,
                  ident=ident, oneh=oneh, oneh16=oneh16)
    xs = [np.ascontiguousarray(ins["L0_attn_x"][b].T.astype(f))
          for b in range(B)]
    return common, xs


def kernel(**inputs):
    common, xs = _pack_inputs(inputs)
    nc = build_program()
    in_maps = [dict(common, x=xs[b]) for b in range(B)]
    res = run_bass_kernel_spmd(nc, in_maps, core_ids=list(range(B)))
    out = np.stack([np.asarray(r["out"]).T for r in res.results], axis=0)
    return np.ascontiguousarray(out.astype(np.float32))


if __name__ == "__main__":
    import sys
    nc = build_program(sys.argv[1] if len(sys.argv) > 1 else None)
    print("built ok")


# version 02
# speedup vs baseline: 1.1866x; 1.1866x over previous
import numpy as np
import concourse.bass as bass
import concourse.mybir as mybir
from concourse.bass_utils import run_bass_kernel_spmd
from contextlib import ExitStack

F32 = mybir.dt.float32
BF16 = mybir.dt.bfloat16
AF = mybir.ActivationFunctionType
ALU = mybir.AluOpType
AX = mybir.AxisListType

B, T, H = 8, 2048, 1024
NL, DI, DS, DTR, KC = 4, 2048, 16, 64, 4
E, IE, ISH = 64, 512, 1024
NE = E + 2
TC = 512
NTC = T // TC
TCF = 256
NTCF = T // TCF
EPS = 1e-5

QUEUES = ("tensor", "vector", "scalar", "gpsimd", "gpsimd_dma", "sync")
DMA_Q = ("gpsimd_dma", "sync")


class Prog:
    def __init__(self):
        self.ops = []
        self.last_writer = {}
        self.last_reader = {}

    SELF_SYNC = ("vector", "scalar", "gpsimd")

    def emit(self, eng, fn, reads=(), writes=(), tag=""):
        gi = len(self.ops)
        selfs = eng in self.SELF_SYNC
        deps = set()
        for r in reads:
            rr = r[:-1] if r.endswith("!") else r
            w = self.last_writer.get(rr)
            if w is not None and (w[0] != eng or selfs):
                deps.add(w[1])
        for r in writes:
            w = self.last_writer.get(r)
            if w is not None and (w[0] != eng or selfs):
                deps.add(w[1])
            for re, ri in self.last_reader.get(r, {}).items():
                if re != eng or selfs:
                    deps.add(ri)
        self.ops.append(dict(eng=eng, fn=fn, gi=gi, deps=deps, tag=tag))
        for r in writes:
            self.last_writer[r] = (eng, gi)
            self.last_reader[r] = {}
        for r in reads:
            rr = r[:-1] if r.endswith("!") else r
            self.last_reader.setdefault(rr, {})[eng] = gi
        return gi

    def barrier(self, tag=""):
        lasts = {}
        for rec in self.ops:
            lasts[rec["eng"]] = rec["gi"]

        def nopfn(h):
            return h.nop()

        for q in ("tensor", "vector", "scalar", "gpsimd", "sync"):
            gi = len(self.ops)
            deps = {g for e, g in lasts.items() if e != q}
            self.ops.append(dict(eng=q, fn=nopfn, gi=gi, deps=deps,
                                 tag=f"bar_{q}_{tag}"))

    def materialize(self, block, sems):
        for rec in self.ops:
            byeng = {}
            for d in rec["deps"]:
                de = self.ops[d]["eng"]
                if byeng.get(de, -1) < d:
                    byeng[de] = d
            rec["deps"] = set(byeng.values())
        needed = set()
        for rec in self.ops:
            needed.update(rec["deps"])
            if rec["eng"] in DMA_Q:
                needed.add(rec["gi"])
        inc_no = {}
        cum = {k: 0 for k in QUEUES}
        for rec in self.ops:
            if rec["gi"] in needed:
                cum[rec["eng"]] += 16 if rec["eng"] in DMA_Q else 1
                inc_no[rec["gi"]] = cum[rec["eng"]]
        streams = {k: [] for k in QUEUES}
        hw = {k: {s: 0 for s in QUEUES} for k in QUEUES}
        for rec in self.ops:
            eng = rec["eng"]
            waits = []
            for d in sorted(rec["deps"]):
                de = self.ops[d]["eng"]
                tgt = inc_no[d]
                if de == eng:
                    waits.append((de, tgt))
                elif hw[eng][de] < tgt:
                    hw[eng][de] = tgt
                    waits.append((de, tgt))
            streams[eng].append((waits, rec["fn"], inc_no.get(rec["gi"]),
                                 rec["tag"], eng))
        self.cum = cum
        self.n_ops = len(self.ops)

        def run(handle, items):
            for waits, fn, inc, tag, myeng in items:
                for (seng, val) in waits:
                    handle.wait_ge(sems[seng], val)
                instr = fn(handle)
                if inc is not None:
                    assert instr is not None, f"need inc on {tag}"
                    instr.then_inc(sems[myeng], 16 if myeng in DMA_Q else 1)

        @block.tensor
        def _(t):
            run(t, streams["tensor"])

        @block.vector
        def _(v):
            run(v, streams["vector"])

        @block.scalar
        def _(s):
            run(s, streams["scalar"])

        @block.gpsimd
        def _(g):
            merged = []
            ig = ic = 0
            for rec in self.ops:
                if rec["eng"] == "gpsimd":
                    merged.append(streams["gpsimd"][ig]); ig += 1
                elif rec["eng"] == "gpsimd_dma":
                    merged.append(streams["gpsimd_dma"][ic]); ic += 1
            run(g, merged)

        @block.sync
        def _(sy):
            run(sy, streams["sync"])
            sy.wait_ge(sems["sync"], cum["sync"])


def build_program(stop_stage=None, dbg_dump=False, n_experts=NE, n_tcf=NTCF):
    nc = bass.Bass("TRN2", target_bir_lowering=False,
                   detect_race_conditions=False)

    d_x = nc.dram_tensor("x", [H, T], F32, kind="ExternalInput").ap()
    d_wff = nc.dram_tensor("wff", [NE, 3, 128, 4096], F32, kind="ExternalInput").ap()
    d_gateT = nc.dram_tensor("gateT", [128, 512], F32, kind="ExternalInput").ap()
    d_gbias = nc.dram_tensor("gbias", [1, E], F32, kind="ExternalInput").ap()
    d_wwin = nc.dram_tensor("wwin", [NL, 128, 32768], F32, kind="ExternalInput").ap()
    d_wblk = nc.dram_tensor("wblk", [NL, 128, 16384], F32, kind="ExternalInput").ap()
    d_wxp = nc.dram_tensor("wxp", [128, NL * 2048], F32, kind="ExternalInput").ap()
    d_wdt = nc.dram_tensor("wdt", [128, NL * 2048], F32, kind="ExternalInput").ap()
    d_wout = nc.dram_tensor("woutp", [128, 8192], F32, kind="ExternalInput").ap()
    d_normw = nc.dram_tensor("normw", [128, NL * 8], F32, kind="ExternalInput").ap()
    d_convw = nc.dram_tensor("convw", [128, NL * 64], F32, kind="ExternalInput").ap()
    d_convb = nc.dram_tensor("convb", [128, NL * 16], F32, kind="ExternalInput").ap()
    d_bdt = nc.dram_tensor("bdt", [128, NL * 16], F32, kind="ExternalInput").ap()
    d_dpv = nc.dram_tensor("dpv", [128, NL * 16], F32, kind="ExternalInput").ap()
    d_alog = nc.dram_tensor("alog", [128, NL * 256], F32, kind="ExternalInput").ap()
    d_bout = nc.dram_tensor("boutc", [128, 8], F32, kind="ExternalInput").ap()
    d_ident = nc.dram_tensor("ident", [128, 128], F32, kind="ExternalInput").ap()
    d_oneh = nc.dram_tensor("oneh", [E, E], F32, kind="ExternalInput").ap()
    d_oneh16 = nc.dram_tensor("oneh16", [48, 16], F32, kind="ExternalInput").ap()

    d_out = nc.dram_tensor("out", [H, T], F32, kind="ExternalOutput").ap()
    d_dbg = (nc.dram_tensor("dbg", [128, 16384], F32, kind="ExternalOutput").ap()
             if dbg_dump else None)
    d_x1 = nc.dram_tensor("x1scr", [H, T], F32).ap()
    d_h = nc.dram_tensor("hscr", [H, T], F32).ap()
    d_comb = nc.dram_tensor("combscr", [E, T], BF16,
                            kind=("ExternalOutput" if dbg_dump else
                                  "Internal")).ap()

    stack = ExitStack()
    en = stack.enter_context

    R0 = en(nc.sbuf_tensor("R0", [128, 16384], F32))
    SCR = en(nc.sbuf_tensor("SCR", [128, 16384], F32))
    WARN = en(nc.sbuf_tensor("WARN", [128, 20480], BF16))
    ROWS = en(nc.sbuf_tensor("ROWS", [1, 2048], F32))
    ROWSB = en(nc.sbuf_tensor("ROWSB", [1, 1024], BF16))
    s_onesb = en(nc.sbuf_tensor("s_onesb", [128, 128], BF16))
    s_identb = en(nc.sbuf_tensor("s_identb", [128, 128], BF16))
    s_gbb = en(nc.sbuf_tensor("s_gbb", [1, E], BF16))
    CARRY = en(nc.sbuf_tensor("CARRY", [128, 256], F32))
    DTBC = en(nc.sbuf_tensor("DTBC", [128, 512], BF16))
    s_wxp = en(nc.sbuf_tensor("s_wxp", [128, 2048], BF16))
    s_wdt = en(nc.sbuf_tensor("s_wdt", [128, 2048], BF16))
    s_gateT = en(nc.sbuf_tensor("s_gateT", [128, 512], F32))
    s_gateTb = en(nc.sbuf_tensor("s_gateTb", [128, 512], BF16))
    s_normw = en(nc.sbuf_tensor("s_normw", [128, NL * 8], F32))
    s_convw = en(nc.sbuf_tensor("s_convw", [128, NL * 64], F32))
    s_convb = en(nc.sbuf_tensor("s_convb", [128, NL * 16], F32))
    s_bdt = en(nc.sbuf_tensor("s_bdt", [128, NL * 16], F32))
    s_dpv = en(nc.sbuf_tensor("s_dpv", [128, NL * 16], F32))
    s_alog = en(nc.sbuf_tensor("s_alog", [128, 256], F32))
    s_A = en(nc.sbuf_tensor("s_A", [128, 256], F32))
    s_bout = en(nc.sbuf_tensor("s_bout", [128, 8], F32))
    s_ident = en(nc.sbuf_tensor("s_ident", [128, 128], F32))
    s_oneh = en(nc.sbuf_tensor("s_oneh", [E, E], BF16))
    s_oneh16 = en(nc.sbuf_tensor("s_oneh16", [48, 16], BF16))
    s_ones = en(nc.sbuf_tensor("s_ones", [128, 128], F32))
    s_gb = en(nc.sbuf_tensor("s_gb", [1, E], F32))
    s_biasb = en(nc.sbuf_tensor("s_biasb", [128, E], F32))
    XHALO = en(nc.sbuf_tensor("XHALO", [128, 48], F32))
    s_const = en(nc.sbuf_tensor("s_const", [128, 8], F32))

    pA = en(nc.psum_tensor("pA", [128, 1024], F32))
    pB = en(nc.psum_tensor("pB", [128, 1024], F32))
    pC = en(nc.psum_tensor("pC", [128, 2048], F32))

    sems = {q: en(nc.semaphore(f"s_{q}")) for q in QUEUES}
    block = en(nc.Block())
    P = Prog()

    R0bf = R0[:, :].bitcast(BF16)
    SCRbf = SCR[:, :].bitcast(BF16)

    def r0x(hc, t0, t1):
        return R0[:, hc * 2048 + t0: hc * 2048 + t1]

    def war(s):
        return WARN[:, s * 4096:(s + 1) * 4096]

    def xnf_t(hc):
        return SCR[:, hc * 256:(hc + 1) * 256]

    def sg_t(ic):
        return SCR[:, 2048 + ic * 256: 2048 + (ic + 1) * 256]

    def uc_t(ic):
        return SCR[:, 3072 + ic * 256: 3072 + (ic + 1) * 256]

    def sh_t(ic, par=0):
        c = 8192 + par * 1024 + ic * 256
        return SCRbf[:, c: c + 256]

    def combping(par=0):
        return SCRbf[:, 10240: 10240 + 2048]

    RSB = 6656

    def rs(c0, c1):
        return SCR[:, RSB + c0: RSB + c1]

    def xnbf_t(hc, t0, t1):
        return SCRbf[:, 16384 + hc * 2048 + t0: 16384 + hc * 2048 + t1]

    def zs_t(dc):
        return R0bf[:, dc * 512:(dc + 1) * 512]

    def xc_t(dc):
        return R0bf[:, 8192 + dc * 512: 8192 + (dc + 1) * 512]

    def crep_t(n):
        return R0bf[:, 16384 + n * 512: 16384 + (n + 1) * 512]

    def brep_t(n):
        return R0bf[:, 24576 + n * 512: 24576 + (n + 1) * 512]

    def hch(hc):
        return SCR[:, hc * 512:(hc + 1) * 512]

    def xnm_t(hc):
        return SCRbf[:, 8192 + hc * 512: 8192 + (hc + 1) * 512]

    def mp(i, w=512):
        return SCR[:, 6144 + i * 512: 6144 + i * 512 + w]

    def xpb(par):
        c = 13312 + par * 520
        return SCR[:, c: c + 520]

    def ncol(t, c):
        return t[:, c:c + 1]

    P.emit("sync", lambda h: h.dma_start(
        R0.rearrange("p (hc t) -> p hc t", hc=8),
        d_x.rearrange("(hc p) t -> p hc t", p=128)),
        writes=[f"R0.{hc}" for hc in range(8)], tag="ldx")
    for nm, dst, src in [("gateT", s_gateT, d_gateT), ("normw", s_normw, d_normw),
                         ("convw", s_convw, d_convw), ("convb", s_convb, d_convb),
                         ("bdt", s_bdt, d_bdt), ("dpv", s_dpv, d_dpv),
                         ("bout", s_bout, d_bout), ("ident", s_ident, d_ident),
                         ("gb", s_gb, d_gbias)]:
        P.emit("sync", lambda h, dst=dst, src=src: h.dma_start(dst[:, :], src),
               writes=[nm], tag=f"ld_{nm}")
    P.emit("gpsimd_dma", lambda h: h.dma_start(s_oneh[:, :], d_oneh),
           writes=["oneh"])
    P.emit("gpsimd_dma", lambda h: h.dma_start(s_gateTb[:, :], d_gateT),
           writes=["gateTb"])
    P.emit("gpsimd_dma", lambda h: h.dma_start(s_oneh16[:, :], d_oneh16),
           writes=["oneh16"])
    P.emit("vector", lambda h: h.memset(s_ones[:, :], 1.0), writes=["ones"])
    P.emit("vector", lambda h: h.memset(s_onesb[:, :], 1.0), writes=["ones"])
    P.emit("gpsimd_dma", lambda h: h.dma_start(s_identb[:, :], d_ident),
           writes=["identb"])
    P.emit("gpsimd_dma", lambda h: h.dma_start(s_gbb[:, :], d_gbias),
           writes=["gbb"])
    _CONSTS = [EPS, 1.0, 1.0 / H, -0.5, -1.0, 1e-8]
    for ci, cv in enumerate(_CONSTS):
        P.emit("vector", lambda h, ci=ci, cv=cv: h.memset(
            s_const[:, ci:ci + 1], cv), writes=["constap"])
        nc.const_aps.register(F32, cv, s_const[:, ci:ci + 1])
    P.emit("scalar", lambda h: h.copy(ROWS[0:1, 0:1], s_const[0:1, 1:2]),
           reads=["constap"], writes=["row_t"], tag="constfence_s")
    P.emit("vector", lambda h: h.tensor_copy(ROWS[0:1, 1:2], s_const[0:1, 1:2]),
           reads=["constap"], writes=["row_t"], tag="constfence_v")
    P.emit("tensor", lambda h: h.matmul(pA[:, 0:E], s_ones[0:1, 0:128],
                                        s_gb[0:1, :], start=True, stop=True),
           reads=["ones", "gb"], writes=["pA.b0"])
    P.emit("vector", lambda h: h.tensor_copy(s_biasb[:, :], pA[:, 0:E]),
           reads=["pA.b0"], writes=["biasb"])
    P.emit("vector", lambda h: h.memset(SCR[:, :], 0.0), writes=["SCRDUMP"],
           tag="scrinit")
    P.emit("vector", lambda h: h.memset(CARRY[:, :], 0.0), writes=["carryinit"])
    P.emit("vector", lambda h: h.memset(DTBC[:, :], 0.0), writes=["dtbcinit"])
    P.barrier("postinit")

    def layernorm(dst_fn, dst_regions, src_fn, src_regions, W, scratch,
                  scr_regions, also_bf=None, bf_regions=None):
        s0, s1, s2 = scratch
        r0, r1, r2 = scr_regions
        xb = s0.bitcast(BF16)[:, 0:W]
        sqb = s1.bitcast(BF16)[:, 0:W]
        for hc in range(8):
            sap = src_fn(hc)
            P.emit("vector", lambda h, sap=sap, xb=xb: h.tensor_copy(xb, sap),
                   reads=[src_regions[hc]], writes=[r0])
            P.emit("vector", lambda h, sap=sap, sqb=sqb: h.tensor_mul(
                sqb, sap, sap),
                reads=[src_regions[hc]], writes=[r1])
            P.emit("tensor", lambda h, hc=hc, sqb=sqb: h.matmul(
                pB[0:1, 0:W], s_onesb[:, 0:1], sqb,
                start=(hc == 0), stop=(hc == 7)),
                reads=[r1, "ones"], writes=["pB.b0"])
            P.emit("tensor", lambda h, hc=hc, xb=xb: h.matmul(
                pA[0:1, 0:W], s_onesb[:, 0:1], xb,
                start=(hc == 0), stop=(hc == 7)),
                reads=[r0, "ones"], writes=["pA.b0"])
        P.emit("scalar", lambda h: h.activation(
            ROWS[0:1, 0:W], pA[0:1, 0:W], AF.Copy, scale=1.0 / H),
            reads=["pA.b0"], writes=["row_m"])

        P.emit("scalar", lambda h: h.activation(
            ROWS[0:1, 512:512 + W], pB[0:1, 0:W], AF.Copy, scale=1.0 / H),
            reads=["pB.b0"], writes=["row_sq"])
        P.emit("vector", lambda h: h.tensor_mul(
            ROWS[0:1, 1024:1024 + W], ROWS[0:1, 0:W], ROWS[0:1, 0:W]),
            reads=["row_m"], writes=["row_t"])
        P.emit("vector", lambda h: h.tensor_sub(
            ROWS[0:1, 1024:1024 + W], ROWS[0:1, 512:512 + W],
            ROWS[0:1, 1024:1024 + W]),
            reads=["row_sq", "row_t"], writes=["row_t"])
        P.emit("scalar", lambda h: h.activation(
            ROWS[0:1, 1024:1024 + W], ROWS[0:1, 1024:1024 + W], AF.Ln,
            bias=EPS),
            reads=["row_t"], writes=["row_t"])
        P.emit("scalar", lambda h: h.activation(
            ROWS[0:1, 1536:1536 + W], ROWS[0:1, 1024:1024 + W], AF.Exp,
            scale=-0.5),
            reads=["row_t"], writes=["row_r"])
        P.emit("tensor", lambda h: h.matmul(
            pB[:, 0:W], s_ones[0:1, 0:128], ROWS[0:1, 0:W],
            start=True, stop=True),
            reads=["row_m", "ones"], writes=["pB.b0"])
        P.emit("tensor", lambda h: h.matmul(
            pB[:, 512:512 + W], s_ones[0:1, 0:128], ROWS[0:1, 1536:1536 + W],
            start=True, stop=True),
            reads=["row_r", "ones"], writes=["pB.b1"])
        for hc in range(8):
            sap = src_fn(hc)
            dap = dst_fn(hc)
            P.emit("vector", lambda h, sap=sap: h.tensor_sub(
                s2, sap, pB[:, 0:W]),
                reads=[src_regions[hc], "pB.b0"], writes=[r2])
            P.emit("vector", lambda h, dap=dap: h.tensor_mul(
                dap, s2, pB[:, 512:512 + W]),
                reads=[r2, "pB.b1"], writes=[dst_regions[hc]])
            if also_bf is not None:
                bap = also_bf(hc)
                P.emit("vector", lambda h, bap=bap, dap=dap: h.tensor_copy(
                    bap, dap),
                    reads=[dst_regions[hc]], writes=[bf_regions[hc]])

    def ffn_prologue(call):
        for tcf in range(n_tcf):
            t0, t1 = tcf * TCF, (tcf + 1) * TCF
            layernorm(
                xnf_t, [f"xnf{hc}" for hc in range(8)],
                lambda hc: r0x(hc, t0, t1), [f"R0.{hc}" for hc in range(8)],
                TCF,
                (sg_t(0), sg_t(1), sg_t(2)), ("sg0", "sg1", "sg2"),
                also_bf=lambda hc: xnbf_t(hc, t0, t1),
                bf_regions=[f"xnb.{hc}" for hc in range(8)])
            for hc in range(8):
                P.emit("tensor", lambda h, hc=hc: h.matmul(
                    pA[0:E, 0:TCF], s_gateT[:, hc * E:(hc + 1) * E],
                    xnf_t(hc), start=(hc == 0), stop=(hc == 7)),
                    reads=[f"xnf{hc}", "gateT"], writes=["pA.b0"], tag="gate")
            P.emit("scalar", lambda h: h.activation(
                rs(0, TCF)[0:E, :], pA[0:E, 0:TCF], AF.Copy),
                reads=["pA.b0"], writes=["lg"])
            for i in range(2):
                P.emit("tensor", lambda h, i=i: h.matmul(
                    pA[:, 256 + i * 64: 256 + (i + 1) * 64],
                    rs(0, TCF)[0:E, i * 128:(i + 1) * 128],
                    s_ident[0:64, 0:64], start=(i == 0), stop=True,
                    skip_group_check=True),
                    reads=["lg", "ident"], writes=["pA.b0"], tag="ltr")
            for i in range(2):
                c0 = 256 + i * 150
                sco = rs(c0, c0 + 64)
                g1 = rs(c0 + 64, c0 + 96)
                g2 = rs(c0 + 96, c0 + 112)
                g3 = rs(c0 + 112, c0 + 120)
                m8g = rs(c0 + 120, c0 + 128)
                gm = rs(c0 + 128, c0 + 136)
                t8 = rs(c0 + 136, c0 + 144)
                ssum = rs(c0 + 144, c0 + 145)
                sinv = rs(c0 + 145, c0 + 146)
                ms = rs(556 + i * 64, 620 + i * 64)
                sel = rs(684 + i * 64, 748 + i * 64)
                cmb = rs(812 + i * 32, 844 + i * 32).bitcast(BF16)
                P.emit("scalar", lambda h, i=i, sco=sco: h.activation(
                    sco, pA[:, 256 + i * 64: 256 + (i + 1) * 64], AF.Sigmoid),
                    reads=["pA.b0"], writes=[f"sco{i}"])
                P.emit("vector", lambda h, sco=sco: h.tensor_add(
                    sco, sco, s_biasb[:, :]),
                    reads=[f"sco{i}", "biasb"], writes=[f"sco{i}"])
                P.emit("vector", lambda h, sco=sco, g1=g1: h.tensor_tensor(
                    g1.rearrange("p (g q) -> p g q", g=8),
                    sco.rearrange("p (g q) -> p g q", g=8)[:, :, 0:4],
                    sco.rearrange("p (g q) -> p g q", g=8)[:, :, 4:8], ALU.max),
                    reads=[f"sco{i}"], writes=[f"g1{i}"])
                P.emit("vector", lambda h, g1=g1, g2=g2: h.tensor_tensor(
                    g2.rearrange("p (g q) -> p g q", g=8),
                    g1.rearrange("p (g q) -> p g q", g=8)[:, :, 0:2],
                    g1.rearrange("p (g q) -> p g q", g=8)[:, :, 2:4], ALU.max),
                    reads=[f"g1{i}"], writes=[f"g2{i}"])
                P.emit("vector", lambda h, g2=g2, g3=g3: h.tensor_tensor(
                    g3, g2[:, 0:16:2], g2[:, 1:16:2], ALU.max),
                    reads=[f"g2{i}"], writes=[f"g3{i}"])
                P.emit("vector", lambda h, g3=g3, m8g=m8g: h.max(m8g, g3),
                       reads=[f"g3{i}"], writes=[f"m8g{i}"])
                P.emit("vector", lambda h, g3=g3, m8g=m8g, gm=gm: h.tensor_scalar(
                    gm, g3, m8g[:, 3:4], None, ALU.is_ge),
                    reads=[f"g3{i}", f"m8g{i}!"], writes=[f"gm{i}"])
                P.emit("vector", lambda h, sco=sco, gm=gm, ms=ms: h.tensor_tensor(
                    ms.rearrange("p (g q) -> p g q", g=8),
                    sco.rearrange("p (g q) -> p g q", g=8),
                    gm.rearrange("p (g o) -> p g o", g=8).broadcast_to(
                        (128, 8, 8)), ALU.mult),
                    reads=[f"sco{i}", f"gm{i}"], writes=[f"ms{i}"])
                P.emit("vector", lambda h, ms=ms, t8=t8: h.max(t8, ms),
                       reads=[f"ms{i}"], writes=[f"t8{i}"])
                P.emit("vector", lambda h, t8=t8, ssum=ssum: h.tensor_reduce(
                    ssum, t8[:, 0:6], AX.X, ALU.add),
                    reads=[f"t8{i}"], writes=[f"ssum{i}"])
                P.emit("vector", lambda h, ssum=ssum, sinv=sinv:
                       h.tensor_scalar_add(sinv, ssum, 1e-8),
                       reads=[f"ssum{i}"], writes=[f"sinv{i}"])
                P.emit("vector", lambda h, sinv=sinv: h.reciprocal(sinv, sinv),
                       reads=[f"sinv{i}"], writes=[f"sinv{i}"])
                P.emit("vector", lambda h, ms=ms, t8=t8, sel=sel: h.tensor_scalar(
                    sel, ms, t8[:, 5:6], None, ALU.is_ge),
                    reads=[f"ms{i}", f"t8{i}!"], writes=[f"sel{i}"])
                P.emit("vector", lambda h, ms=ms, sel=sel: h.tensor_mul(
                    sel, sel, ms),
                    reads=[f"ms{i}", f"sel{i}"], writes=[f"sel{i}"])
                P.emit("vector", lambda h, sel=sel, sinv=sinv, cmb=cmb:
                       h.tensor_scalar(cmb, sel, sinv[:, 0:1], None, ALU.mult),
                       reads=[f"sel{i}", f"sinv{i}!"], writes=[f"cmb{i}"])
            ce = rs(940, 1068).bitcast(BF16)
            for i in range(2):
                P.emit("tensor", lambda h, i=i: h.matmul(
                    pA[0:64, 512 + i * 128: 512 + (i + 1) * 128],
                    rs(812 + i * 32, 844 + i * 32).bitcast(BF16),
                    s_identb[:, :],
                    start=(i == 0), stop=True, skip_group_check=True),
                    reads=[f"cmb{i}", "identb"], writes=["pA.b1"], tag="ctr")
                P.emit("vector", lambda h, i=i, ce=ce: h.tensor_copy(
                    ce[0:64, i * 128:(i + 1) * 128],
                    pA[0:64, 512 + i * 128: 512 + (i + 1) * 128]),
                    reads=["pA.b1"], writes=[f"ce{i}"])
            P.emit("sync", lambda h, t0=t0, t1=t1, ce=ce: h.dma_start(
                d_comb[:, t0:t1], ce[0:64, :]),
                reads=["ce0", "ce1"],
                writes=[f"combd{call}.{tcf}"], tag=f"combsp{call}.{tcf}")
            if d_dbg is not None:
                P.emit("sync", lambda h, tcf=tcf: h.dma_start(
                    d_dbg[:, tcf * 64:(tcf + 1) * 64].bitcast(BF16),
                    rs(812, 876).bitcast(BF16)),
                    reads=["cmb0", "cmb1"], writes=["dbgspill"],
                    tag=f"dbgsp{call}.{tcf}")
                P.emit("sync", lambda h, tcf=tcf: h.dma_start(
                    d_dbg[:, 1024 + tcf * 64:1024 + (tcf + 1) * 64],
                    rs(256, 320)),
                    reads=["sco0"], writes=["dbgspill"], tag="dbgsco0")
                P.emit("sync", lambda h, tcf=tcf: h.dma_start(
                    d_dbg[:, 2048 + tcf * 64:2048 + (tcf + 1) * 64],
                    rs(406, 470)),
                    reads=["sco1"], writes=["dbgspill"], tag="dbgsco1")
                P.emit("sync", lambda h, tcf=tcf: h.dma_start(
                    d_dbg[:, 3072 + tcf * 256:3072 + (tcf + 1) * 256],
                    rs(0, 256)),
                    reads=["lg"], writes=["dbgspill"], tag="dbglg")
                P.emit("sync", lambda h, tcf=tcf: h.dma_start(
                    d_dbg[:, 5120 + tcf * 256:5120 + (tcf + 1) * 256],
                    rs(256, 512)),
                    reads=["g30", "m8g0", "gm0", "t80", "ssum0", "sinv0"],
                    writes=["dbgspill"], tag="dbgblk0")
                P.emit("sync", lambda h, tcf=tcf: h.dma_start(
                    d_dbg[:, 7168 + tcf * 64:7168 + (tcf + 1) * 64],
                    rs(556, 620)),
                    reads=["ms0"], writes=["dbgspill"], tag="dbgms0")

    def ffn_sweep(call):
        def em_gu(eni, par, tcb):
            t0, t1 = tcb * TCF, (tcb + 1) * TCF
            sp = tcb % 2
            for ic in range(4):
                for hc in range(8):
                    P.emit("tensor", lambda h, ic=ic, hc=hc, par=par,
                           t0=t0, t1=t1: h.matmul(
                               pA[:, ic * 256:(ic + 1) * 256],
                               war(par)[:, hc * 512 + ic * 128:
                                        hc * 512 + (ic + 1) * 128],
                               xnbf_t(hc, t0, t1),
                               start=(hc == 0 and ic % 2 == 0),
                               stop=(hc == 7 and ic % 2 == 1),
                               skip_group_check=True),
                           reads=[f"war{par}", f"xnb.{hc}"],
                           writes=[f"pA.b{ic // 2}"],
                           tag=f"g{call}.{eni}.{tcb}")
            for ic in range(4):
                for hc in range(8):
                    P.emit("tensor", lambda h, ic=ic, hc=hc, par=par,
                           t0=t0, t1=t1: h.matmul(
                               pB[:, ic * 256:(ic + 1) * 256],
                               war(2 + par)[:, hc * 512 + ic * 128:
                                            hc * 512 + (ic + 1) * 128],
                               xnbf_t(hc, t0, t1),
                               start=(hc == 0 and ic % 2 == 0),
                               stop=(hc == 7 and ic % 2 == 1),
                               skip_group_check=True),
                           reads=[f"war{2 + par}", f"xnb.{hc}"],
                           writes=[f"pB.b{ic // 2}"],
                           tag=f"u{call}.{eni}.{tcb}")
            is_shared = eni < 2
            for ic in range(4):
                P.emit("scalar", lambda h, ic=ic: h.activation(
                    sg_t(ic), pA[:, ic * 256:(ic + 1) * 256], AF.Silu),
                    reads=[f"pA.b{ic // 2}"], writes=[f"sg{ic}"])
                if is_shared:
                    P.emit("vector", lambda h, ic=ic: h.tensor_copy(
                        uc_t(ic), pB[:, ic * 256:(ic + 1) * 256]),
                        reads=[f"pB.b{ic // 2}"], writes=[f"uc{ic}"])
                else:
                    P.emit("vector", lambda h, ic=ic, t0=t0, t1=t1:
                           h.tensor_tensor(
                               uc_t(ic), pB[:, ic * 256:(ic + 1) * 256],
                               combping()[:, t0:t1], ALU.mult),
                           reads=[f"pB.b{ic // 2}", "combL"],
                           writes=[f"uc{ic}"])
                P.emit("gpsimd", lambda h, ic=ic, sp=sp: h.tensor_tensor(
                    sh_t(ic, sp), sg_t(ic), uc_t(ic), ALU.mult),
                    reads=[f"sg{ic}", f"uc{ic}"], writes=[f"sh{sp}{ic}"])

        def em_ed(eni, tcb):
            sp = tcb % 2
            t0 = tcb * TCF
            for hc in range(8):
                for ic in range(4):
                    P.emit("tensor", lambda h, hc=hc, ic=ic, sp=sp: h.matmul(
                        pC[:, hc * 256:(hc + 1) * 256],
                        war(4)[:, ic * 1024 + hc * 128:
                               ic * 1024 + (hc + 1) * 128],
                        sh_t(ic, sp),
                        start=(ic == 0 and hc % 2 == 0),
                        stop=(ic == 3 and hc % 2 == 1),
                        skip_group_check=True),
                        reads=["war4", f"sh{sp}{ic}"],
                        writes=[f"pC.b{hc // 2}"],
                        tag=f"ed{call}.{eni}.{tcb}")
            P.emit("vector", lambda h, t0=t0: h.scalar_tensor_tensor(
                R0.rearrange("p (hc t) -> p hc t", hc=8)[:, :, t0:t0 + TCF],
                pC[:, :].rearrange("p (hc t) -> p hc t", hc=8), 0.5,
                R0.rearrange("p (hc t) -> p hc t", hc=8)[:, :, t0:t0 + TCF],
                ALU.mult, ALU.add),
                reads=[f"pC.b{b}" for b in range(4)] +
                      [f"R0.{hc}" for hc in range(8)],
                writes=[f"R0.{hc}" for hc in range(8)],
                tag=f"facc{call}.{eni}.{tcb}")

        for eni in range(n_experts):
            par = eni % 2
            is_shared = eni < 2
            eidx = eni - 2
            P.emit("gpsimd_dma", lambda h, eni=eni, par=par: h.dma_start(
                war(par), d_wff[eni, 0]),
                writes=[f"war{par}"], tag=f"dmaeg{call}.{eni}")
            P.emit("gpsimd_dma", lambda h, eni=eni, par=par: h.dma_start(
                war(2 + par), d_wff[eni, 1]),
                writes=[f"war{2 + par}"], tag=f"dmaeu{call}.{eni}")
            P.emit("gpsimd_dma", lambda h, eni=eni: h.dma_start(
                war(4), d_wff[eni, 2]),
                writes=["war4"], tag=f"dmaed{call}.{eni}")
            if not is_shared:
                P.emit("gpsimd_dma", lambda h, eidx=eidx: h.dma_start(
                    combping()[:, :],
                    d_comb[eidx:eidx + 1, :].broadcast_to((128, T))),
                    reads=[f"combd{call}.{t}" for t in range(NTCF)],
                    writes=["combL"], tag=f"dmacomb{call}.{eidx}")
            for tcb in range(NTCF):
                em_gu(eni, par, tcb)
                if tcb > 0:
                    em_ed(eni, tcb - 1)
            em_ed(eni, NTCF - 1)

    def mamba_entry():
        for tcq in range(NTC):
            t0, t1 = tcq * TC, (tcq + 1) * TC
            layernorm(
                hch, [f"hch{hc}" for hc in range(8)],
                lambda hc: r0x(hc, t0, t1), [f"R0.{hc}" for hc in range(8)],
                TC, (mp(0), mp(1), mp(2)), ("mp0", "mp1", "mp2"))
            P.emit("sync", lambda h, t0=t0, t1=t1: h.dma_start(
                d_h[:, t0:t1].rearrange("(hc p) t -> p hc t", p=128),
                SCR[:, 0:4096].rearrange("p (hc t) -> p hc t", hc=8)),
                reads=[f"hch{hc}" for hc in range(8)],
                writes=[f"hd{tcq}"], tag=f"hsave{tcq}")
            P.emit("sync", lambda h, t0=t0, t1=t1: h.dma_start(
                d_x1[:, t0:t1].rearrange("(hc p) t -> p hc t", p=128),
                R0.rearrange("p (hc t) -> p hc t", hc=8)[:, :, t0:t1]),
                reads=[f"R0.{hc}" for hc in range(8)],
                writes=[f"x1d{tcq}"], tag=f"x1save{tcq}")

    def mamba_layer(l):
        P.emit("gpsimd_dma", lambda h, l=l: h.dma_start(
            s_wxp[:, :], d_wxp[:, l * 2048:(l + 1) * 2048]),
            writes=["wxp"], tag=f"dwxp{l}")
        P.emit("gpsimd_dma", lambda h, l=l: h.dma_start(
            s_wdt[:, :], d_wdt[:, l * 2048:(l + 1) * 2048]),
            writes=["wdt"], tag=f"dwdt{l}")
        P.emit("gpsimd_dma", lambda h, l=l: h.dma_start(
            s_alog[:, :], d_alog[:, l * 256:(l + 1) * 256]),
            writes=["alog"], tag=f"dalog{l}")
        P.emit("scalar", lambda h: h.activation(s_A[:, :], s_alog[:, :], AF.Exp),
               reads=["alog"], writes=["A"], tag=f"expA{l}")
        P.emit("vector", lambda h: h.tensor_scalar_mul(s_A[:, :], s_A[:, :],
                                                       -1.0),
               reads=["A"], writes=["A"], tag=f"negA{l}")
        P.emit("vector", lambda h: h.memset(XHALO[:, :], 0.0),
               writes=["xhalo"], tag=f"haloz{l}")

        for tcq in range(NTC):
            t0, t1 = tcq * TC, (tcq + 1) * TC
            P.emit("sync", lambda h, t0=t0, t1=t1: h.dma_start(
                SCR[:, 0:4096].rearrange("p (hc t) -> p hc t", hc=8),
                d_h[:, t0:t1].rearrange("(hc p) t -> p hc t", p=128)),
                reads=[(f"hwb{l - 1}.{tcq}" if l > 0 else f"hd{tcq}")],
                writes=[f"hch{hc}" for hc in range(8)], tag=f"hload{l}.{tcq}")
            for hc in range(8):
                sqb = mp(hc % 2).bitcast(BF16)[:, 0:TC]
                P.emit("vector", lambda h, hc=hc, sqb=sqb: h.tensor_mul(
                    sqb, hch(hc), hch(hc)),
                    reads=[f"hch{hc}"], writes=[f"mp{hc % 2}"])
                P.emit("tensor", lambda h, hc=hc, sqb=sqb: h.matmul(
                    pB[0:1, 0:TC], s_onesb[:, 0:1], sqb,
                    start=(hc == 0), stop=(hc == 7)),
                    reads=[f"mp{hc % 2}", "ones"], writes=["pB.b0"])
            P.emit("scalar", lambda h: h.activation(
                ROWS[0:1, 0:512], pB[0:1, 0:TC], AF.Ln, scale=1.0 / H,
                bias=EPS),
                reads=["pB.b0"], writes=["row_t"])
            P.emit("scalar", lambda h: h.activation(
                ROWS[0:1, 1536:2048], ROWS[0:1, 0:512], AF.Exp, scale=-0.5),
                reads=["row_t"], writes=["row_r"])
            P.emit("tensor", lambda h: h.matmul(
                pB[:, 512:1024], s_ones[0:1, 0:128], ROWS[0:1, 1536:2048],
                start=True, stop=True),
                reads=["row_r", "ones"], writes=["pB.b1"])
            for hc in range(8):
                P.emit("vector", lambda h, hc=hc, l=l: h.scalar_tensor_tensor(
                    xnm_t(hc), hch(hc), ncol(s_normw, l * 8 + hc),
                    pB[:, 512:1024], ALU.mult, ALU.mult),
                    reads=[f"hch{hc}", "pB.b1", "normw"], writes=[f"xnm{hc}"])
            def em_win_dma(piece):
                slot = piece % 3
                P.emit("gpsimd_dma", lambda h, l=l, piece=piece, slot=slot:
                       h.dma_start(war(slot),
                                   d_wwin[l, :, piece * 4096:(piece + 1) * 4096]),
                       writes=[f"war{slot}"], tag=f"dwin{l}.{tcq}.{piece}")

            em_win_dma(0)
            em_win_dma(1)
            for oc in range(32):
                if oc % 4 == 0 and oc // 4 + 2 < 8:
                    em_win_dma(oc // 4 + 2)
                slot = (oc // 4) % 3
                colb = (oc % 4) * 1024
                preg = f"pA.b{oc % 2}"
                for hc in range(8):
                    P.emit("tensor", lambda h, oc=oc, hc=hc, slot=slot,
                           colb=colb: h.matmul(
                               pA[:, (oc % 2) * 512:(oc % 2) * 512 + 512],
                               war(slot)[:, colb + hc * 128:
                                         colb + (hc + 1) * 128],
                               xnm_t(hc), start=(hc == 0), stop=(hc == 7)),
                           reads=[f"war{slot}", f"xnm{hc}"], writes=[preg],
                           tag=f"win{l}.{tcq}.{oc}")
                if oc < 16:
                    dc = oc
                    xb = xpb(dc % 2)
                    P.emit("vector", lambda h, xb=xb, oc=oc: h.tensor_copy(
                        xb[:, 3:515],
                        pA[:, (oc % 2) * 512:(oc % 2) * 512 + 512]),
                        reads=[preg], writes=[f"xp{dc % 2}"])
                    P.emit("vector", lambda h, xb=xb, dc=dc: h.tensor_copy(
                        xb[:, 0:3], XHALO[:, dc * 3:(dc + 1) * 3]),
                        reads=["xhalo"], writes=[f"xp{dc % 2}"])
                    P.emit("vector", lambda h, xb=xb, dc=dc, l=l: h.tensor_scalar(
                        mp(4), xb[:, 0:512],
                        ncol(s_convw, l * 64 + dc * 4 + 0), None, ALU.mult),
                        reads=[f"xp{dc % 2}", "convw"], writes=["mp4"])
                    P.emit("vector", lambda h, xb=xb, dc=dc, l=l:
                           h.scalar_tensor_tensor(
                               mp(5), xb[:, 1:513],
                               ncol(s_convw, l * 64 + dc * 4 + 1),
                               mp(4), ALU.mult, ALU.add),
                           reads=[f"xp{dc % 2}", "mp4", "convw"], writes=["mp5"])
                    P.emit("vector", lambda h, xb=xb, dc=dc, l=l:
                           h.scalar_tensor_tensor(
                               mp(4), xb[:, 2:514],
                               ncol(s_convw, l * 64 + dc * 4 + 2),
                               mp(5), ALU.mult, ALU.add),
                           reads=[f"xp{dc % 2}", "mp5", "convw"], writes=["mp4"])
                    P.emit("vector", lambda h, xb=xb, dc=dc, l=l:
                           h.scalar_tensor_tensor(
                               mp(5), xb[:, 3:515],
                               ncol(s_convw, l * 64 + dc * 4 + 3),
                               mp(4), ALU.mult, ALU.add),
                           reads=[f"xp{dc % 2}", "mp4", "convw"], writes=["mp5"])
                    P.emit("vector", lambda h, xb=xb, dc=dc: h.tensor_copy(
                        XHALO[:, dc * 3:(dc + 1) * 3], xb[:, 512:515]),
                        reads=[f"xp{dc % 2}"], writes=["xhalo"],
                        tag=f"halo{l}.{tcq}.{dc}")
                    P.emit("scalar", lambda h, dc=dc, l=l: h.activation(
                        xc_t(dc), mp(5), AF.Silu,
                        bias=ncol(s_convb, l * 16 + dc)),
                        reads=["mp5", "convb"], writes=[f"xc{dc}"],
                        tag=f"xcsilu{l}.{tcq}.{dc}")
                else:
                    dc2 = oc - 16
                    P.emit("scalar", lambda h, dc2=dc2, oc=oc: h.activation(
                        zs_t(dc2), pA[:, (oc % 2) * 512:(oc % 2) * 512 + 512],
                        AF.Silu),
                        reads=[preg], writes=[f"zs{dc2}"],
                        tag=f"zsilu{l}.{tcq}.{dc2}")
            for dc in range(16):
                P.emit("tensor", lambda h, dc=dc: h.matmul(
                    pC[:, 0:512], s_wxp[:, dc * 128:(dc + 1) * 128],
                    xc_t(dc), start=(dc == 0), stop=(dc == 15)),
                    reads=["wxp", f"xc{dc}"], writes=["pC.b0"],
                    tag=f"wx{l}.{tcq}")
            P.emit("vector", lambda h: h.tensor_copy(
                DTBC[0:16, :], pC[0:16, 0:512]),
                reads=["pC.b0"], writes=["dtbcC"])
            P.emit("vector", lambda h: h.tensor_copy(
                DTBC[32:48, :], pC[32:48, 0:512]),
                reads=["pC.b0"], writes=["dtbcB"])
            P.emit("vector", lambda h: h.tensor_copy(
                DTBC[64:128, :], pC[64:128, 0:512]),
                reads=["pC.b0"], writes=["dtbcD"])
            for n in range(16):
                pb = f"pC.b{1 + n % 2}"
                off = 512 + (n % 2) * 512
                P.emit("tensor", lambda h, n=n, off=off: h.matmul(
                    pC[:, off:off + 512],
                    s_oneh16[0:16, n:n + 1].broadcast_to((16, 128)),
                    DTBC[0:16, :], start=True, stop=True),
                    reads=["dtbcC", "oneh16"], writes=[pb], tag="cbcast")
                P.emit("vector", lambda h, n=n, off=off: h.tensor_copy(
                    crep_t(n), pC[:, off:off + 512]),
                    reads=[pb], writes=[f"crep{n}"])
                P.emit("tensor", lambda h, n=n, off=off: h.matmul(
                    pC[:, off:off + 512],
                    s_oneh16[32:48, n:n + 1].broadcast_to((16, 128)),
                    DTBC[32:48, :], start=True, stop=True),
                    reads=["dtbcB", "oneh16"], writes=[pb], tag="bbcast")
                P.emit("vector", lambda h, n=n, off=off: h.tensor_copy(
                    brep_t(n), pC[:, off:off + 512]),
                    reads=[pb], writes=[f"brep{n}"])
            for dc in range(16):
                pr = dc % 2
                pDt = f"pB.b{pr}"
                sS, sE, sL = (mp(4), mp(5), mp(6)) if pr == 0 else \
                             (mp(7), mp(8), mp(9))
                rS, rE, rL = (("mp4", "mp5", "mp6") if pr == 0 else
                              ("mp7", "mp8", "mp9"))
                delta, rD = mp(pr), f"mp{pr}"
                du, rU = mp(2 + pr), f"mp{2 + pr}"
                P.emit("tensor", lambda h, dc=dc, pr=pr: h.matmul(
                    pB[:, pr * 512: pr * 512 + 512],
                    s_wdt[64:128, dc * 128:(dc + 1) * 128],
                    DTBC[64:128, :], start=True, stop=True),
                    reads=["wdt", "dtbcD"], writes=[pDt],
                    tag=f"wdt{l}.{tcq}.{dc}")
                P.emit("scalar", lambda h, pr=pr, l=l, dc=dc, sS=sS:
                       h.activation(sS, pB[:, pr * 512: pr * 512 + 512],
                                    AF.Identity, bias=ncol(s_bdt, l * 16 + dc)),
                       reads=[pDt, "bdt"], writes=[rS])
                P.emit("scalar", lambda h, pr=pr, l=l, dc=dc, sE=sE:
                       h.activation(sE, pB[:, pr * 512: pr * 512 + 512],
                                    AF.Abs, bias=ncol(s_bdt, l * 16 + dc)),
                       reads=[pDt, "bdt"], writes=[rE])
                P.emit("scalar", lambda h, sE=sE: h.activation(
                    sE, sE, AF.Exp, scale=-1.0), reads=[rE], writes=[rE])
                P.emit("scalar", lambda h, sE=sE, sL=sL: h.activation(
                    sL, sE, AF.Ln, bias=1.0), reads=[rE], writes=[rL])
                P.emit("vector", lambda h, sS=sS: h.tensor_scalar_max(
                    sS, sS, 0.0), reads=[rS], writes=[rS])
                P.emit("vector", lambda h, sS=sS, sL=sL, delta=delta:
                       h.tensor_add(delta, sS, sL),
                       reads=[rS, rL], writes=[rD])
                P.emit("vector", lambda h, delta=delta, dc=dc, du=du:
                       h.tensor_tensor(du, delta, xc_t(dc), ALU.mult),
                       reads=[rD, f"xc{dc}"], writes=[rU])
                py = f"pC.b{pr}"
                pyo = pr * 512
                P.emit("vector", lambda h, dc=dc, l=l, pyo=pyo:
                       h.tensor_scalar(pC[:, pyo:pyo + 512], xc_t(dc),
                                       ncol(s_dpv, l * 16 + dc), None,
                                       ALU.mult),
                       reads=[f"xc{dc}", "dpv"], writes=[py],
                       tag=f"yinit{l}.{tcq}.{dc}")
                def em_yacc(n):
                    npr = n % 2
                    ch, rC = mp(12 + npr), f"mp{12 + npr}"
                    P.emit("vector", lambda h, ch=ch, pyo=pyo:
                           h.tensor_tensor(pC[:, pyo:pyo + 512],
                                           pC[:, pyo:pyo + 512], ch, ALU.add),
                           reads=[rC, py], writes=[py],
                           tag=f"yacc{l}.{tcq}.{dc}.{n}")

                for n in range(16):
                    npr = n % 2
                    dA = f"pA.b{npr}"
                    dAo = npr * 512
                    dbu, rB = mp(10 + npr), f"mp{10 + npr}"
                    ch, rC = mp(12 + npr), f"mp{12 + npr}"
                    P.emit("scalar", lambda h, n=n, dc=dc, delta=delta,
                           dAo=dAo: h.activation(
                               pA[:, dAo:dAo + 512], delta, AF.Exp,
                               scale=ncol(s_A, dc * 16 + n)),
                           reads=[rD, "A"], writes=[dA],
                           tag=f"dA{l}.{tcq}.{dc}.{n}")
                    P.emit("vector", lambda h, du=du, n=n, dbu=dbu:
                           h.tensor_tensor(dbu, du, brep_t(n), ALU.mult),
                           reads=[rU, f"brep{n}"], writes=[rB])
                    P.emit("vector", lambda h, dbu=dbu, dc=dc, n=n, tcq=tcq,
                           dAo=dAo: h.tensor_tensor_scan(
                               dbu, pA[:, dAo:dAo + 512], dbu,
                               (0.0 if tcq == 0 else
                                ncol(CARRY, dc * 16 + n)),
                               ALU.mult, ALU.add),
                           reads=([dA, rB] +
                                  ([f"carry.{dc}.{n}!"] if tcq > 0 else [])),
                           writes=[rB],
                           tag=f"scan{l}.{tcq}.{dc}.{n}")
                    P.emit("vector", lambda h, dbu=dbu, dc=dc, n=n:
                           h.tensor_copy(ncol(CARRY, dc * 16 + n),
                                         dbu[:, 511:512]),
                           reads=[rB], writes=[f"carry.{dc}.{n}"])
                    P.emit("gpsimd", lambda h, dbu=dbu, n=n, ch=ch:
                           h.tensor_tensor(ch, dbu, crep_t(n), ALU.mult),
                           reads=[rB, f"crep{n}"], writes=[rC])
                    if n > 0:
                        em_yacc(n - 1)
                em_yacc(15)
                P.emit("vector", lambda h, dc=dc, pyo=pyo: h.tensor_tensor(
                    xc_t(dc), pC[:, pyo:pyo + 512], zs_t(dc), ALU.mult),
                    reads=[py, f"zs{dc}"], writes=[f"ybf{dc}", f"xc{dc}"],
                    tag=f"ygate{l}.{tcq}.{dc}")
            def em_wblk_dma(piece):
                slot = 3 + piece % 2
                P.emit("gpsimd_dma", lambda h, l=l, piece=piece, slot=slot:
                       h.dma_start(war(slot),
                                   d_wblk[l, :, piece * 4096:(piece + 1) * 4096]),
                       writes=[f"war{slot}"], tag=f"dwblk{l}.{tcq}.{piece}")

            em_wblk_dma(0)
            for hc in range(8):
                if hc % 2 == 0 and hc // 2 + 1 < 4:
                    em_wblk_dma(hc // 2 + 1)
                ppr = f"pC.b{2 + hc % 2}"
                poff = 1024 + (hc % 2) * 512
                piece = hc // 2
                slot = 3 + piece % 2
                for dc in range(16):
                    inner = (hc % 2) * 2048 + dc * 128
                    P.emit("tensor", lambda h, hc=hc, dc=dc, slot=slot,
                           inner=inner, poff=poff: h.matmul(
                               pC[:, poff:poff + 512],
                               war(slot)[:, inner:inner + 128],
                               xc_t(dc), start=(dc == 0), stop=(dc == 15)),
                           reads=[f"war{slot}", f"ybf{dc}"], writes=[ppr],
                           tag=f"wblk{l}.{tcq}.{hc}")
                P.emit("vector", lambda h, hc=hc, poff=poff: h.tensor_tensor(
                    hch(hc), hch(hc), pC[:, poff:poff + 512], ALU.add),
                    reads=[ppr, f"hch{hc}"], writes=[f"hch{hc}"],
                    tag=f"hres{l}.{tcq}.{hc}")
            P.emit("sync", lambda h, t0=t0, t1=t1: h.dma_start(
                d_h[:, t0:t1].rearrange("(hc p) t -> p hc t", p=128),
                SCR[:, 0:4096].rearrange("p (hc t) -> p hc t", hc=8)),
                reads=[f"hch{hc}" for hc in range(8)],
                writes=[f"hwb{l}.{tcq}"], tag=f"hstore{l}.{tcq}")

    def wout_phase():
        P.emit("gpsimd_dma", lambda h: h.dma_start(
            war(0), d_wout[:, 0:4096]), writes=["war0"], tag="dwo0")
        P.emit("gpsimd_dma", lambda h: h.dma_start(
            war(1), d_wout[:, 4096:8192]), writes=["war1"], tag="dwo1")
        for tcq in range(NTC):
            t0, t1 = tcq * TC, (tcq + 1) * TC
            P.emit("sync", lambda h, t0=t0, t1=t1: h.dma_start(
                SCR[:, 0:4096].rearrange("p (hc t) -> p hc t", hc=8),
                d_h[:, t0:t1].rearrange("(hc p) t -> p hc t", p=128)),
                reads=[f"hwb{NL - 1}.{tcq}"],
                writes=[f"hch{hc}" for hc in range(8)], tag=f"hload4.{tcq}")
            P.emit("sync", lambda h, t0=t0, t1=t1: h.dma_start(
                R0.rearrange("p (hc t) -> p hc t", hc=8)[:, :, t0:t1],
                d_x1[:, t0:t1].rearrange("(hc p) t -> p hc t", p=128)),
                reads=[f"x1d{tcq}"],
                writes=[f"R0.{hc}" for hc in range(8)], tag=f"x1load{tcq}")
            for hc in range(8):
                P.emit("vector", lambda h, hc=hc: h.tensor_copy(
                    xnm_t(hc), hch(hc)),
                    reads=[f"hch{hc}"], writes=[f"xnm{hc}"])
            for oc in range(8):
                pr = f"pA.b{oc % 2}"
                poff = (oc % 2) * 512
                for hc in range(8):
                    P.emit("tensor", lambda h, oc=oc, hc=hc, poff=poff:
                           h.matmul(
                               pA[:, poff:poff + 512],
                               war(oc // 4)[:, (oc % 4) * 1024 + hc * 128:
                                            (oc % 4) * 1024 + (hc + 1) * 128],
                               xnm_t(hc), start=(hc == 0), stop=(hc == 7)),
                           reads=[f"war{oc // 4}", f"xnm{hc}"], writes=[pr],
                           tag=f"wout{tcq}.{oc}")
                P.emit("scalar", lambda h, oc=oc, poff=poff: h.activation(
                    mp(10), pA[:, poff:poff + 512], AF.Identity,
                    bias=ncol(s_bout, oc)),
                    reads=[pr, "bout"], writes=["mp10"])
                P.emit("vector", lambda h, oc=oc, t0=t0, t1=t1: h.tensor_add(
                    r0x(oc, t0, t1), r0x(oc, t0, t1), mp(10)),
                    reads=["mp10", f"R0.{oc}"], writes=[f"R0.{oc}"],
                    tag=f"x2{tcq}.{oc}")

    def dbg_store():
        pass

    def final_store():
        dbg_store()
        P.emit("sync", lambda h: h.dma_start(
            d_out.rearrange("(hc p) t -> p hc t", p=128),
            R0.rearrange("p (hc t) -> p hc t", hc=8)),
            reads=[f"R0.{hc}" for hc in range(8)], writes=["outd"],
            tag="outstore")

    ffn_prologue(1)
    ffn_sweep(1)
    if stop_stage == "ffn1":
        final_store()
    else:
        P.barrier("postffn1")
        mamba_entry()
        P.barrier("postentry")
        nlayers = NL
        if stop_stage and stop_stage.startswith("mamba"):
            nlayers = int(stop_stage[5:])
        for l in range(nlayers):
            mamba_layer(l)
        if stop_stage and stop_stage.startswith("mamba"):
            for tcq in range(NTC):
                t0, t1 = tcq * TC, (tcq + 1) * TC
                P.emit("sync", lambda h, t0=t0, t1=t1:
                       h.dma_start(d_out[:, t0:t1], d_h[:, t0:t1]),
                       reads=[(f"hwb{nlayers - 1}.{tcq}" if nlayers > 0 else
                               f"hd{tcq}")],
                       writes=["outd"], tag=f"hdump{tcq}")
        else:
            P.barrier("postmamba")
            wout_phase()
            if stop_stage == "wout":
                final_store()
            else:
                P.barrier("postwout")
                ffn_prologue(2)
                ffn_sweep(2)
                final_store()

    P.materialize(block, sems)
    stack.close()
    return nc


def _pack_inputs(inputs):
    f = np.float32
    ins = {k: np.asarray(v) for k, v in inputs.items()}

    def pack_gu(w):
        return np.ascontiguousarray(
            w.reshape(8, 128, 4, 128).transpose(1, 0, 2, 3).reshape(128, 4096))

    def pack_ed(w):
        return np.ascontiguousarray(
            w.reshape(4, 128, 8, 128).transpose(1, 0, 2, 3).reshape(128, 4096))

    sgT = np.ascontiguousarray(ins["L0_ffn_sg_w"].T.astype(f))
    suT = np.ascontiguousarray(ins["L0_ffn_su_w"].T.astype(f))
    sdT = np.ascontiguousarray(ins["L0_ffn_sd_w"].T.astype(f))
    wff = np.empty((NE, 3, 128, 4096), dtype=f)
    wff[0, 0] = pack_gu(sgT[:, 0:512])
    wff[0, 1] = pack_gu(suT[:, 0:512])
    wff[0, 2] = pack_ed(sdT[0:512, :])
    wff[1, 0] = pack_gu(sgT[:, 512:1024])
    wff[1, 1] = pack_gu(suT[:, 512:1024])
    wff[1, 2] = pack_ed(sdT[512:1024, :])
    egT = ins["L0_ffn_eg_w"].transpose(0, 2, 1)
    euT = ins["L0_ffn_eu_w"].transpose(0, 2, 1)
    edT = ins["L0_ffn_ed_w"].transpose(0, 2, 1)
    for eidx in range(E):
        wff[2 + eidx, 0] = pack_gu(egT[eidx])
        wff[2 + eidx, 1] = pack_gu(euT[eidx])
        wff[2 + eidx, 2] = pack_ed(edT[eidx])

    gateT = np.ascontiguousarray(
        ins["L0_ffn_gate_w"].T.reshape(8, 128, E).transpose(1, 0, 2)
        .reshape(128, 8 * E).astype(f))
    gbias = ins["L0_ffn_gate_bias"].reshape(1, E).astype(f)

    Win = ins["L0_attn_Win"].astype(f)
    wwin = np.ascontiguousarray(
        Win.reshape(NL, 8, 128, 32, 128).transpose(0, 2, 3, 1, 4)
        .reshape(NL, 128, 32768))
    Wblk = ins["L0_attn_Wblk"].astype(f)
    wblk = np.ascontiguousarray(
        Wblk.reshape(NL, 16, 128, 8, 128).transpose(0, 2, 3, 1, 4)
        .reshape(NL, 128, 16384))
    Wx = ins["L0_attn_Wx"].astype(f)
    WxP = np.zeros((NL, DI, 128), dtype=f)
    WxP[:, :, 0:16] = Wx[:, :, 80:96]
    WxP[:, :, 32:48] = Wx[:, :, 64:80]
    WxP[:, :, 64:128] = Wx[:, :, 0:64]
    wxp = np.ascontiguousarray(
        WxP.reshape(NL, 16, 128, 128).transpose(2, 0, 1, 3)
        .reshape(128, NL * 2048))
    Wdt = ins["L0_attn_Wdt"].astype(f)
    wdt = np.zeros((128, NL * 2048), dtype=f)
    for l in range(NL):
        wdt[64:128, l * 2048:(l + 1) * 2048] = Wdt[l]
    Wout = ins["L0_attn_Wout"].astype(f)
    woutp = np.ascontiguousarray(
        Wout.reshape(8, 128, 8, 128).transpose(1, 2, 0, 3).reshape(128, 8192))

    def cols(a):
        return np.ascontiguousarray(
            a.reshape(NL, 16, 128).transpose(2, 0, 1).reshape(128, NL * 16)
            .astype(f))

    normw = np.ascontiguousarray(
        ins["L0_attn_norm_w"].reshape(NL, 8, 128).transpose(2, 0, 1)
        .reshape(128, NL * 8).astype(f))
    convw = np.ascontiguousarray(
        ins["L0_attn_conv_w"].reshape(NL, 16, 128, KC).transpose(2, 0, 1, 3)
        .reshape(128, NL * 64).astype(f))
    convb = cols(ins["L0_attn_conv_b"])
    bdt = cols(ins["L0_attn_bdt"])
    dpv = cols(ins["L0_attn_Dp"])
    alog = np.ascontiguousarray(
        ins["L0_attn_A_log"].reshape(NL, 16, 128, DS).transpose(2, 0, 1, 3)
        .reshape(128, NL * 256).astype(f))
    boutc = np.ascontiguousarray(
        ins["L0_attn_bout"].reshape(8, 128).T.astype(f))

    ident = np.eye(128, dtype=f)
    oneh = np.eye(E, dtype=f)
    oneh16 = np.zeros((48, 16), dtype=f)
    oneh16[0:16] = np.eye(16)
    oneh16[32:48] = np.eye(16)

    common = dict(wff=wff, gateT=gateT, gbias=gbias, wwin=wwin, wblk=wblk,
                  wxp=wxp, wdt=wdt, woutp=woutp, normw=normw, convw=convw,
                  convb=convb, bdt=bdt, dpv=dpv, alog=alog, boutc=boutc,
                  ident=ident, oneh=oneh, oneh16=oneh16)
    xs = [np.ascontiguousarray(ins["L0_attn_x"][b].T.astype(f))
          for b in range(B)]
    return common, xs


def kernel(**inputs):
    common, xs = _pack_inputs(inputs)
    nc = build_program()
    in_maps = [dict(common, x=xs[b]) for b in range(B)]
    res = run_bass_kernel_spmd(nc, in_maps, core_ids=list(range(B)))
    out = np.stack([np.asarray(r["out"]).T for r in res.results], axis=0)
    return np.ascontiguousarray(out.astype(np.float32))


if __name__ == "__main__":
    import sys
    nc = build_program(sys.argv[1] if len(sys.argv) > 1 else None)
    print("built ok")


# version 04
# speedup vs baseline: 1.2102x; 1.0199x over previous
import numpy as np
import concourse.bass as bass
import concourse.mybir as mybir
from concourse.bass_utils import run_bass_kernel_spmd
from contextlib import ExitStack

F32 = mybir.dt.float32
BF16 = mybir.dt.bfloat16
AF = mybir.ActivationFunctionType
ALU = mybir.AluOpType
AX = mybir.AxisListType

B, T, H = 8, 2048, 1024
NL, DI, DS, DTR, KC = 4, 2048, 16, 64, 4
E, IE, ISH = 64, 512, 1024
NE = E + 2
TC = 512
NTC = T // TC
TCF = 256
NTCF = T // TCF
EPS = 1e-5

QUEUES = ("tensor", "vector", "scalar", "gpsimd", "gpsimd_dma", "sync")
DMA_Q = ("gpsimd_dma", "sync")


class Prog:
    def __init__(self):
        self.ops = []
        self.last_writer = {}
        self.last_reader = {}

    SELF_SYNC = ("vector", "scalar", "gpsimd")

    def emit(self, eng, fn, reads=(), writes=(), tag=""):
        gi = len(self.ops)
        selfs = eng in self.SELF_SYNC
        deps = set()
        for r in reads:
            rr = r[:-1] if r.endswith("!") else r
            w = self.last_writer.get(rr)
            if w is not None and (w[0] != eng or selfs):
                deps.add(w[1])
        for r in writes:
            w = self.last_writer.get(r)
            if w is not None and (w[0] != eng or selfs):
                deps.add(w[1])
            for re, ri in self.last_reader.get(r, {}).items():
                if re != eng or selfs:
                    deps.add(ri)
        self.ops.append(dict(eng=eng, fn=fn, gi=gi, deps=deps, tag=tag))
        for r in writes:
            self.last_writer[r] = (eng, gi)
            self.last_reader[r] = {}
        for r in reads:
            rr = r[:-1] if r.endswith("!") else r
            self.last_reader.setdefault(rr, {})[eng] = gi
        return gi

    def barrier(self, tag=""):
        lasts = {}
        for rec in self.ops:
            lasts[rec["eng"]] = rec["gi"]

        def nopfn(h):
            return h.nop()

        for q in ("tensor", "vector", "scalar", "gpsimd", "sync"):
            gi = len(self.ops)
            deps = {g for e, g in lasts.items() if e != q}
            self.ops.append(dict(eng=q, fn=nopfn, gi=gi, deps=deps,
                                 tag=f"bar_{q}_{tag}"))

    def materialize(self, block, sems):
        for rec in self.ops:
            byeng = {}
            for d in rec["deps"]:
                de = self.ops[d]["eng"]
                if byeng.get(de, -1) < d:
                    byeng[de] = d
            rec["deps"] = set(byeng.values())
        needed = set()
        for rec in self.ops:
            needed.update(rec["deps"])
            if rec["eng"] in DMA_Q:
                needed.add(rec["gi"])
        inc_no = {}
        cum = {k: 0 for k in QUEUES}
        for rec in self.ops:
            if rec["gi"] in needed:
                cum[rec["eng"]] += 16 if rec["eng"] in DMA_Q else 1
                inc_no[rec["gi"]] = cum[rec["eng"]]
        streams = {k: [] for k in QUEUES}
        hw = {k: {s: 0 for s in QUEUES} for k in QUEUES}
        for rec in self.ops:
            eng = rec["eng"]
            waits = []
            for d in sorted(rec["deps"]):
                de = self.ops[d]["eng"]
                tgt = inc_no[d]
                if de == eng:
                    waits.append((de, tgt))
                elif hw[eng][de] < tgt:
                    hw[eng][de] = tgt
                    waits.append((de, tgt))
            streams[eng].append((waits, rec["fn"], inc_no.get(rec["gi"]),
                                 rec["tag"], eng))
        self.cum = cum
        self.n_ops = len(self.ops)

        def run(handle, items):
            for waits, fn, inc, tag, myeng in items:
                for (seng, val) in waits:
                    handle.wait_ge(sems[seng], val)
                instr = fn(handle)
                if inc is not None:
                    assert instr is not None, f"need inc on {tag}"
                    instr.then_inc(sems[myeng], 16 if myeng in DMA_Q else 1)

        @block.tensor
        def _(t):
            run(t, streams["tensor"])

        @block.vector
        def _(v):
            run(v, streams["vector"])

        @block.scalar
        def _(s):
            run(s, streams["scalar"])

        @block.gpsimd
        def _(g):
            merged = []
            ig = ic = 0
            for rec in self.ops:
                if rec["eng"] == "gpsimd":
                    merged.append(streams["gpsimd"][ig]); ig += 1
                elif rec["eng"] == "gpsimd_dma":
                    merged.append(streams["gpsimd_dma"][ic]); ic += 1
            run(g, merged)

        @block.sync
        def _(sy):
            run(sy, streams["sync"])
            sy.wait_ge(sems["sync"], cum["sync"])


def build_program(stop_stage=None, dbg_dump=False, n_experts=NE, n_tcf=NTCF):
    nc = bass.Bass("TRN2", target_bir_lowering=False,
                   detect_race_conditions=False)

    d_x = nc.dram_tensor("x", [H, T], F32, kind="ExternalInput").ap()
    d_wff = nc.dram_tensor("wff", [NE, 3, 128, 4096], F32, kind="ExternalInput").ap()
    d_gateT = nc.dram_tensor("gateT", [128, 512], F32, kind="ExternalInput").ap()
    d_gbias = nc.dram_tensor("gbias", [1, E], F32, kind="ExternalInput").ap()
    d_wwin = nc.dram_tensor("wwin", [NL, 128, 32768], F32, kind="ExternalInput").ap()
    d_wblk = nc.dram_tensor("wblk", [NL, 128, 16384], F32, kind="ExternalInput").ap()
    d_wxp = nc.dram_tensor("wxp", [128, NL * 2048], F32, kind="ExternalInput").ap()
    d_wdt = nc.dram_tensor("wdt", [128, NL * 2048], F32, kind="ExternalInput").ap()
    d_wout = nc.dram_tensor("woutp", [128, 8192], F32, kind="ExternalInput").ap()
    d_normw = nc.dram_tensor("normw", [128, NL * 8], F32, kind="ExternalInput").ap()
    d_convw = nc.dram_tensor("convw", [128, NL * 64], F32, kind="ExternalInput").ap()
    d_convb = nc.dram_tensor("convb", [128, NL * 16], F32, kind="ExternalInput").ap()
    d_bdt = nc.dram_tensor("bdt", [128, NL * 16], F32, kind="ExternalInput").ap()
    d_dpv = nc.dram_tensor("dpv", [128, NL * 16], F32, kind="ExternalInput").ap()
    d_alog = nc.dram_tensor("alog", [128, NL * 256], F32, kind="ExternalInput").ap()
    d_bout = nc.dram_tensor("boutc", [128, 8], F32, kind="ExternalInput").ap()
    d_ident = nc.dram_tensor("ident", [128, 128], F32, kind="ExternalInput").ap()
    d_oneh = nc.dram_tensor("oneh", [E, E], F32, kind="ExternalInput").ap()
    d_oneh16 = nc.dram_tensor("oneh16", [48, 16], F32, kind="ExternalInput").ap()

    d_out = nc.dram_tensor("out", [H, T], F32, kind="ExternalOutput").ap()
    d_dbg = (nc.dram_tensor("dbg", [128, 16384], F32, kind="ExternalOutput").ap()
             if dbg_dump else None)
    d_x1 = nc.dram_tensor("x1scr", [H, T], F32).ap()
    d_h = nc.dram_tensor("hscr", [H, T], F32).ap()
    d_comb = nc.dram_tensor("combscr", [E, T], BF16,
                            kind=("ExternalOutput" if dbg_dump else
                                  "Internal")).ap()

    stack = ExitStack()
    en = stack.enter_context

    R0 = en(nc.sbuf_tensor("R0", [128, 16384], F32))
    SCR = en(nc.sbuf_tensor("SCR", [128, 16384], F32))
    WARN = en(nc.sbuf_tensor("WARN", [128, 20480], BF16))
    ROWS = en(nc.sbuf_tensor("ROWS", [1, 2048], F32))
    ROWSB = en(nc.sbuf_tensor("ROWSB", [1, 1024], BF16))
    s_onesb = en(nc.sbuf_tensor("s_onesb", [128, 128], BF16))
    s_identb = en(nc.sbuf_tensor("s_identb", [128, 128], BF16))
    s_gbb = en(nc.sbuf_tensor("s_gbb", [1, E], BF16))
    CARRY = en(nc.sbuf_tensor("CARRY", [128, 256], F32))
    DTBC = en(nc.sbuf_tensor("DTBC", [128, 512], BF16))
    s_wxp = en(nc.sbuf_tensor("s_wxp", [128, 2048], BF16))
    s_wdt = en(nc.sbuf_tensor("s_wdt", [128, 2048], BF16))
    s_gateT = en(nc.sbuf_tensor("s_gateT", [128, 512], F32))
    s_gateTb = en(nc.sbuf_tensor("s_gateTb", [128, 512], BF16))
    s_normw = en(nc.sbuf_tensor("s_normw", [128, NL * 8], F32))
    s_convw = en(nc.sbuf_tensor("s_convw", [128, NL * 64], F32))
    s_convb = en(nc.sbuf_tensor("s_convb", [128, NL * 16], F32))
    s_bdt = en(nc.sbuf_tensor("s_bdt", [128, NL * 16], F32))
    s_dpv = en(nc.sbuf_tensor("s_dpv", [128, NL * 16], F32))
    s_alog = en(nc.sbuf_tensor("s_alog", [128, 256], F32))
    s_A = en(nc.sbuf_tensor("s_A", [128, 256], F32))
    s_bout = en(nc.sbuf_tensor("s_bout", [128, 8], F32))
    s_ident = en(nc.sbuf_tensor("s_ident", [128, 128], F32))
    s_oneh = en(nc.sbuf_tensor("s_oneh", [E, E], BF16))
    s_oneh16 = en(nc.sbuf_tensor("s_oneh16", [48, 16], BF16))
    s_ones = en(nc.sbuf_tensor("s_ones", [128, 128], F32))
    s_gb = en(nc.sbuf_tensor("s_gb", [1, E], F32))
    s_biasb = en(nc.sbuf_tensor("s_biasb", [128, E], F32))
    XHALO = en(nc.sbuf_tensor("XHALO", [128, 48], F32))
    s_const = en(nc.sbuf_tensor("s_const", [128, 8], F32))

    pA = en(nc.psum_tensor("pA", [128, 1024], F32))
    pB = en(nc.psum_tensor("pB", [128, 1024], F32))
    pC = en(nc.psum_tensor("pC", [128, 2048], F32))

    sems = {q: en(nc.semaphore(f"s_{q}")) for q in QUEUES}
    block = en(nc.Block())
    P = Prog()

    R0bf = R0[:, :].bitcast(BF16)
    SCRbf = SCR[:, :].bitcast(BF16)

    def r0x(hc, t0, t1):
        return R0[:, hc * 2048 + t0: hc * 2048 + t1]

    def war(s):
        return WARN[:, s * 4096:(s + 1) * 4096]

    def xnf_t(hc):
        return SCR[:, hc * 256:(hc + 1) * 256]

    def sg_t(ic):
        return SCR[:, 2048 + ic * 256: 2048 + (ic + 1) * 256]

    def uc_t(ic):
        return SCR[:, 3072 + ic * 256: 3072 + (ic + 1) * 256]

    def sh_t(ic, par=0):
        c = 8192 + par * 1024 + ic * 256
        return SCRbf[:, c: c + 256]

    def combping(par=0):
        return SCRbf[:, 10240: 10240 + 2048]

    RSB = 6656

    def rs(c0, c1):
        return SCR[:, RSB + c0: RSB + c1]

    def xnbf_t(hc, t0, t1):
        return SCRbf[:, 16384 + hc * 2048 + t0: 16384 + hc * 2048 + t1]

    def zs_t(dc):
        return R0bf[:, dc * 512:(dc + 1) * 512]

    def xc_t(dc):
        return R0bf[:, 8192 + dc * 512: 8192 + (dc + 1) * 512]

    def crep_t(n):
        return R0bf[:, 16384 + n * 512: 16384 + (n + 1) * 512]

    def brep_t(n):
        return R0bf[:, 24576 + n * 512: 24576 + (n + 1) * 512]

    def hch(hc):
        return SCR[:, hc * 512:(hc + 1) * 512]

    def xnm_t(hc):
        return SCRbf[:, 8192 + hc * 512: 8192 + (hc + 1) * 512]

    def mp(i, w=512):
        return SCR[:, 6144 + i * 512: 6144 + i * 512 + w]

    def xpb(par):
        c = 13312 + par * 520
        return SCR[:, c: c + 520]

    def ncol(t, c):
        return t[:, c:c + 1]

    P.emit("sync", lambda h: h.dma_start(
        R0.rearrange("p (hc t) -> p hc t", hc=8),
        d_x.rearrange("(hc p) t -> p hc t", p=128)),
        writes=[f"R0.{hc}" for hc in range(8)], tag="ldx")
    for nm, dst, src in [("gateT", s_gateT, d_gateT), ("normw", s_normw, d_normw),
                         ("convw", s_convw, d_convw), ("convb", s_convb, d_convb),
                         ("bdt", s_bdt, d_bdt), ("dpv", s_dpv, d_dpv),
                         ("bout", s_bout, d_bout), ("ident", s_ident, d_ident),
                         ("gb", s_gb, d_gbias)]:
        P.emit("sync", lambda h, dst=dst, src=src: h.dma_start(dst[:, :], src),
               writes=[nm], tag=f"ld_{nm}")
    P.emit("gpsimd_dma", lambda h: h.dma_start(s_oneh[:, :], d_oneh),
           writes=["oneh"])
    P.emit("gpsimd_dma", lambda h: h.dma_start(s_gateTb[:, :], d_gateT),
           writes=["gateTb"])
    P.emit("gpsimd_dma", lambda h: h.dma_start(s_oneh16[:, :], d_oneh16),
           writes=["oneh16"])
    P.emit("vector", lambda h: h.memset(s_ones[:, :], 1.0), writes=["ones"])
    P.emit("vector", lambda h: h.memset(s_onesb[:, :], 1.0), writes=["ones"])
    P.emit("gpsimd_dma", lambda h: h.dma_start(s_identb[:, :], d_ident),
           writes=["identb"])
    P.emit("gpsimd_dma", lambda h: h.dma_start(s_gbb[:, :], d_gbias),
           writes=["gbb"])
    _CONSTS = [EPS, 1.0, 1.0 / H, -0.5, -1.0, 1e-8]
    for ci, cv in enumerate(_CONSTS):
        P.emit("vector", lambda h, ci=ci, cv=cv: h.memset(
            s_const[:, ci:ci + 1], cv), writes=["constap"])
        nc.const_aps.register(F32, cv, s_const[:, ci:ci + 1])
    P.emit("scalar", lambda h: h.copy(ROWS[0:1, 0:1], s_const[0:1, 1:2]),
           reads=["constap"], writes=["row_t"], tag="constfence_s")
    P.emit("vector", lambda h: h.tensor_copy(ROWS[0:1, 1:2], s_const[0:1, 1:2]),
           reads=["constap"], writes=["row_t"], tag="constfence_v")
    P.emit("tensor", lambda h: h.matmul(pA[:, 0:E], s_ones[0:1, 0:128],
                                        s_gb[0:1, :], start=True, stop=True),
           reads=["ones", "gb"], writes=["pA.b0"])
    P.emit("vector", lambda h: h.tensor_copy(s_biasb[:, :], pA[:, 0:E]),
           reads=["pA.b0"], writes=["biasb"])
    P.emit("vector", lambda h: h.memset(SCR[:, :], 0.0), writes=["SCRDUMP"],
           tag="scrinit")
    P.emit("vector", lambda h: h.memset(CARRY[:, :], 0.0), writes=["carryinit"])
    P.emit("vector", lambda h: h.memset(DTBC[:, :], 0.0), writes=["dtbcinit"])
    P.barrier("postinit")

    def layernorm(dst_fn, dst_regions, src_fn, src_regions, W, scratch,
                  scr_regions, also_bf=None, bf_regions=None):
        s0, s1, s2 = scratch
        r0, r1, r2 = scr_regions
        xb = s0.bitcast(BF16)[:, 0:W]
        sqb = s1.bitcast(BF16)[:, 0:W]
        for hc in range(8):
            sap = src_fn(hc)
            P.emit("vector", lambda h, sap=sap, xb=xb: h.tensor_copy(xb, sap),
                   reads=[src_regions[hc]], writes=[r0])
            P.emit("vector", lambda h, sap=sap, sqb=sqb: h.tensor_mul(
                sqb, sap, sap),
                reads=[src_regions[hc]], writes=[r1])
            P.emit("tensor", lambda h, hc=hc, sqb=sqb: h.matmul(
                pB[0:1, 0:W], s_onesb[:, 0:1], sqb,
                start=(hc == 0), stop=(hc == 7)),
                reads=[r1, "ones"], writes=["pB.b0"])
            P.emit("tensor", lambda h, hc=hc, xb=xb: h.matmul(
                pA[0:1, 0:W], s_onesb[:, 0:1], xb,
                start=(hc == 0), stop=(hc == 7)),
                reads=[r0, "ones"], writes=["pA.b0"])
        P.emit("scalar", lambda h: h.activation(
            ROWS[0:1, 0:W], pA[0:1, 0:W], AF.Copy, scale=1.0 / H),
            reads=["pA.b0"], writes=["row_m"])

        P.emit("scalar", lambda h: h.activation(
            ROWS[0:1, 512:512 + W], pB[0:1, 0:W], AF.Copy, scale=1.0 / H),
            reads=["pB.b0"], writes=["row_sq"])
        P.emit("vector", lambda h: h.tensor_mul(
            ROWS[0:1, 1024:1024 + W], ROWS[0:1, 0:W], ROWS[0:1, 0:W]),
            reads=["row_m"], writes=["row_t"])
        P.emit("vector", lambda h: h.tensor_sub(
            ROWS[0:1, 1024:1024 + W], ROWS[0:1, 512:512 + W],
            ROWS[0:1, 1024:1024 + W]),
            reads=["row_sq", "row_t"], writes=["row_t"])
        P.emit("scalar", lambda h: h.activation(
            ROWS[0:1, 1024:1024 + W], ROWS[0:1, 1024:1024 + W], AF.Ln,
            bias=EPS),
            reads=["row_t"], writes=["row_t"])
        P.emit("scalar", lambda h: h.activation(
            ROWS[0:1, 1536:1536 + W], ROWS[0:1, 1024:1024 + W], AF.Exp,
            scale=-0.5),
            reads=["row_t"], writes=["row_r"])
        P.emit("tensor", lambda h: h.matmul(
            pB[:, 0:W], s_ones[0:1, 0:128], ROWS[0:1, 0:W],
            start=True, stop=True),
            reads=["row_m", "ones"], writes=["pB.b0"])
        P.emit("tensor", lambda h: h.matmul(
            pB[:, 512:512 + W], s_ones[0:1, 0:128], ROWS[0:1, 1536:1536 + W],
            start=True, stop=True),
            reads=["row_r", "ones"], writes=["pB.b1"])
        for hc in range(8):
            sap = src_fn(hc)
            dap = dst_fn(hc)
            P.emit("vector", lambda h, sap=sap: h.tensor_sub(
                s2, sap, pB[:, 0:W]),
                reads=[src_regions[hc], "pB.b0"], writes=[r2])
            P.emit("vector", lambda h, dap=dap: h.tensor_mul(
                dap, s2, pB[:, 512:512 + W]),
                reads=[r2, "pB.b1"], writes=[dst_regions[hc]])
            if also_bf is not None:
                bap = also_bf(hc)
                P.emit("vector", lambda h, bap=bap, dap=dap: h.tensor_copy(
                    bap, dap),
                    reads=[dst_regions[hc]], writes=[bf_regions[hc]])

    def ffn_prologue(call):
        for tcf in range(n_tcf):
            t0, t1 = tcf * TCF, (tcf + 1) * TCF
            layernorm(
                xnf_t, [f"xnf{hc}" for hc in range(8)],
                lambda hc: r0x(hc, t0, t1), [f"R0.{hc}" for hc in range(8)],
                TCF,
                (sg_t(0), sg_t(1), sg_t(2)), ("sg0", "sg1", "sg2"),
                also_bf=lambda hc: xnbf_t(hc, t0, t1),
                bf_regions=[f"xnb.{hc}" for hc in range(8)])
            for hc in range(8):
                P.emit("tensor", lambda h, hc=hc: h.matmul(
                    pA[0:E, 0:TCF], s_gateT[:, hc * E:(hc + 1) * E],
                    xnf_t(hc), start=(hc == 0), stop=(hc == 7)),
                    reads=[f"xnf{hc}", "gateT"], writes=["pA.b0"], tag="gate")
            P.emit("scalar", lambda h: h.activation(
                rs(0, TCF)[0:E, :], pA[0:E, 0:TCF], AF.Copy),
                reads=["pA.b0"], writes=["lg"])
            for i in range(2):
                P.emit("tensor", lambda h, i=i: h.matmul(
                    pA[:, 256 + i * 64: 256 + (i + 1) * 64],
                    rs(0, TCF)[0:E, i * 128:(i + 1) * 128],
                    s_ident[0:64, 0:64], start=(i == 0), stop=True,
                    skip_group_check=True),
                    reads=["lg", "ident"], writes=["pA.b0"], tag="ltr")
            for i in range(2):
                c0 = 256 + i * 150
                sco = rs(c0, c0 + 64)
                g1 = rs(c0 + 64, c0 + 96)
                g2 = rs(c0 + 96, c0 + 112)
                g3 = rs(c0 + 112, c0 + 120)
                m8g = rs(c0 + 120, c0 + 128)
                gm = rs(c0 + 128, c0 + 136)
                t8 = rs(c0 + 136, c0 + 144)
                ssum = rs(c0 + 144, c0 + 145)
                sinv = rs(c0 + 145, c0 + 146)
                ms = rs(556 + i * 64, 620 + i * 64)
                sel = rs(684 + i * 64, 748 + i * 64)
                cmb = rs(812 + i * 32, 844 + i * 32).bitcast(BF16)
                P.emit("scalar", lambda h, i=i, sco=sco: h.activation(
                    sco, pA[:, 256 + i * 64: 256 + (i + 1) * 64], AF.Sigmoid),
                    reads=["pA.b0"], writes=[f"sco{i}"])
                P.emit("vector", lambda h, sco=sco: h.tensor_add(
                    sco, sco, s_biasb[:, :]),
                    reads=[f"sco{i}", "biasb"], writes=[f"sco{i}"])
                P.emit("vector", lambda h, sco=sco, g1=g1: h.tensor_tensor(
                    g1.rearrange("p (g q) -> p g q", g=8),
                    sco.rearrange("p (g q) -> p g q", g=8)[:, :, 0:4],
                    sco.rearrange("p (g q) -> p g q", g=8)[:, :, 4:8], ALU.max),
                    reads=[f"sco{i}"], writes=[f"g1{i}"])
                P.emit("vector", lambda h, g1=g1, g2=g2: h.tensor_tensor(
                    g2.rearrange("p (g q) -> p g q", g=8),
                    g1.rearrange("p (g q) -> p g q", g=8)[:, :, 0:2],
                    g1.rearrange("p (g q) -> p g q", g=8)[:, :, 2:4], ALU.max),
                    reads=[f"g1{i}"], writes=[f"g2{i}"])
                P.emit("vector", lambda h, g2=g2, g3=g3: h.tensor_tensor(
                    g3, g2[:, 0:16:2], g2[:, 1:16:2], ALU.max),
                    reads=[f"g2{i}"], writes=[f"g3{i}"])
                P.emit("vector", lambda h, g3=g3, m8g=m8g: h.max(m8g, g3),
                       reads=[f"g3{i}"], writes=[f"m8g{i}"])
                P.emit("vector", lambda h, g3=g3, m8g=m8g, gm=gm: h.tensor_scalar(
                    gm, g3, m8g[:, 3:4], None, ALU.is_ge),
                    reads=[f"g3{i}", f"m8g{i}!"], writes=[f"gm{i}"])
                P.emit("vector", lambda h, sco=sco, gm=gm, ms=ms: h.tensor_tensor(
                    ms.rearrange("p (g q) -> p g q", g=8),
                    sco.rearrange("p (g q) -> p g q", g=8),
                    gm.rearrange("p (g o) -> p g o", g=8).broadcast_to(
                        (128, 8, 8)), ALU.mult),
                    reads=[f"sco{i}", f"gm{i}"], writes=[f"ms{i}"])
                P.emit("vector", lambda h, ms=ms, t8=t8: h.max(t8, ms),
                       reads=[f"ms{i}"], writes=[f"t8{i}"])
                P.emit("vector", lambda h, t8=t8, ssum=ssum: h.tensor_reduce(
                    ssum, t8[:, 0:6], AX.X, ALU.add),
                    reads=[f"t8{i}"], writes=[f"ssum{i}"])
                P.emit("vector", lambda h, ssum=ssum, sinv=sinv:
                       h.tensor_scalar_add(sinv, ssum, 1e-8),
                       reads=[f"ssum{i}"], writes=[f"sinv{i}"])
                P.emit("vector", lambda h, sinv=sinv: h.reciprocal(sinv, sinv),
                       reads=[f"sinv{i}"], writes=[f"sinv{i}"])
                P.emit("vector", lambda h, ms=ms, t8=t8, sel=sel: h.tensor_scalar(
                    sel, ms, t8[:, 5:6], None, ALU.is_ge),
                    reads=[f"ms{i}", f"t8{i}!"], writes=[f"sel{i}"])
                P.emit("vector", lambda h, ms=ms, sel=sel: h.tensor_mul(
                    sel, sel, ms),
                    reads=[f"ms{i}", f"sel{i}"], writes=[f"sel{i}"])
                P.emit("vector", lambda h, sel=sel, sinv=sinv, cmb=cmb:
                       h.tensor_scalar(cmb, sel, sinv[:, 0:1], None, ALU.mult),
                       reads=[f"sel{i}", f"sinv{i}!"], writes=[f"cmb{i}"])
            ce = rs(940, 1068).bitcast(BF16)
            for i in range(2):
                P.emit("tensor", lambda h, i=i: h.matmul(
                    pA[0:64, 512 + i * 128: 512 + (i + 1) * 128],
                    rs(812 + i * 32, 844 + i * 32).bitcast(BF16),
                    s_identb[:, :],
                    start=(i == 0), stop=True, skip_group_check=True),
                    reads=[f"cmb{i}", "identb"], writes=["pA.b1"], tag="ctr")
                P.emit("vector", lambda h, i=i, ce=ce: h.tensor_copy(
                    ce[0:64, i * 128:(i + 1) * 128],
                    pA[0:64, 512 + i * 128: 512 + (i + 1) * 128]),
                    reads=["pA.b1"], writes=[f"ce{i}"])
            P.emit("sync", lambda h, t0=t0, t1=t1, ce=ce: h.dma_start(
                d_comb[:, t0:t1], ce[0:64, :]),
                reads=["ce0", "ce1"],
                writes=[f"combd{call}.{tcf}"], tag=f"combsp{call}.{tcf}")
            if d_dbg is not None:
                P.emit("sync", lambda h, tcf=tcf: h.dma_start(
                    d_dbg[:, tcf * 64:(tcf + 1) * 64].bitcast(BF16),
                    rs(812, 876).bitcast(BF16)),
                    reads=["cmb0", "cmb1"], writes=["dbgspill"],
                    tag=f"dbgsp{call}.{tcf}")
                P.emit("sync", lambda h, tcf=tcf: h.dma_start(
                    d_dbg[:, 1024 + tcf * 64:1024 + (tcf + 1) * 64],
                    rs(256, 320)),
                    reads=["sco0"], writes=["dbgspill"], tag="dbgsco0")
                P.emit("sync", lambda h, tcf=tcf: h.dma_start(
                    d_dbg[:, 2048 + tcf * 64:2048 + (tcf + 1) * 64],
                    rs(406, 470)),
                    reads=["sco1"], writes=["dbgspill"], tag="dbgsco1")
                P.emit("sync", lambda h, tcf=tcf: h.dma_start(
                    d_dbg[:, 3072 + tcf * 256:3072 + (tcf + 1) * 256],
                    rs(0, 256)),
                    reads=["lg"], writes=["dbgspill"], tag="dbglg")
                P.emit("sync", lambda h, tcf=tcf: h.dma_start(
                    d_dbg[:, 5120 + tcf * 256:5120 + (tcf + 1) * 256],
                    rs(256, 512)),
                    reads=["g30", "m8g0", "gm0", "t80", "ssum0", "sinv0"],
                    writes=["dbgspill"], tag="dbgblk0")
                P.emit("sync", lambda h, tcf=tcf: h.dma_start(
                    d_dbg[:, 7168 + tcf * 64:7168 + (tcf + 1) * 64],
                    rs(556, 620)),
                    reads=["ms0"], writes=["dbgspill"], tag="dbgms0")

    def ffn_sweep(call):
        def em_gu(eni, par, tcb):
            t0, t1 = tcb * TCF, (tcb + 1) * TCF
            sp = tcb % 2
            for ic in range(4):
                for hc in range(8):
                    P.emit("tensor", lambda h, ic=ic, hc=hc, par=par,
                           t0=t0, t1=t1: h.matmul(
                               pA[:, ic * 256:(ic + 1) * 256],
                               war(par)[:, hc * 512 + ic * 128:
                                        hc * 512 + (ic + 1) * 128],
                               xnbf_t(hc, t0, t1),
                               start=(hc == 0 and ic % 2 == 0),
                               stop=(hc == 7 and ic % 2 == 1),
                               skip_group_check=True),
                           reads=[f"war{par}", f"xnb.{hc}"],
                           writes=[f"pA.b{ic // 2}"],
                           tag=f"g{call}.{eni}.{tcb}")
            for ic in range(4):
                for hc in range(8):
                    P.emit("tensor", lambda h, ic=ic, hc=hc, par=par,
                           t0=t0, t1=t1: h.matmul(
                               pB[:, ic * 256:(ic + 1) * 256],
                               war(2 + par)[:, hc * 512 + ic * 128:
                                            hc * 512 + (ic + 1) * 128],
                               xnbf_t(hc, t0, t1),
                               start=(hc == 0 and ic % 2 == 0),
                               stop=(hc == 7 and ic % 2 == 1),
                               skip_group_check=True),
                           reads=[f"war{2 + par}", f"xnb.{hc}"],
                           writes=[f"pB.b{ic // 2}"],
                           tag=f"u{call}.{eni}.{tcb}")
            is_shared = eni < 2
            for ic in range(4):
                P.emit("scalar", lambda h, ic=ic: h.activation(
                    sg_t(ic), pA[:, ic * 256:(ic + 1) * 256], AF.Silu),
                    reads=[f"pA.b{ic // 2}"], writes=[f"sg{ic}"])
                if is_shared:
                    P.emit("vector", lambda h, ic=ic: h.tensor_copy(
                        uc_t(ic), pB[:, ic * 256:(ic + 1) * 256]),
                        reads=[f"pB.b{ic // 2}"], writes=[f"uc{ic}"])
                else:
                    P.emit("vector", lambda h, ic=ic, t0=t0, t1=t1:
                           h.tensor_tensor(
                               uc_t(ic), pB[:, ic * 256:(ic + 1) * 256],
                               combping()[:, t0:t1], ALU.mult),
                           reads=[f"pB.b{ic // 2}", "combL"],
                           writes=[f"uc{ic}"])
                P.emit("gpsimd", lambda h, ic=ic, sp=sp: h.tensor_tensor(
                    sh_t(ic, sp), sg_t(ic), uc_t(ic), ALU.mult),
                    reads=[f"sg{ic}", f"uc{ic}"], writes=[f"sh{sp}{ic}"])

        def em_ed(eni, tcb):
            sp = tcb % 2
            t0 = tcb * TCF
            grp_first = True
            grp_last = True
            for hc in range(8):
                for ic in range(4):
                    P.emit("tensor", lambda h, hc=hc, ic=ic, sp=sp: h.matmul(
                        pC[:, hc * 256:(hc + 1) * 256],
                        war(4)[:, ic * 1024 + hc * 128:
                               ic * 1024 + (hc + 1) * 128],
                        sh_t(ic, sp),
                        start=(grp_first and ic == 0 and hc % 2 == 0),
                        stop=(grp_last and ic == 3 and hc % 2 == 1),
                        skip_group_check=True),
                        reads=["war4", f"sh{sp}{ic}"],
                        writes=[f"pC.b{hc // 2}"],
                        tag=f"ed{call}.{eni}.{tcb}")
            if grp_last:
                P.emit("vector", lambda h, t0=t0: h.scalar_tensor_tensor(
                    R0.rearrange("p (hc t) -> p hc t", hc=8)[:, :, t0:t0 + TCF],
                    pC[:, :].rearrange("p (hc t) -> p hc t", hc=8), 0.5,
                    R0.rearrange("p (hc t) -> p hc t", hc=8)[:, :, t0:t0 + TCF],
                    ALU.mult, ALU.add),
                    reads=[f"pC.b{b}" for b in range(4)] +
                          [f"R0.{hc}" for hc in range(8)],
                    writes=[f"R0.{hc}" for hc in range(8)],
                    tag=f"facc{call}.{eni}.{tcb}")

        for eni in range(n_experts):
            par = eni % 2
            is_shared = eni < 2
            eidx = eni - 2
            P.emit("gpsimd_dma", lambda h, eni=eni, par=par: h.dma_start(
                war(par), d_wff[eni, 0]),
                writes=[f"war{par}"], tag=f"dmaeg{call}.{eni}")
            P.emit("gpsimd_dma", lambda h, eni=eni, par=par: h.dma_start(
                war(2 + par), d_wff[eni, 1]),
                writes=[f"war{2 + par}"], tag=f"dmaeu{call}.{eni}")
            P.emit("gpsimd_dma", lambda h, eni=eni: h.dma_start(
                war(4), d_wff[eni, 2]),
                writes=["war4"], tag=f"dmaed{call}.{eni}")
            if not is_shared:
                P.emit("gpsimd_dma", lambda h, eidx=eidx: h.dma_start(
                    combping()[:, :],
                    d_comb[eidx:eidx + 1, :].broadcast_to((128, T))),
                    reads=[f"combd{call}.{t}" for t in range(NTCF)],
                    writes=["combL"], tag=f"dmacomb{call}.{eidx}")
            for tcb in range(NTCF):
                em_gu(eni, par, tcb)
                if tcb > 0:
                    em_ed(eni, tcb - 1)
            em_ed(eni, NTCF - 1)

    def mamba_entry():
        for tcq in range(NTC):
            t0, t1 = tcq * TC, (tcq + 1) * TC
            layernorm(
                hch, [f"hch{hc}" for hc in range(8)],
                lambda hc: r0x(hc, t0, t1), [f"R0.{hc}" for hc in range(8)],
                TC, (mp(0), mp(1), mp(2)), ("mp0", "mp1", "mp2"))
            P.emit("sync", lambda h, t0=t0, t1=t1: h.dma_start(
                d_h[:, t0:t1].rearrange("(hc p) t -> p hc t", p=128),
                SCR[:, 0:4096].rearrange("p (hc t) -> p hc t", hc=8)),
                reads=[f"hch{hc}" for hc in range(8)],
                writes=[f"hd{tcq}"], tag=f"hsave{tcq}")
            P.emit("sync", lambda h, t0=t0, t1=t1: h.dma_start(
                d_x1[:, t0:t1].rearrange("(hc p) t -> p hc t", p=128),
                R0.rearrange("p (hc t) -> p hc t", hc=8)[:, :, t0:t1]),
                reads=[f"R0.{hc}" for hc in range(8)],
                writes=[f"x1d{tcq}"], tag=f"x1save{tcq}")

    def mamba_layer(l):
        P.emit("gpsimd_dma", lambda h, l=l: h.dma_start(
            s_wxp[:, :], d_wxp[:, l * 2048:(l + 1) * 2048]),
            writes=["wxp"], tag=f"dwxp{l}")
        P.emit("gpsimd_dma", lambda h, l=l: h.dma_start(
            s_wdt[:, :], d_wdt[:, l * 2048:(l + 1) * 2048]),
            writes=["wdt"], tag=f"dwdt{l}")
        P.emit("gpsimd_dma", lambda h, l=l: h.dma_start(
            s_alog[:, :], d_alog[:, l * 256:(l + 1) * 256]),
            writes=["alog"], tag=f"dalog{l}")
        P.emit("scalar", lambda h: h.activation(s_A[:, :], s_alog[:, :], AF.Exp),
               reads=["alog"], writes=["A"], tag=f"expA{l}")
        P.emit("vector", lambda h: h.tensor_scalar_mul(s_A[:, :], s_A[:, :],
                                                       -1.0),
               reads=["A"], writes=["A"], tag=f"negA{l}")
        P.emit("vector", lambda h: h.memset(XHALO[:, :], 0.0),
               writes=["xhalo"], tag=f"haloz{l}")

        for tcq in range(NTC):
            t0, t1 = tcq * TC, (tcq + 1) * TC
            P.emit("sync", lambda h, t0=t0, t1=t1: h.dma_start(
                SCR[:, 0:4096].rearrange("p (hc t) -> p hc t", hc=8),
                d_h[:, t0:t1].rearrange("(hc p) t -> p hc t", p=128)),
                reads=[(f"hwb{l - 1}.{tcq}" if l > 0 else f"hd{tcq}")],
                writes=[f"hch{hc}" for hc in range(8)], tag=f"hload{l}.{tcq}")
            for hc in range(8):
                sqb = mp(hc % 2).bitcast(BF16)[:, 0:TC]
                P.emit("vector", lambda h, hc=hc, sqb=sqb: h.tensor_mul(
                    sqb, hch(hc), hch(hc)),
                    reads=[f"hch{hc}"], writes=[f"mp{hc % 2}"])
                P.emit("tensor", lambda h, hc=hc, sqb=sqb: h.matmul(
                    pB[0:1, 0:TC], s_onesb[:, 0:1], sqb,
                    start=(hc == 0), stop=(hc == 7)),
                    reads=[f"mp{hc % 2}", "ones"], writes=["pB.b0"])
            P.emit("scalar", lambda h: h.activation(
                ROWS[0:1, 0:512], pB[0:1, 0:TC], AF.Ln, scale=1.0 / H,
                bias=EPS),
                reads=["pB.b0"], writes=["row_t"])
            P.emit("scalar", lambda h: h.activation(
                ROWS[0:1, 1536:2048], ROWS[0:1, 0:512], AF.Exp, scale=-0.5),
                reads=["row_t"], writes=["row_r"])
            P.emit("tensor", lambda h: h.matmul(
                pB[:, 512:1024], s_ones[0:1, 0:128], ROWS[0:1, 1536:2048],
                start=True, stop=True),
                reads=["row_r", "ones"], writes=["pB.b1"])
            for hc in range(8):
                P.emit("vector", lambda h, hc=hc, l=l: h.scalar_tensor_tensor(
                    xnm_t(hc), hch(hc), ncol(s_normw, l * 8 + hc),
                    pB[:, 512:1024], ALU.mult, ALU.mult),
                    reads=[f"hch{hc}", "pB.b1", "normw"], writes=[f"xnm{hc}"])
            def em_win_dma(piece):
                slot = piece % 3
                P.emit("gpsimd_dma", lambda h, l=l, piece=piece, slot=slot:
                       h.dma_start(war(slot),
                                   d_wwin[l, :, piece * 4096:(piece + 1) * 4096]),
                       writes=[f"war{slot}"], tag=f"dwin{l}.{tcq}.{piece}")

            em_win_dma(0)
            em_win_dma(1)
            for oc in range(32):
                if oc % 4 == 0 and oc // 4 + 2 < 8:
                    em_win_dma(oc // 4 + 2)
                slot = (oc // 4) % 3
                colb = (oc % 4) * 1024
                preg = f"pA.b{oc % 2}"
                for hc in range(8):
                    P.emit("tensor", lambda h, oc=oc, hc=hc, slot=slot,
                           colb=colb: h.matmul(
                               pA[:, (oc % 2) * 512:(oc % 2) * 512 + 512],
                               war(slot)[:, colb + hc * 128:
                                         colb + (hc + 1) * 128],
                               xnm_t(hc), start=(hc == 0), stop=(hc == 7)),
                           reads=[f"war{slot}", f"xnm{hc}"], writes=[preg],
                           tag=f"win{l}.{tcq}.{oc}")
                if oc < 16:
                    dc = oc
                    xb = xpb(dc % 2)
                    P.emit("vector", lambda h, xb=xb, oc=oc: h.tensor_copy(
                        xb[:, 3:515],
                        pA[:, (oc % 2) * 512:(oc % 2) * 512 + 512]),
                        reads=[preg], writes=[f"xp{dc % 2}"])
                    P.emit("vector", lambda h, xb=xb, dc=dc: h.tensor_copy(
                        xb[:, 0:3], XHALO[:, dc * 3:(dc + 1) * 3]),
                        reads=["xhalo"], writes=[f"xp{dc % 2}"])
                    P.emit("vector", lambda h, xb=xb, dc=dc, l=l: h.tensor_scalar(
                        mp(4), xb[:, 0:512],
                        ncol(s_convw, l * 64 + dc * 4 + 0), None, ALU.mult),
                        reads=[f"xp{dc % 2}", "convw"], writes=["mp4"])
                    P.emit("vector", lambda h, xb=xb, dc=dc, l=l:
                           h.scalar_tensor_tensor(
                               mp(5), xb[:, 1:513],
                               ncol(s_convw, l * 64 + dc * 4 + 1),
                               mp(4), ALU.mult, ALU.add),
                           reads=[f"xp{dc % 2}", "mp4", "convw"], writes=["mp5"])
                    P.emit("vector", lambda h, xb=xb, dc=dc, l=l:
                           h.scalar_tensor_tensor(
                               mp(4), xb[:, 2:514],
                               ncol(s_convw, l * 64 + dc * 4 + 2),
                               mp(5), ALU.mult, ALU.add),
                           reads=[f"xp{dc % 2}", "mp5", "convw"], writes=["mp4"])
                    P.emit("vector", lambda h, xb=xb, dc=dc, l=l:
                           h.scalar_tensor_tensor(
                               mp(5), xb[:, 3:515],
                               ncol(s_convw, l * 64 + dc * 4 + 3),
                               mp(4), ALU.mult, ALU.add),
                           reads=[f"xp{dc % 2}", "mp4", "convw"], writes=["mp5"])
                    P.emit("vector", lambda h, xb=xb, dc=dc: h.tensor_copy(
                        XHALO[:, dc * 3:(dc + 1) * 3], xb[:, 512:515]),
                        reads=[f"xp{dc % 2}"], writes=["xhalo"],
                        tag=f"halo{l}.{tcq}.{dc}")
                    P.emit("scalar", lambda h, dc=dc, l=l: h.activation(
                        xc_t(dc), mp(5), AF.Silu,
                        bias=ncol(s_convb, l * 16 + dc)),
                        reads=["mp5", "convb"], writes=[f"xc{dc}"],
                        tag=f"xcsilu{l}.{tcq}.{dc}")
                else:
                    dc2 = oc - 16
                    P.emit("scalar", lambda h, dc2=dc2, oc=oc: h.activation(
                        zs_t(dc2), pA[:, (oc % 2) * 512:(oc % 2) * 512 + 512],
                        AF.Silu),
                        reads=[preg], writes=[f"zs{dc2}"],
                        tag=f"zsilu{l}.{tcq}.{dc2}")
            for dc in range(16):
                P.emit("tensor", lambda h, dc=dc: h.matmul(
                    pC[:, 0:512], s_wxp[:, dc * 128:(dc + 1) * 128],
                    xc_t(dc), start=(dc == 0), stop=(dc == 15)),
                    reads=["wxp", f"xc{dc}"], writes=["pC.b0"],
                    tag=f"wx{l}.{tcq}")
            P.emit("vector", lambda h: h.tensor_copy(
                DTBC[0:16, :], pC[0:16, 0:512]),
                reads=["pC.b0"], writes=["dtbcC"])
            P.emit("vector", lambda h: h.tensor_copy(
                DTBC[32:48, :], pC[32:48, 0:512]),
                reads=["pC.b0"], writes=["dtbcB"])
            P.emit("vector", lambda h: h.tensor_copy(
                DTBC[64:128, :], pC[64:128, 0:512]),
                reads=["pC.b0"], writes=["dtbcD"])
            for n in range(16):
                pb = f"pC.b{1 + n % 2}"
                off = 512 + (n % 2) * 512
                P.emit("tensor", lambda h, n=n, off=off: h.matmul(
                    pC[:, off:off + 512],
                    s_oneh16[0:16, n:n + 1].broadcast_to((16, 128)),
                    DTBC[0:16, :], start=True, stop=True),
                    reads=["dtbcC", "oneh16"], writes=[pb], tag="cbcast")
                P.emit("vector", lambda h, n=n, off=off: h.tensor_copy(
                    crep_t(n), pC[:, off:off + 512]),
                    reads=[pb], writes=[f"crep{n}"])
                P.emit("tensor", lambda h, n=n, off=off: h.matmul(
                    pC[:, off:off + 512],
                    s_oneh16[32:48, n:n + 1].broadcast_to((16, 128)),
                    DTBC[32:48, :], start=True, stop=True),
                    reads=["dtbcB", "oneh16"], writes=[pb], tag="bbcast")
                P.emit("vector", lambda h, n=n, off=off: h.tensor_copy(
                    brep_t(n), pC[:, off:off + 512]),
                    reads=[pb], writes=[f"brep{n}"])
            for dc in range(16):
                pr = dc % 2
                pDt = f"pB.b{pr}"
                sS, sE, sL = (mp(4), mp(5), mp(6)) if pr == 0 else \
                             (mp(7), mp(8), mp(9))
                rS, rE, rL = (("mp4", "mp5", "mp6") if pr == 0 else
                              ("mp7", "mp8", "mp9"))
                delta, rD = mp(pr), f"mp{pr}"
                du, rU = mp(2 + pr), f"mp{2 + pr}"
                P.emit("tensor", lambda h, dc=dc, pr=pr: h.matmul(
                    pB[:, pr * 512: pr * 512 + 512],
                    s_wdt[64:128, dc * 128:(dc + 1) * 128],
                    DTBC[64:128, :], start=True, stop=True),
                    reads=["wdt", "dtbcD"], writes=[pDt],
                    tag=f"wdt{l}.{tcq}.{dc}")
                P.emit("scalar", lambda h, pr=pr, l=l, dc=dc, sS=sS:
                       h.activation(sS, pB[:, pr * 512: pr * 512 + 512],
                                    AF.Identity, bias=ncol(s_bdt, l * 16 + dc)),
                       reads=[pDt, "bdt"], writes=[rS])
                P.emit("scalar", lambda h, pr=pr, l=l, dc=dc, sE=sE:
                       h.activation(sE, pB[:, pr * 512: pr * 512 + 512],
                                    AF.Abs, bias=ncol(s_bdt, l * 16 + dc)),
                       reads=[pDt, "bdt"], writes=[rE])
                P.emit("scalar", lambda h, sE=sE: h.activation(
                    sE, sE, AF.Exp, scale=-1.0), reads=[rE], writes=[rE])
                P.emit("scalar", lambda h, sE=sE, sL=sL: h.activation(
                    sL, sE, AF.Ln, bias=1.0), reads=[rE], writes=[rL])
                P.emit("vector", lambda h, sS=sS: h.tensor_scalar_max(
                    sS, sS, 0.0), reads=[rS], writes=[rS])
                P.emit("vector", lambda h, sS=sS, sL=sL, delta=delta:
                       h.tensor_add(delta, sS, sL),
                       reads=[rS, rL], writes=[rD])
                P.emit("vector", lambda h, delta=delta, dc=dc, du=du:
                       h.tensor_tensor(du, delta, xc_t(dc), ALU.mult),
                       reads=[rD, f"xc{dc}"], writes=[rU])
                py = f"pC.b{pr}"
                pyo = pr * 512
                P.emit("vector", lambda h, dc=dc, l=l, pyo=pyo:
                       h.tensor_scalar(pC[:, pyo:pyo + 512], xc_t(dc),
                                       ncol(s_dpv, l * 16 + dc), None,
                                       ALU.mult),
                       reads=[f"xc{dc}", "dpv"], writes=[py],
                       tag=f"yinit{l}.{tcq}.{dc}")
                def em_yacc(n):
                    npr = n % 2
                    ch, rC = mp(12 + npr), f"mp{12 + npr}"
                    P.emit("vector", lambda h, ch=ch, pyo=pyo:
                           h.tensor_tensor(pC[:, pyo:pyo + 512],
                                           pC[:, pyo:pyo + 512], ch, ALU.add),
                           reads=[rC, py], writes=[py],
                           tag=f"yacc{l}.{tcq}.{dc}.{n}")

                for n in range(16):
                    npr = n % 2
                    dA = f"pA.b{npr}"
                    dAo = npr * 512
                    dbu, rB = mp(10 + npr), f"mp{10 + npr}"
                    ch, rC = mp(12 + npr), f"mp{12 + npr}"
                    P.emit("scalar", lambda h, n=n, dc=dc, delta=delta,
                           dAo=dAo: h.activation(
                               pA[:, dAo:dAo + 512], delta, AF.Exp,
                               scale=ncol(s_A, dc * 16 + n)),
                           reads=[rD, "A"], writes=[dA],
                           tag=f"dA{l}.{tcq}.{dc}.{n}")
                    P.emit("vector", lambda h, du=du, n=n, dbu=dbu:
                           h.tensor_tensor(dbu, du, brep_t(n), ALU.mult),
                           reads=[rU, f"brep{n}"], writes=[rB])
                    P.emit("vector", lambda h, dbu=dbu, dc=dc, n=n, tcq=tcq,
                           dAo=dAo: h.tensor_tensor_scan(
                               dbu, pA[:, dAo:dAo + 512], dbu,
                               (0.0 if tcq == 0 else
                                ncol(CARRY, dc * 16 + n)),
                               ALU.mult, ALU.add),
                           reads=([dA, rB] +
                                  ([f"carry.{dc}.{n}!"] if tcq > 0 else [])),
                           writes=[rB],
                           tag=f"scan{l}.{tcq}.{dc}.{n}")
                    P.emit("vector", lambda h, dbu=dbu, dc=dc, n=n:
                           h.tensor_copy(ncol(CARRY, dc * 16 + n),
                                         dbu[:, 511:512]),
                           reads=[rB], writes=[f"carry.{dc}.{n}"])
                    P.emit("gpsimd", lambda h, dbu=dbu, n=n, ch=ch:
                           h.tensor_tensor(ch, dbu, crep_t(n), ALU.mult),
                           reads=[rB, f"crep{n}"], writes=[rC])
                    if n > 0:
                        em_yacc(n - 1)
                em_yacc(15)
                P.emit("vector", lambda h, dc=dc, pyo=pyo: h.tensor_tensor(
                    xc_t(dc), pC[:, pyo:pyo + 512], zs_t(dc), ALU.mult),
                    reads=[py, f"zs{dc}"], writes=[f"ybf{dc}", f"xc{dc}"],
                    tag=f"ygate{l}.{tcq}.{dc}")
            def em_wblk_dma(piece):
                slot = 3 + piece % 2
                P.emit("gpsimd_dma", lambda h, l=l, piece=piece, slot=slot:
                       h.dma_start(war(slot),
                                   d_wblk[l, :, piece * 4096:(piece + 1) * 4096]),
                       writes=[f"war{slot}"], tag=f"dwblk{l}.{tcq}.{piece}")

            em_wblk_dma(0)
            for hc in range(8):
                if hc % 2 == 0 and hc // 2 + 1 < 4:
                    em_wblk_dma(hc // 2 + 1)
                ppr = f"pC.b{2 + hc % 2}"
                poff = 1024 + (hc % 2) * 512
                piece = hc // 2
                slot = 3 + piece % 2
                for dc in range(16):
                    inner = (hc % 2) * 2048 + dc * 128
                    P.emit("tensor", lambda h, hc=hc, dc=dc, slot=slot,
                           inner=inner, poff=poff: h.matmul(
                               pC[:, poff:poff + 512],
                               war(slot)[:, inner:inner + 128],
                               xc_t(dc), start=(dc == 0), stop=(dc == 15)),
                           reads=[f"war{slot}", f"ybf{dc}"], writes=[ppr],
                           tag=f"wblk{l}.{tcq}.{hc}")
                P.emit("vector", lambda h, hc=hc, poff=poff: h.tensor_tensor(
                    hch(hc), hch(hc), pC[:, poff:poff + 512], ALU.add),
                    reads=[ppr, f"hch{hc}"], writes=[f"hch{hc}"],
                    tag=f"hres{l}.{tcq}.{hc}")
            P.emit("sync", lambda h, t0=t0, t1=t1: h.dma_start(
                d_h[:, t0:t1].rearrange("(hc p) t -> p hc t", p=128),
                SCR[:, 0:4096].rearrange("p (hc t) -> p hc t", hc=8)),
                reads=[f"hch{hc}" for hc in range(8)],
                writes=[f"hwb{l}.{tcq}"], tag=f"hstore{l}.{tcq}")

    def wout_phase():
        P.emit("gpsimd_dma", lambda h: h.dma_start(
            war(0), d_wout[:, 0:4096]), writes=["war0"], tag="dwo0")
        P.emit("gpsimd_dma", lambda h: h.dma_start(
            war(1), d_wout[:, 4096:8192]), writes=["war1"], tag="dwo1")
        for tcq in range(NTC):
            t0, t1 = tcq * TC, (tcq + 1) * TC
            P.emit("sync", lambda h, t0=t0, t1=t1: h.dma_start(
                SCR[:, 0:4096].rearrange("p (hc t) -> p hc t", hc=8),
                d_h[:, t0:t1].rearrange("(hc p) t -> p hc t", p=128)),
                reads=[f"hwb{NL - 1}.{tcq}"],
                writes=[f"hch{hc}" for hc in range(8)], tag=f"hload4.{tcq}")
            P.emit("sync", lambda h, t0=t0, t1=t1: h.dma_start(
                R0.rearrange("p (hc t) -> p hc t", hc=8)[:, :, t0:t1],
                d_x1[:, t0:t1].rearrange("(hc p) t -> p hc t", p=128)),
                reads=[f"x1d{tcq}"],
                writes=[f"R0.{hc}" for hc in range(8)], tag=f"x1load{tcq}")
            for hc in range(8):
                P.emit("vector", lambda h, hc=hc: h.tensor_copy(
                    xnm_t(hc), hch(hc)),
                    reads=[f"hch{hc}"], writes=[f"xnm{hc}"])
            for oc in range(8):
                pr = f"pA.b{oc % 2}"
                poff = (oc % 2) * 512
                for hc in range(8):
                    P.emit("tensor", lambda h, oc=oc, hc=hc, poff=poff:
                           h.matmul(
                               pA[:, poff:poff + 512],
                               war(oc // 4)[:, (oc % 4) * 1024 + hc * 128:
                                            (oc % 4) * 1024 + (hc + 1) * 128],
                               xnm_t(hc), start=(hc == 0), stop=(hc == 7)),
                           reads=[f"war{oc // 4}", f"xnm{hc}"], writes=[pr],
                           tag=f"wout{tcq}.{oc}")
                P.emit("scalar", lambda h, oc=oc, poff=poff: h.activation(
                    mp(10), pA[:, poff:poff + 512], AF.Identity,
                    bias=ncol(s_bout, oc)),
                    reads=[pr, "bout"], writes=["mp10"])
                P.emit("vector", lambda h, oc=oc, t0=t0, t1=t1: h.tensor_add(
                    r0x(oc, t0, t1), r0x(oc, t0, t1), mp(10)),
                    reads=["mp10", f"R0.{oc}"], writes=[f"R0.{oc}"],
                    tag=f"x2{tcq}.{oc}")

    def dbg_store():
        pass

    def final_store():
        dbg_store()
        P.emit("sync", lambda h: h.dma_start(
            d_out.rearrange("(hc p) t -> p hc t", p=128),
            R0.rearrange("p (hc t) -> p hc t", hc=8)),
            reads=[f"R0.{hc}" for hc in range(8)], writes=["outd"],
            tag="outstore")

    ffn_prologue(1)
    ffn_sweep(1)
    if stop_stage == "ffn1":
        final_store()
    else:
        P.barrier("postffn1")
        mamba_entry()
        P.barrier("postentry")
        nlayers = NL
        if stop_stage and stop_stage.startswith("mamba"):
            nlayers = int(stop_stage[5:])
        for l in range(nlayers):
            mamba_layer(l)
        if stop_stage and stop_stage.startswith("mamba"):
            for tcq in range(NTC):
                t0, t1 = tcq * TC, (tcq + 1) * TC
                P.emit("sync", lambda h, t0=t0, t1=t1:
                       h.dma_start(d_out[:, t0:t1], d_h[:, t0:t1]),
                       reads=[(f"hwb{nlayers - 1}.{tcq}" if nlayers > 0 else
                               f"hd{tcq}")],
                       writes=["outd"], tag=f"hdump{tcq}")
        else:
            P.barrier("postmamba")
            wout_phase()
            if stop_stage == "wout":
                final_store()
            else:
                P.barrier("postwout")
                ffn_prologue(2)
                ffn_sweep(2)
                final_store()

    P.materialize(block, sems)
    stack.close()
    return nc


def _pack_inputs(inputs):
    f = np.float32
    ins = {k: np.asarray(v) for k, v in inputs.items()}

    def pack_gu(w):
        return np.ascontiguousarray(
            w.reshape(8, 128, 4, 128).transpose(1, 0, 2, 3).reshape(128, 4096))

    def pack_ed(w):
        return np.ascontiguousarray(
            w.reshape(4, 128, 8, 128).transpose(1, 0, 2, 3).reshape(128, 4096))

    sgT = np.ascontiguousarray(ins["L0_ffn_sg_w"].T.astype(f))
    suT = np.ascontiguousarray(ins["L0_ffn_su_w"].T.astype(f))
    sdT = np.ascontiguousarray(ins["L0_ffn_sd_w"].T.astype(f))
    wff = np.empty((NE, 3, 128, 4096), dtype=f)
    wff[0, 0] = pack_gu(sgT[:, 0:512])
    wff[0, 1] = pack_gu(suT[:, 0:512])
    wff[0, 2] = pack_ed(sdT[0:512, :])
    wff[1, 0] = pack_gu(sgT[:, 512:1024])
    wff[1, 1] = pack_gu(suT[:, 512:1024])
    wff[1, 2] = pack_ed(sdT[512:1024, :])
    egT = ins["L0_ffn_eg_w"].transpose(0, 2, 1)
    euT = ins["L0_ffn_eu_w"].transpose(0, 2, 1)
    edT = ins["L0_ffn_ed_w"].transpose(0, 2, 1)
    for eidx in range(E):
        wff[2 + eidx, 0] = pack_gu(egT[eidx])
        wff[2 + eidx, 1] = pack_gu(euT[eidx])
        wff[2 + eidx, 2] = pack_ed(edT[eidx])

    gateT = np.ascontiguousarray(
        ins["L0_ffn_gate_w"].T.reshape(8, 128, E).transpose(1, 0, 2)
        .reshape(128, 8 * E).astype(f))
    gbias = ins["L0_ffn_gate_bias"].reshape(1, E).astype(f)

    Win = ins["L0_attn_Win"].astype(f)
    wwin = np.ascontiguousarray(
        Win.reshape(NL, 8, 128, 32, 128).transpose(0, 2, 3, 1, 4)
        .reshape(NL, 128, 32768))
    Wblk = ins["L0_attn_Wblk"].astype(f)
    wblk = np.ascontiguousarray(
        Wblk.reshape(NL, 16, 128, 8, 128).transpose(0, 2, 3, 1, 4)
        .reshape(NL, 128, 16384))
    Wx = ins["L0_attn_Wx"].astype(f)
    WxP = np.zeros((NL, DI, 128), dtype=f)
    WxP[:, :, 0:16] = Wx[:, :, 80:96]
    WxP[:, :, 32:48] = Wx[:, :, 64:80]
    WxP[:, :, 64:128] = Wx[:, :, 0:64]
    wxp = np.ascontiguousarray(
        WxP.reshape(NL, 16, 128, 128).transpose(2, 0, 1, 3)
        .reshape(128, NL * 2048))
    Wdt = ins["L0_attn_Wdt"].astype(f)
    wdt = np.zeros((128, NL * 2048), dtype=f)
    for l in range(NL):
        wdt[64:128, l * 2048:(l + 1) * 2048] = Wdt[l]
    Wout = ins["L0_attn_Wout"].astype(f)
    woutp = np.ascontiguousarray(
        Wout.reshape(8, 128, 8, 128).transpose(1, 2, 0, 3).reshape(128, 8192))

    def cols(a):
        return np.ascontiguousarray(
            a.reshape(NL, 16, 128).transpose(2, 0, 1).reshape(128, NL * 16)
            .astype(f))

    normw = np.ascontiguousarray(
        ins["L0_attn_norm_w"].reshape(NL, 8, 128).transpose(2, 0, 1)
        .reshape(128, NL * 8).astype(f))
    convw = np.ascontiguousarray(
        ins["L0_attn_conv_w"].reshape(NL, 16, 128, KC).transpose(2, 0, 1, 3)
        .reshape(128, NL * 64).astype(f))
    convb = cols(ins["L0_attn_conv_b"])
    bdt = cols(ins["L0_attn_bdt"])
    dpv = cols(ins["L0_attn_Dp"])
    alog = np.ascontiguousarray(
        ins["L0_attn_A_log"].reshape(NL, 16, 128, DS).transpose(2, 0, 1, 3)
        .reshape(128, NL * 256).astype(f))
    boutc = np.ascontiguousarray(
        ins["L0_attn_bout"].reshape(8, 128).T.astype(f))

    ident = np.eye(128, dtype=f)
    oneh = np.eye(E, dtype=f)
    oneh16 = np.zeros((48, 16), dtype=f)
    oneh16[0:16] = np.eye(16)
    oneh16[32:48] = np.eye(16)

    common = dict(wff=wff, gateT=gateT, gbias=gbias, wwin=wwin, wblk=wblk,
                  wxp=wxp, wdt=wdt, woutp=woutp, normw=normw, convw=convw,
                  convb=convb, bdt=bdt, dpv=dpv, alog=alog, boutc=boutc,
                  ident=ident, oneh=oneh, oneh16=oneh16)
    xs = [np.ascontiguousarray(ins["L0_attn_x"][b].T.astype(f))
          for b in range(B)]
    return common, xs


def kernel(**inputs):
    common, xs = _pack_inputs(inputs)
    nc = build_program()
    in_maps = [dict(common, x=xs[b]) for b in range(B)]
    res = run_bass_kernel_spmd(nc, in_maps, core_ids=list(range(B)))
    out = np.stack([np.asarray(r["out"]).T for r in res.results], axis=0)
    return np.ascontiguousarray(out.astype(np.float32))


if __name__ == "__main__":
    import sys
    nc = build_program(sys.argv[1] if len(sys.argv) > 1 else None)
    print("built ok")


# version 05 (final)
# speedup vs baseline: 1.3859x; 1.1452x over previous
import numpy as np
import concourse.bass as bass
import concourse.mybir as mybir
from concourse.bass_utils import run_bass_kernel_spmd
from contextlib import ExitStack

F32 = mybir.dt.float32
BF16 = mybir.dt.bfloat16
AF = mybir.ActivationFunctionType
ALU = mybir.AluOpType
AX = mybir.AxisListType

B, T, H = 8, 2048, 1024
NL, DI, DS, DTR, KC = 4, 2048, 16, 64, 4
E, IE, ISH = 64, 512, 1024
NE = E + 2
TC = 512
NTC = T // TC
TCF = 256
NTCF = T // TCF
EPS = 1e-5

QUEUES = ("tensor", "vector", "scalar", "gpsimd", "gpsimd_dma", "sync")
DMA_Q = ("gpsimd_dma", "sync")


class Prog:
    def __init__(self):
        self.ops = []
        self.last_writer = {}
        self.last_reader = {}

    SELF_SYNC = ("vector", "scalar", "gpsimd")

    def emit(self, eng, fn, reads=(), writes=(), tag=""):
        gi = len(self.ops)
        selfs = eng in self.SELF_SYNC
        deps = set()
        for r in reads:
            rr = r[:-1] if r.endswith("!") else r
            w = self.last_writer.get(rr)
            if w is not None and (w[0] != eng or selfs):
                deps.add(w[1])
        for r in writes:
            w = self.last_writer.get(r)
            if w is not None and (w[0] != eng or selfs):
                deps.add(w[1])
            for re, ri in self.last_reader.get(r, {}).items():
                if re != eng or selfs:
                    deps.add(ri)
        self.ops.append(dict(eng=eng, fn=fn, gi=gi, deps=deps, tag=tag))
        for r in writes:
            self.last_writer[r] = (eng, gi)
            self.last_reader[r] = {}
        for r in reads:
            rr = r[:-1] if r.endswith("!") else r
            self.last_reader.setdefault(rr, {})[eng] = gi
        return gi

    def barrier(self, tag=""):
        lasts = {}
        for rec in self.ops:
            lasts[rec["eng"]] = rec["gi"]

        def nopfn(h):
            return h.nop()

        for q in ("tensor", "vector", "scalar", "gpsimd", "sync"):
            gi = len(self.ops)
            deps = {g for e, g in lasts.items() if e != q}
            self.ops.append(dict(eng=q, fn=nopfn, gi=gi, deps=deps,
                                 tag=f"bar_{q}_{tag}"))

    def materialize(self, block, sems):
        for rec in self.ops:
            byeng = {}
            for d in rec["deps"]:
                de = self.ops[d]["eng"]
                if byeng.get(de, -1) < d:
                    byeng[de] = d
            rec["deps"] = set(byeng.values())
        needed = set()
        for rec in self.ops:
            needed.update(rec["deps"])
            if rec["eng"] in DMA_Q:
                needed.add(rec["gi"])
        inc_no = {}
        cum = {k: 0 for k in QUEUES}
        for rec in self.ops:
            if rec["gi"] in needed:
                cum[rec["eng"]] += 16 if rec["eng"] in DMA_Q else 1
                inc_no[rec["gi"]] = cum[rec["eng"]]
        streams = {k: [] for k in QUEUES}
        hw = {k: {s: 0 for s in QUEUES} for k in QUEUES}
        for rec in self.ops:
            eng = rec["eng"]
            waits = []
            for d in sorted(rec["deps"]):
                de = self.ops[d]["eng"]
                tgt = inc_no[d]
                if de == eng:
                    waits.append((de, tgt))
                elif hw[eng][de] < tgt:
                    hw[eng][de] = tgt
                    waits.append((de, tgt))
            streams[eng].append((waits, rec["fn"], inc_no.get(rec["gi"]),
                                 rec["tag"], eng))
        self.cum = cum
        self.n_ops = len(self.ops)

        def run(handle, items):
            for waits, fn, inc, tag, myeng in items:
                for (seng, val) in waits:
                    handle.wait_ge(sems[seng], val)
                instr = fn(handle)
                if inc is not None:
                    assert instr is not None, f"need inc on {tag}"
                    instr.then_inc(sems[myeng], 16 if myeng in DMA_Q else 1)

        @block.tensor
        def _(t):
            run(t, streams["tensor"])

        @block.vector
        def _(v):
            run(v, streams["vector"])

        @block.scalar
        def _(s):
            run(s, streams["scalar"])

        @block.gpsimd
        def _(g):
            merged = []
            ig = ic = 0
            for rec in self.ops:
                if rec["eng"] == "gpsimd":
                    merged.append(streams["gpsimd"][ig]); ig += 1
                elif rec["eng"] == "gpsimd_dma":
                    merged.append(streams["gpsimd_dma"][ic]); ic += 1
            run(g, merged)

        @block.sync
        def _(sy):
            run(sy, streams["sync"])
            sy.wait_ge(sems["sync"], cum["sync"])


def build_program(stop_stage=None, dbg_dump=False, n_experts=NE, n_tcf=NTCF):
    nc = bass.Bass("TRN2", target_bir_lowering=False,
                   detect_race_conditions=False)

    d_x = nc.dram_tensor("x", [H, T], F32, kind="ExternalInput").ap()
    d_wff = nc.dram_tensor("wff", [NE, 3, 128, 4096], F32, kind="ExternalInput").ap()
    d_gateT = nc.dram_tensor("gateT", [128, 512], F32, kind="ExternalInput").ap()
    d_gbias = nc.dram_tensor("gbias", [1, E], F32, kind="ExternalInput").ap()
    d_wwin = nc.dram_tensor("wwin", [NL, 128, 32768], F32, kind="ExternalInput").ap()
    d_wblk = nc.dram_tensor("wblk", [NL, 128, 16384], F32, kind="ExternalInput").ap()
    d_wxp = nc.dram_tensor("wxp", [128, NL * 2048], F32, kind="ExternalInput").ap()
    d_wdt = nc.dram_tensor("wdt", [128, NL * 2048], F32, kind="ExternalInput").ap()
    d_wout = nc.dram_tensor("woutp", [128, 8192], F32, kind="ExternalInput").ap()
    d_normw = nc.dram_tensor("normw", [128, NL * 8], F32, kind="ExternalInput").ap()
    d_convw = nc.dram_tensor("convw", [128, NL * 64], F32, kind="ExternalInput").ap()
    d_convb = nc.dram_tensor("convb", [128, NL * 16], F32, kind="ExternalInput").ap()
    d_bdt = nc.dram_tensor("bdt", [128, NL * 16], F32, kind="ExternalInput").ap()
    d_dpv = nc.dram_tensor("dpv", [128, NL * 16], F32, kind="ExternalInput").ap()
    d_alog = nc.dram_tensor("alog", [128, NL * 256], F32, kind="ExternalInput").ap()
    d_bout = nc.dram_tensor("boutc", [128, 8], F32, kind="ExternalInput").ap()
    d_ident = nc.dram_tensor("ident", [128, 128], F32, kind="ExternalInput").ap()
    d_oneh = nc.dram_tensor("oneh", [E, E], F32, kind="ExternalInput").ap()
    d_oneh16 = nc.dram_tensor("oneh16", [48, 16], F32, kind="ExternalInput").ap()

    d_out = nc.dram_tensor("out", [H, T], F32, kind="ExternalOutput").ap()
    d_dbg = (nc.dram_tensor("dbg", [128, 16384], F32, kind="ExternalOutput").ap()
             if dbg_dump else None)
    d_x1 = nc.dram_tensor("x1scr", [H, T], F32).ap()
    d_h = nc.dram_tensor("hscr", [H, T], F32).ap()
    d_comb = nc.dram_tensor("combscr", [E, T], BF16,
                            kind=("ExternalOutput" if dbg_dump else
                                  "Internal")).ap()

    stack = ExitStack()
    en = stack.enter_context

    R0 = en(nc.sbuf_tensor("R0", [128, 16384], F32))
    SCR = en(nc.sbuf_tensor("SCR", [128, 16384], F32))
    WARN = en(nc.sbuf_tensor("WARN", [128, 20480], BF16))
    ROWS = en(nc.sbuf_tensor("ROWS", [1, 2048], F32))
    ROWSB = en(nc.sbuf_tensor("ROWSB", [1, 1024], BF16))
    s_onesb = en(nc.sbuf_tensor("s_onesb", [128, 128], BF16))
    s_identb = en(nc.sbuf_tensor("s_identb", [128, 128], BF16))
    s_gbb = en(nc.sbuf_tensor("s_gbb", [1, E], BF16))
    CARRY = en(nc.sbuf_tensor("CARRY", [128, 256], F32))
    DTBC = en(nc.sbuf_tensor("DTBC", [128, 512], BF16))
    s_wxp = en(nc.sbuf_tensor("s_wxp", [128, 2048], BF16))
    s_wdt = en(nc.sbuf_tensor("s_wdt", [128, 2048], BF16))
    s_gateT = en(nc.sbuf_tensor("s_gateT", [128, 512], F32))
    s_gateTb = en(nc.sbuf_tensor("s_gateTb", [128, 512], BF16))
    s_normw = en(nc.sbuf_tensor("s_normw", [128, NL * 8], F32))
    s_convw = en(nc.sbuf_tensor("s_convw", [128, NL * 64], F32))
    s_convb = en(nc.sbuf_tensor("s_convb", [128, NL * 16], F32))
    s_bdt = en(nc.sbuf_tensor("s_bdt", [128, NL * 16], F32))
    s_dpv = en(nc.sbuf_tensor("s_dpv", [128, NL * 16], F32))
    s_alog = en(nc.sbuf_tensor("s_alog", [128, 256], F32))
    s_A = en(nc.sbuf_tensor("s_A", [128, 256], F32))
    s_bout = en(nc.sbuf_tensor("s_bout", [128, 8], F32))
    s_ident = en(nc.sbuf_tensor("s_ident", [128, 128], F32))
    s_oneh = en(nc.sbuf_tensor("s_oneh", [E, E], BF16))
    s_oneh16 = en(nc.sbuf_tensor("s_oneh16", [48, 16], BF16))
    s_ones = en(nc.sbuf_tensor("s_ones", [128, 128], F32))
    s_gb = en(nc.sbuf_tensor("s_gb", [1, E], F32))
    s_biasb = en(nc.sbuf_tensor("s_biasb", [128, E], F32))
    XHALO = en(nc.sbuf_tensor("XHALO", [128, 48], F32))
    s_const = en(nc.sbuf_tensor("s_const", [128, 8], F32))

    pA = en(nc.psum_tensor("pA", [128, 1024], F32))
    pB = en(nc.psum_tensor("pB", [128, 1024], F32))
    pC = en(nc.psum_tensor("pC", [128, 2048], F32))

    sems = {q: en(nc.semaphore(f"s_{q}")) for q in QUEUES}
    block = en(nc.Block())
    P = Prog()

    R0bf = R0[:, :].bitcast(BF16)
    SCRbf = SCR[:, :].bitcast(BF16)

    def r0x(hc, t0, t1):
        return R0[:, hc * 2048 + t0: hc * 2048 + t1]

    def war(s):
        return WARN[:, s * 4096:(s + 1) * 4096]

    def xnf_t(hc):
        return SCR[:, hc * 256:(hc + 1) * 256]

    def sg_t(ic):
        return SCR[:, 2048 + ic * 256: 2048 + (ic + 1) * 256]

    def uc_t(ic):
        return SCR[:, 3072 + ic * 256: 3072 + (ic + 1) * 256]

    def sh_t(ic, par=0):
        c = 8192 + par * 1024 + ic * 256
        return SCRbf[:, c: c + 256]

    def combping(par=0):
        return SCRbf[:, 10240: 10240 + 2048]

    RSB = 6656

    def rs(c0, c1):
        return SCR[:, RSB + c0: RSB + c1]

    def xnbf_t(hc, t0, t1):
        return SCRbf[:, 16384 + hc * 2048 + t0: 16384 + hc * 2048 + t1]

    def zs_t(dc):
        return R0bf[:, dc * 512:(dc + 1) * 512]

    def xc_t(dc):
        return R0bf[:, 8192 + dc * 512: 8192 + (dc + 1) * 512]

    def crep_t(n):
        return R0bf[:, 16384 + n * 512: 16384 + (n + 1) * 512]

    def brep_t(n):
        return R0bf[:, 24576 + n * 512: 24576 + (n + 1) * 512]

    def hch(hc):
        return SCR[:, hc * 512:(hc + 1) * 512]

    def xnm_t(hc):
        return SCRbf[:, 8192 + hc * 512: 8192 + (hc + 1) * 512]

    def mp(i, w=512):
        return SCR[:, 6144 + i * 512: 6144 + i * 512 + w]

    def xpb(par):
        c = 13312 + par * 520
        return SCR[:, c: c + 520]

    def ncol(t, c):
        return t[:, c:c + 1]

    P.emit("sync", lambda h: h.dma_start(
        R0.rearrange("p (hc t) -> p hc t", hc=8),
        d_x.rearrange("(hc p) t -> p hc t", p=128)),
        writes=[f"R0.{hc}" for hc in range(8)], tag="ldx")
    for nm, dst, src in [("gateT", s_gateT, d_gateT), ("normw", s_normw, d_normw),
                         ("convw", s_convw, d_convw), ("convb", s_convb, d_convb),
                         ("bdt", s_bdt, d_bdt), ("dpv", s_dpv, d_dpv),
                         ("bout", s_bout, d_bout), ("ident", s_ident, d_ident),
                         ("gb", s_gb, d_gbias)]:
        P.emit("sync", lambda h, dst=dst, src=src: h.dma_start(dst[:, :], src),
               writes=[nm], tag=f"ld_{nm}")
    P.emit("gpsimd_dma", lambda h: h.dma_start(s_oneh[:, :], d_oneh),
           writes=["oneh"])
    P.emit("gpsimd_dma", lambda h: h.dma_start(s_gateTb[:, :], d_gateT),
           writes=["gateTb"])
    P.emit("gpsimd_dma", lambda h: h.dma_start(s_oneh16[:, :], d_oneh16),
           writes=["oneh16"])
    P.emit("vector", lambda h: h.memset(s_ones[:, :], 1.0), writes=["ones"])
    P.emit("vector", lambda h: h.memset(s_onesb[:, :], 1.0), writes=["ones"])
    P.emit("gpsimd_dma", lambda h: h.dma_start(s_identb[:, :], d_ident),
           writes=["identb"])
    P.emit("gpsimd_dma", lambda h: h.dma_start(s_gbb[:, :], d_gbias),
           writes=["gbb"])
    _CONSTS = [EPS, 1.0, 1.0 / H, -0.5, -1.0, 1e-8]
    for ci, cv in enumerate(_CONSTS):
        P.emit("vector", lambda h, ci=ci, cv=cv: h.memset(
            s_const[:, ci:ci + 1], cv), writes=["constap"])
        nc.const_aps.register(F32, cv, s_const[:, ci:ci + 1])
    P.emit("scalar", lambda h: h.copy(ROWS[0:1, 0:1], s_const[0:1, 1:2]),
           reads=["constap"], writes=["row_t"], tag="constfence_s")
    P.emit("vector", lambda h: h.tensor_copy(ROWS[0:1, 1:2], s_const[0:1, 1:2]),
           reads=["constap"], writes=["row_t"], tag="constfence_v")
    P.emit("tensor", lambda h: h.matmul(pA[:, 0:E], s_ones[0:1, 0:128],
                                        s_gb[0:1, :], start=True, stop=True),
           reads=["ones", "gb"], writes=["pA.b0"])
    P.emit("vector", lambda h: h.tensor_copy(s_biasb[:, :], pA[:, 0:E]),
           reads=["pA.b0"], writes=["biasb"])
    P.emit("vector", lambda h: h.memset(SCR[:, :], 0.0), writes=["SCRDUMP"],
           tag="scrinit")
    P.emit("vector", lambda h: h.memset(CARRY[:, :], 0.0), writes=["carryinit"])
    P.emit("vector", lambda h: h.memset(DTBC[:, :], 0.0), writes=["dtbcinit"])
    P.barrier("postinit")

    def layernorm(dst_fn, dst_regions, src_fn, src_regions, W, scratch,
                  scr_regions, also_bf=None, bf_regions=None):
        s0, s1, s2 = scratch
        r0, r1, r2 = scr_regions
        xb = s0.bitcast(BF16)[:, 0:W]
        sqb = s1.bitcast(BF16)[:, 0:W]
        for hc in range(8):
            sap = src_fn(hc)
            P.emit("vector", lambda h, sap=sap, xb=xb: h.tensor_copy(xb, sap),
                   reads=[src_regions[hc]], writes=[r0])
            P.emit("vector", lambda h, sap=sap, sqb=sqb: h.tensor_mul(
                sqb, sap, sap),
                reads=[src_regions[hc]], writes=[r1])
            P.emit("tensor", lambda h, hc=hc, sqb=sqb: h.matmul(
                pB[0:1, 0:W], s_onesb[:, 0:1], sqb,
                start=(hc == 0), stop=(hc == 7)),
                reads=[r1, "ones"], writes=["pB.b0"])
            P.emit("tensor", lambda h, hc=hc, xb=xb: h.matmul(
                pA[0:1, 0:W], s_onesb[:, 0:1], xb,
                start=(hc == 0), stop=(hc == 7)),
                reads=[r0, "ones"], writes=["pA.b0"])
        P.emit("scalar", lambda h: h.activation(
            ROWS[0:1, 0:W], pA[0:1, 0:W], AF.Copy, scale=1.0 / H),
            reads=["pA.b0"], writes=["row_m"])

        P.emit("scalar", lambda h: h.activation(
            ROWS[0:1, 512:512 + W], pB[0:1, 0:W], AF.Copy, scale=1.0 / H),
            reads=["pB.b0"], writes=["row_sq"])
        P.emit("vector", lambda h: h.tensor_mul(
            ROWS[0:1, 1024:1024 + W], ROWS[0:1, 0:W], ROWS[0:1, 0:W]),
            reads=["row_m"], writes=["row_t"])
        P.emit("vector", lambda h: h.tensor_sub(
            ROWS[0:1, 1024:1024 + W], ROWS[0:1, 512:512 + W],
            ROWS[0:1, 1024:1024 + W]),
            reads=["row_sq", "row_t"], writes=["row_t"])
        P.emit("scalar", lambda h: h.activation(
            ROWS[0:1, 1024:1024 + W], ROWS[0:1, 1024:1024 + W], AF.Ln,
            bias=EPS),
            reads=["row_t"], writes=["row_t"])
        P.emit("scalar", lambda h: h.activation(
            ROWS[0:1, 1536:1536 + W], ROWS[0:1, 1024:1024 + W], AF.Exp,
            scale=-0.5),
            reads=["row_t"], writes=["row_r"])
        P.emit("tensor", lambda h: h.matmul(
            pB[:, 0:W], s_ones[0:1, 0:128], ROWS[0:1, 0:W],
            start=True, stop=True),
            reads=["row_m", "ones"], writes=["pB.b0"])
        P.emit("tensor", lambda h: h.matmul(
            pB[:, 512:512 + W], s_ones[0:1, 0:128], ROWS[0:1, 1536:1536 + W],
            start=True, stop=True),
            reads=["row_r", "ones"], writes=["pB.b1"])
        for hc in range(8):
            sap = src_fn(hc)
            dap = dst_fn(hc)
            P.emit("vector", lambda h, sap=sap: h.tensor_sub(
                s2, sap, pB[:, 0:W]),
                reads=[src_regions[hc], "pB.b0"], writes=[r2])
            P.emit("vector", lambda h, dap=dap: h.tensor_mul(
                dap, s2, pB[:, 512:512 + W]),
                reads=[r2, "pB.b1"], writes=[dst_regions[hc]])
            if also_bf is not None:
                bap = also_bf(hc)
                P.emit("vector", lambda h, bap=bap, dap=dap: h.tensor_copy(
                    bap, dap),
                    reads=[dst_regions[hc]], writes=[bf_regions[hc]])

    def ffn_prologue(call):
        for tcf in range(n_tcf):
            t0, t1 = tcf * TCF, (tcf + 1) * TCF
            layernorm(
                xnf_t, [f"xnf{hc}" for hc in range(8)],
                lambda hc: r0x(hc, t0, t1), [f"R0.{hc}" for hc in range(8)],
                TCF,
                (sg_t(0), sg_t(1), sg_t(2)), ("sg0", "sg1", "sg2"),
                also_bf=lambda hc: xnbf_t(hc, t0, t1),
                bf_regions=[f"xnb.{hc}" for hc in range(8)])
            for hc in range(8):
                P.emit("tensor", lambda h, hc=hc: h.matmul(
                    pA[0:E, 0:TCF], s_gateT[:, hc * E:(hc + 1) * E],
                    xnf_t(hc), start=(hc == 0), stop=(hc == 7)),
                    reads=[f"xnf{hc}", "gateT"], writes=["pA.b0"], tag="gate")
            P.emit("scalar", lambda h: h.activation(
                rs(0, TCF)[0:E, :], pA[0:E, 0:TCF], AF.Copy),
                reads=["pA.b0"], writes=["lg"])
            for i in range(2):
                P.emit("tensor", lambda h, i=i: h.matmul(
                    pA[:, 256 + i * 64: 256 + (i + 1) * 64],
                    rs(0, TCF)[0:E, i * 128:(i + 1) * 128],
                    s_ident[0:64, 0:64], start=(i == 0), stop=True,
                    skip_group_check=True),
                    reads=["lg", "ident"], writes=["pA.b0"], tag="ltr")
            for i in range(2):
                c0 = 256 + i * 150
                sco = rs(c0, c0 + 64)
                g1 = rs(c0 + 64, c0 + 96)
                g2 = rs(c0 + 96, c0 + 112)
                g3 = rs(c0 + 112, c0 + 120)
                m8g = rs(c0 + 120, c0 + 128)
                gm = rs(c0 + 128, c0 + 136)
                t8 = rs(c0 + 136, c0 + 144)
                ssum = rs(c0 + 144, c0 + 145)
                sinv = rs(c0 + 145, c0 + 146)
                ms = rs(556 + i * 64, 620 + i * 64)
                sel = rs(684 + i * 64, 748 + i * 64)
                cmb = rs(812 + i * 32, 844 + i * 32).bitcast(BF16)
                P.emit("scalar", lambda h, i=i, sco=sco: h.activation(
                    sco, pA[:, 256 + i * 64: 256 + (i + 1) * 64], AF.Sigmoid),
                    reads=["pA.b0"], writes=[f"sco{i}"])
                P.emit("vector", lambda h, sco=sco: h.tensor_add(
                    sco, sco, s_biasb[:, :]),
                    reads=[f"sco{i}", "biasb"], writes=[f"sco{i}"])
                P.emit("vector", lambda h, sco=sco, g1=g1: h.tensor_tensor(
                    g1.rearrange("p (g q) -> p g q", g=8),
                    sco.rearrange("p (g q) -> p g q", g=8)[:, :, 0:4],
                    sco.rearrange("p (g q) -> p g q", g=8)[:, :, 4:8], ALU.max),
                    reads=[f"sco{i}"], writes=[f"g1{i}"])
                P.emit("vector", lambda h, g1=g1, g2=g2: h.tensor_tensor(
                    g2.rearrange("p (g q) -> p g q", g=8),
                    g1.rearrange("p (g q) -> p g q", g=8)[:, :, 0:2],
                    g1.rearrange("p (g q) -> p g q", g=8)[:, :, 2:4], ALU.max),
                    reads=[f"g1{i}"], writes=[f"g2{i}"])
                P.emit("vector", lambda h, g2=g2, g3=g3: h.tensor_tensor(
                    g3, g2[:, 0:16:2], g2[:, 1:16:2], ALU.max),
                    reads=[f"g2{i}"], writes=[f"g3{i}"])
                P.emit("vector", lambda h, g3=g3, m8g=m8g: h.max(m8g, g3),
                       reads=[f"g3{i}"], writes=[f"m8g{i}"])
                P.emit("vector", lambda h, g3=g3, m8g=m8g, gm=gm: h.tensor_scalar(
                    gm, g3, m8g[:, 3:4], None, ALU.is_ge),
                    reads=[f"g3{i}", f"m8g{i}!"], writes=[f"gm{i}"])
                P.emit("vector", lambda h, sco=sco, gm=gm, ms=ms: h.tensor_tensor(
                    ms.rearrange("p (g q) -> p g q", g=8),
                    sco.rearrange("p (g q) -> p g q", g=8),
                    gm.rearrange("p (g o) -> p g o", g=8).broadcast_to(
                        (128, 8, 8)), ALU.mult),
                    reads=[f"sco{i}", f"gm{i}"], writes=[f"ms{i}"])
                P.emit("vector", lambda h, ms=ms, t8=t8: h.max(t8, ms),
                       reads=[f"ms{i}"], writes=[f"t8{i}"])
                P.emit("vector", lambda h, t8=t8, ssum=ssum: h.tensor_reduce(
                    ssum, t8[:, 0:6], AX.X, ALU.add),
                    reads=[f"t8{i}"], writes=[f"ssum{i}"])
                P.emit("vector", lambda h, ssum=ssum, sinv=sinv:
                       h.tensor_scalar_add(sinv, ssum, 1e-8),
                       reads=[f"ssum{i}"], writes=[f"sinv{i}"])
                P.emit("vector", lambda h, sinv=sinv: h.reciprocal(sinv, sinv),
                       reads=[f"sinv{i}"], writes=[f"sinv{i}"])
                P.emit("vector", lambda h, ms=ms, t8=t8, sel=sel: h.tensor_scalar(
                    sel, ms, t8[:, 5:6], None, ALU.is_ge),
                    reads=[f"ms{i}", f"t8{i}!"], writes=[f"sel{i}"])
                P.emit("vector", lambda h, ms=ms, sel=sel: h.tensor_mul(
                    sel, sel, ms),
                    reads=[f"ms{i}", f"sel{i}"], writes=[f"sel{i}"])
                P.emit("vector", lambda h, sel=sel, sinv=sinv, cmb=cmb:
                       h.tensor_scalar(cmb, sel, sinv[:, 0:1], None, ALU.mult),
                       reads=[f"sel{i}", f"sinv{i}!"], writes=[f"cmb{i}"])
            ce = rs(940, 1068).bitcast(BF16)
            for i in range(2):
                P.emit("tensor", lambda h, i=i: h.matmul(
                    pA[0:64, 512 + i * 128: 512 + (i + 1) * 128],
                    rs(812 + i * 32, 844 + i * 32).bitcast(BF16),
                    s_identb[:, :],
                    start=(i == 0), stop=True, skip_group_check=True),
                    reads=[f"cmb{i}", "identb"], writes=["pA.b1"], tag="ctr")
                P.emit("vector", lambda h, i=i, ce=ce: h.tensor_copy(
                    ce[0:64, i * 128:(i + 1) * 128],
                    pA[0:64, 512 + i * 128: 512 + (i + 1) * 128]),
                    reads=["pA.b1"], writes=[f"ce{i}"])
            P.emit("sync", lambda h, t0=t0, t1=t1, ce=ce: h.dma_start(
                d_comb[:, t0:t1], ce[0:64, :]),
                reads=["ce0", "ce1"],
                writes=[f"combd{call}.{tcf}"], tag=f"combsp{call}.{tcf}")
            if d_dbg is not None:
                P.emit("sync", lambda h, tcf=tcf: h.dma_start(
                    d_dbg[:, tcf * 64:(tcf + 1) * 64].bitcast(BF16),
                    rs(812, 876).bitcast(BF16)),
                    reads=["cmb0", "cmb1"], writes=["dbgspill"],
                    tag=f"dbgsp{call}.{tcf}")
                P.emit("sync", lambda h, tcf=tcf: h.dma_start(
                    d_dbg[:, 1024 + tcf * 64:1024 + (tcf + 1) * 64],
                    rs(256, 320)),
                    reads=["sco0"], writes=["dbgspill"], tag="dbgsco0")
                P.emit("sync", lambda h, tcf=tcf: h.dma_start(
                    d_dbg[:, 2048 + tcf * 64:2048 + (tcf + 1) * 64],
                    rs(406, 470)),
                    reads=["sco1"], writes=["dbgspill"], tag="dbgsco1")
                P.emit("sync", lambda h, tcf=tcf: h.dma_start(
                    d_dbg[:, 3072 + tcf * 256:3072 + (tcf + 1) * 256],
                    rs(0, 256)),
                    reads=["lg"], writes=["dbgspill"], tag="dbglg")
                P.emit("sync", lambda h, tcf=tcf: h.dma_start(
                    d_dbg[:, 5120 + tcf * 256:5120 + (tcf + 1) * 256],
                    rs(256, 512)),
                    reads=["g30", "m8g0", "gm0", "t80", "ssum0", "sinv0"],
                    writes=["dbgspill"], tag="dbgblk0")
                P.emit("sync", lambda h, tcf=tcf: h.dma_start(
                    d_dbg[:, 7168 + tcf * 64:7168 + (tcf + 1) * 64],
                    rs(556, 620)),
                    reads=["ms0"], writes=["dbgspill"], tag="dbgms0")

    def ffn_sweep(call):
        def em_gu(eni, par, tcb):
            t0, t1 = tcb * TCF, (tcb + 1) * TCF
            sp = tcb % 2
            for ic in range(4):
                for hc in range(8):
                    P.emit("tensor", lambda h, ic=ic, hc=hc, par=par,
                           t0=t0, t1=t1: h.matmul(
                               pA[:, ic * 256:(ic + 1) * 256],
                               war(par)[:, hc * 512 + ic * 128:
                                        hc * 512 + (ic + 1) * 128],
                               xnbf_t(hc, t0, t1),
                               start=(hc == 0 and ic % 2 == 0),
                               stop=(hc == 7 and ic % 2 == 1),
                               skip_group_check=True),
                           reads=[f"war{par}", f"xnb.{hc}"],
                           writes=[f"pA.b{ic // 2}"],
                           tag=f"g{call}.{eni}.{tcb}")
            for ic in range(4):
                for hc in range(8):
                    P.emit("tensor", lambda h, ic=ic, hc=hc, par=par,
                           t0=t0, t1=t1: h.matmul(
                               pB[:, ic * 256:(ic + 1) * 256],
                               war(2 + par)[:, hc * 512 + ic * 128:
                                            hc * 512 + (ic + 1) * 128],
                               xnbf_t(hc, t0, t1),
                               start=(hc == 0 and ic % 2 == 0),
                               stop=(hc == 7 and ic % 2 == 1),
                               skip_group_check=True),
                           reads=[f"war{2 + par}", f"xnb.{hc}"],
                           writes=[f"pB.b{ic // 2}"],
                           tag=f"u{call}.{eni}.{tcb}")
            is_shared = eni < 2
            for ic in range(4):
                P.emit("scalar", lambda h, ic=ic: h.activation(
                    sg_t(ic), pA[:, ic * 256:(ic + 1) * 256], AF.Silu),
                    reads=[f"pA.b{ic // 2}"], writes=[f"sg{ic}"])
                if is_shared:
                    P.emit("vector", lambda h, ic=ic: h.tensor_copy(
                        uc_t(ic), pB[:, ic * 256:(ic + 1) * 256]),
                        reads=[f"pB.b{ic // 2}"], writes=[f"uc{ic}"])
                else:
                    P.emit("vector", lambda h, ic=ic, t0=t0, t1=t1:
                           h.tensor_tensor(
                               uc_t(ic), pB[:, ic * 256:(ic + 1) * 256],
                               combping()[:, t0:t1], ALU.mult),
                           reads=[f"pB.b{ic // 2}", "combL"],
                           writes=[f"uc{ic}"])
                P.emit("gpsimd", lambda h, ic=ic, sp=sp: h.tensor_tensor(
                    sh_t(ic, sp), sg_t(ic), uc_t(ic), ALU.mult),
                    reads=[f"sg{ic}", f"uc{ic}"], writes=[f"sh{sp}{ic}"])

        def em_ed(eni, tcb):
            sp = tcb % 2
            t0 = tcb * TCF
            grp_first = True
            grp_last = True
            for hc in range(8):
                for ic in range(4):
                    P.emit("tensor", lambda h, hc=hc, ic=ic, sp=sp: h.matmul(
                        pC[:, hc * 256:(hc + 1) * 256],
                        war(4)[:, ic * 1024 + hc * 128:
                               ic * 1024 + (hc + 1) * 128],
                        sh_t(ic, sp),
                        start=(grp_first and ic == 0 and hc % 2 == 0),
                        stop=(grp_last and ic == 3 and hc % 2 == 1),
                        skip_group_check=True),
                        reads=["war4", f"sh{sp}{ic}"],
                        writes=[f"pC.b{hc // 2}"],
                        tag=f"ed{call}.{eni}.{tcb}")
            if grp_last:
                P.emit("vector", lambda h, t0=t0: h.scalar_tensor_tensor(
                    R0.rearrange("p (hc t) -> p hc t", hc=8)[:, :, t0:t0 + TCF],
                    pC[:, :].rearrange("p (hc t) -> p hc t", hc=8), 0.5,
                    R0.rearrange("p (hc t) -> p hc t", hc=8)[:, :, t0:t0 + TCF],
                    ALU.mult, ALU.add),
                    reads=[f"pC.b{b}" for b in range(4)] +
                          [f"R0.{hc}" for hc in range(8)],
                    writes=[f"R0.{hc}" for hc in range(8)],
                    tag=f"facc{call}.{eni}.{tcb}")

        for eni in range(n_experts):
            par = eni % 2
            is_shared = eni < 2
            eidx = eni - 2
            P.emit("gpsimd_dma", lambda h, eni=eni, par=par: h.dma_start(
                war(par), d_wff[eni, 0]),
                writes=[f"war{par}"], tag=f"dmaeg{call}.{eni}")
            P.emit("gpsimd_dma", lambda h, eni=eni, par=par: h.dma_start(
                war(2 + par), d_wff[eni, 1]),
                writes=[f"war{2 + par}"], tag=f"dmaeu{call}.{eni}")
            P.emit("gpsimd_dma", lambda h, eni=eni: h.dma_start(
                war(4), d_wff[eni, 2]),
                writes=["war4"], tag=f"dmaed{call}.{eni}")
            if not is_shared:
                P.emit("gpsimd_dma", lambda h, eidx=eidx: h.dma_start(
                    combping()[:, :],
                    d_comb[eidx:eidx + 1, :].broadcast_to((128, T))),
                    reads=[f"combd{call}.{t}" for t in range(NTCF)],
                    writes=["combL"], tag=f"dmacomb{call}.{eidx}")
            for tcb in range(NTCF):
                em_gu(eni, par, tcb)
                if tcb > 0:
                    em_ed(eni, tcb - 1)
            em_ed(eni, NTCF - 1)

    def mamba_entry():
        for tcq in range(NTC):
            t0, t1 = tcq * TC, (tcq + 1) * TC
            layernorm(
                hch, [f"hch{hc}" for hc in range(8)],
                lambda hc: r0x(hc, t0, t1), [f"R0.{hc}" for hc in range(8)],
                TC, (mp(0), mp(1), mp(2)), ("mp0", "mp1", "mp2"))
            P.emit("sync", lambda h, t0=t0, t1=t1: h.dma_start(
                d_h[:, t0:t1].rearrange("(hc p) t -> p hc t", p=128),
                SCR[:, 0:4096].rearrange("p (hc t) -> p hc t", hc=8)),
                reads=[f"hch{hc}" for hc in range(8)],
                writes=[f"hd{tcq}"], tag=f"hsave{tcq}")
            P.emit("sync", lambda h, t0=t0, t1=t1: h.dma_start(
                d_x1[:, t0:t1].rearrange("(hc p) t -> p hc t", p=128),
                R0.rearrange("p (hc t) -> p hc t", hc=8)[:, :, t0:t1]),
                reads=[f"R0.{hc}" for hc in range(8)],
                writes=[f"x1d{tcq}"], tag=f"x1save{tcq}")

    def mamba_layer(l):
        P.emit("gpsimd_dma", lambda h, l=l: h.dma_start(
            s_wxp[:, :], d_wxp[:, l * 2048:(l + 1) * 2048]),
            writes=["wxp"], tag=f"dwxp{l}")
        P.emit("gpsimd_dma", lambda h, l=l: h.dma_start(
            s_wdt[:, :], d_wdt[:, l * 2048:(l + 1) * 2048]),
            writes=["wdt"], tag=f"dwdt{l}")
        P.emit("gpsimd_dma", lambda h, l=l: h.dma_start(
            s_alog[:, :], d_alog[:, l * 256:(l + 1) * 256]),
            writes=["alog"], tag=f"dalog{l}")
        P.emit("scalar", lambda h: h.activation(s_A[:, :], s_alog[:, :], AF.Exp),
               reads=["alog"], writes=["A"], tag=f"expA{l}")
        P.emit("vector", lambda h: h.tensor_scalar_mul(s_A[:, :], s_A[:, :],
                                                       -1.0),
               reads=["A"], writes=["A"], tag=f"negA{l}")
        P.emit("vector", lambda h: h.memset(XHALO[:, :], 0.0),
               writes=["xhalo"], tag=f"haloz{l}")

        for tcq in range(NTC):
            t0, t1 = tcq * TC, (tcq + 1) * TC
            P.emit("sync", lambda h, t0=t0, t1=t1: h.dma_start(
                SCR[:, 0:4096].rearrange("p (hc t) -> p hc t", hc=8),
                d_h[:, t0:t1].rearrange("(hc p) t -> p hc t", p=128)),
                reads=[(f"hwb{l - 1}.{tcq}" if l > 0 else f"hd{tcq}")],
                writes=[f"hch{hc}" for hc in range(8)], tag=f"hload{l}.{tcq}")
            for hc in range(8):
                sqb = mp(hc % 2).bitcast(BF16)[:, 0:TC]
                P.emit("vector", lambda h, hc=hc, sqb=sqb: h.tensor_mul(
                    sqb, hch(hc), hch(hc)),
                    reads=[f"hch{hc}"], writes=[f"mp{hc % 2}"])
                P.emit("tensor", lambda h, hc=hc, sqb=sqb: h.matmul(
                    pB[0:1, 0:TC], s_onesb[:, 0:1], sqb,
                    start=(hc == 0), stop=(hc == 7)),
                    reads=[f"mp{hc % 2}", "ones"], writes=["pB.b0"])
            P.emit("scalar", lambda h: h.activation(
                ROWS[0:1, 0:512], pB[0:1, 0:TC], AF.Ln, scale=1.0 / H,
                bias=EPS),
                reads=["pB.b0"], writes=["row_t"])
            P.emit("scalar", lambda h: h.activation(
                ROWS[0:1, 1536:2048], ROWS[0:1, 0:512], AF.Exp, scale=-0.5),
                reads=["row_t"], writes=["row_r"])
            P.emit("tensor", lambda h: h.matmul(
                pB[:, 512:1024], s_ones[0:1, 0:128], ROWS[0:1, 1536:2048],
                start=True, stop=True),
                reads=["row_r", "ones"], writes=["pB.b1"])
            for hc in range(8):
                P.emit("vector", lambda h, hc=hc, l=l: h.scalar_tensor_tensor(
                    xnm_t(hc), hch(hc), ncol(s_normw, l * 8 + hc),
                    pB[:, 512:1024], ALU.mult, ALU.mult),
                    reads=[f"hch{hc}", "pB.b1", "normw"], writes=[f"xnm{hc}"])
            def em_win_dma(piece):
                slot = piece % 3
                P.emit("gpsimd_dma", lambda h, l=l, piece=piece, slot=slot:
                       h.dma_start(war(slot),
                                   d_wwin[l, :, piece * 4096:(piece + 1) * 4096]),
                       writes=[f"war{slot}"], tag=f"dwin{l}.{tcq}.{piece}")

            em_win_dma(0)
            em_win_dma(1)
            for oc in range(32):
                if oc % 4 == 0 and oc // 4 + 2 < 8:
                    em_win_dma(oc // 4 + 2)
                slot = (oc // 4) % 3
                colb = (oc % 4) * 1024
                preg = f"pA.b{oc % 2}"
                for hc in range(8):
                    P.emit("tensor", lambda h, oc=oc, hc=hc, slot=slot,
                           colb=colb: h.matmul(
                               pA[:, (oc % 2) * 512:(oc % 2) * 512 + 512],
                               war(slot)[:, colb + hc * 128:
                                         colb + (hc + 1) * 128],
                               xnm_t(hc), start=(hc == 0), stop=(hc == 7)),
                           reads=[f"war{slot}", f"xnm{hc}"], writes=[preg],
                           tag=f"win{l}.{tcq}.{oc}")
                if oc < 16:
                    dc = oc
                    xb = xpb(dc % 2)
                    P.emit("vector", lambda h, xb=xb, oc=oc: h.tensor_copy(
                        xb[:, 3:515],
                        pA[:, (oc % 2) * 512:(oc % 2) * 512 + 512]),
                        reads=[preg], writes=[f"xp{dc % 2}"])
                    P.emit("vector", lambda h, xb=xb, dc=dc: h.tensor_copy(
                        xb[:, 0:3], XHALO[:, dc * 3:(dc + 1) * 3]),
                        reads=["xhalo"], writes=[f"xp{dc % 2}"])
                    P.emit("vector", lambda h, xb=xb, dc=dc, l=l: h.tensor_scalar(
                        mp(4), xb[:, 0:512],
                        ncol(s_convw, l * 64 + dc * 4 + 0), None, ALU.mult),
                        reads=[f"xp{dc % 2}", "convw"], writes=["mp4"])
                    P.emit("vector", lambda h, xb=xb, dc=dc, l=l:
                           h.scalar_tensor_tensor(
                               mp(5), xb[:, 1:513],
                               ncol(s_convw, l * 64 + dc * 4 + 1),
                               mp(4), ALU.mult, ALU.add),
                           reads=[f"xp{dc % 2}", "mp4", "convw"], writes=["mp5"])
                    P.emit("vector", lambda h, xb=xb, dc=dc, l=l:
                           h.scalar_tensor_tensor(
                               mp(4), xb[:, 2:514],
                               ncol(s_convw, l * 64 + dc * 4 + 2),
                               mp(5), ALU.mult, ALU.add),
                           reads=[f"xp{dc % 2}", "mp5", "convw"], writes=["mp4"])
                    P.emit("vector", lambda h, xb=xb, dc=dc, l=l:
                           h.scalar_tensor_tensor(
                               mp(5), xb[:, 3:515],
                               ncol(s_convw, l * 64 + dc * 4 + 3),
                               mp(4), ALU.mult, ALU.add),
                           reads=[f"xp{dc % 2}", "mp4", "convw"], writes=["mp5"])
                    P.emit("vector", lambda h, xb=xb, dc=dc: h.tensor_copy(
                        XHALO[:, dc * 3:(dc + 1) * 3], xb[:, 512:515]),
                        reads=[f"xp{dc % 2}"], writes=["xhalo"],
                        tag=f"halo{l}.{tcq}.{dc}")
                    P.emit("scalar", lambda h, dc=dc, l=l: h.activation(
                        xc_t(dc), mp(5), AF.Silu,
                        bias=ncol(s_convb, l * 16 + dc)),
                        reads=["mp5", "convb"], writes=[f"xc{dc}"],
                        tag=f"xcsilu{l}.{tcq}.{dc}")
                else:
                    dc2 = oc - 16
                    P.emit("scalar", lambda h, dc2=dc2, oc=oc: h.activation(
                        zs_t(dc2), pA[:, (oc % 2) * 512:(oc % 2) * 512 + 512],
                        AF.Silu),
                        reads=[preg], writes=[f"zs{dc2}"],
                        tag=f"zsilu{l}.{tcq}.{dc2}")
            for dc in range(16):
                P.emit("tensor", lambda h, dc=dc: h.matmul(
                    pC[:, 0:512], s_wxp[:, dc * 128:(dc + 1) * 128],
                    xc_t(dc), start=(dc == 0), stop=(dc == 15)),
                    reads=["wxp", f"xc{dc}"], writes=["pC.b0"],
                    tag=f"wx{l}.{tcq}")
            P.emit("vector", lambda h: h.tensor_copy(
                DTBC[0:16, :], pC[0:16, 0:512]),
                reads=["pC.b0"], writes=["dtbcC"])
            P.emit("vector", lambda h: h.tensor_copy(
                DTBC[32:48, :], pC[32:48, 0:512]),
                reads=["pC.b0"], writes=["dtbcB"])
            P.emit("vector", lambda h: h.tensor_copy(
                DTBC[64:128, :], pC[64:128, 0:512]),
                reads=["pC.b0"], writes=["dtbcD"])
            for n in range(16):
                pb = f"pC.b{1 + n % 2}"
                off = 512 + (n % 2) * 512
                P.emit("tensor", lambda h, n=n, off=off: h.matmul(
                    pC[:, off:off + 512],
                    s_oneh16[0:16, n:n + 1].broadcast_to((16, 128)),
                    DTBC[0:16, :], start=True, stop=True),
                    reads=["dtbcC", "oneh16"], writes=[pb], tag="cbcast")
                P.emit("vector", lambda h, n=n, off=off: h.tensor_copy(
                    crep_t(n), pC[:, off:off + 512]),
                    reads=[pb], writes=[f"crep{n}"])
                P.emit("tensor", lambda h, n=n, off=off: h.matmul(
                    pC[:, off:off + 512],
                    s_oneh16[32:48, n:n + 1].broadcast_to((16, 128)),
                    DTBC[32:48, :], start=True, stop=True),
                    reads=["dtbcB", "oneh16"], writes=[pb], tag="bbcast")
                P.emit("vector", lambda h, n=n, off=off: h.tensor_copy(
                    brep_t(n), pC[:, off:off + 512]),
                    reads=[pb], writes=[f"brep{n}"])
            for dc in range(16):
                pr = dc % 2
                pDt = f"pB.b{pr}"
                sS, sE, sL = (mp(4), mp(5), mp(6)) if pr == 0 else \
                             (mp(7), mp(8), mp(9))
                rS, rE, rL = (("mp4", "mp5", "mp6") if pr == 0 else
                              ("mp7", "mp8", "mp9"))
                delta, rD = mp(pr), f"mp{pr}"
                du, rU = mp(2 + pr), f"mp{2 + pr}"
                P.emit("tensor", lambda h, dc=dc, pr=pr: h.matmul(
                    pB[:, pr * 512: pr * 512 + 512],
                    s_wdt[64:128, dc * 128:(dc + 1) * 128],
                    DTBC[64:128, :], start=True, stop=True),
                    reads=["wdt", "dtbcD"], writes=[pDt],
                    tag=f"wdt{l}.{tcq}.{dc}")
                P.emit("scalar", lambda h, pr=pr, l=l, dc=dc, sS=sS:
                       h.activation(sS, pB[:, pr * 512: pr * 512 + 512],
                                    AF.Identity, bias=ncol(s_bdt, l * 16 + dc)),
                       reads=[pDt, "bdt"], writes=[rS])
                P.emit("scalar", lambda h, pr=pr, l=l, dc=dc, sE=sE:
                       h.activation(sE, pB[:, pr * 512: pr * 512 + 512],
                                    AF.Abs, bias=ncol(s_bdt, l * 16 + dc)),
                       reads=[pDt, "bdt"], writes=[rE])
                P.emit("scalar", lambda h, sE=sE: h.activation(
                    sE, sE, AF.Exp, scale=-1.0), reads=[rE], writes=[rE])
                P.emit("scalar", lambda h, sE=sE, sL=sL: h.activation(
                    sL, sE, AF.Ln, bias=1.0), reads=[rE], writes=[rL])
                P.emit("vector", lambda h, sS=sS: h.tensor_scalar_max(
                    sS, sS, 0.0), reads=[rS], writes=[rS])
                P.emit("vector", lambda h, sS=sS, sL=sL, delta=delta:
                       h.tensor_add(delta, sS, sL),
                       reads=[rS, rL], writes=[rD])
                P.emit("vector", lambda h, delta=delta, dc=dc, du=du:
                       h.tensor_tensor(du, delta, xc_t(dc), ALU.mult),
                       reads=[rD, f"xc{dc}"], writes=[rU])
                py = f"pC.b{pr}"
                pyo = pr * 512
                P.emit("vector", lambda h, dc=dc, l=l, pyo=pyo:
                       h.tensor_scalar(pC[:, pyo:pyo + 512], xc_t(dc),
                                       ncol(s_dpv, l * 16 + dc), None,
                                       ALU.mult),
                       reads=[f"xc{dc}", "dpv"], writes=[py],
                       tag=f"yinit{l}.{tcq}.{dc}")
                def em_yacc(n):
                    npr = n % 2
                    ch, rC = mp(12 + npr), f"mp{12 + npr}"
                    P.emit("vector", lambda h, ch=ch, pyo=pyo:
                           h.tensor_tensor(pC[:, pyo:pyo + 512],
                                           pC[:, pyo:pyo + 512], ch, ALU.add),
                           reads=[rC, py], writes=[py],
                           tag=f"yacc{l}.{tcq}.{dc}.{n}")

                for n in range(16):
                    npr = n % 2
                    dA = f"pA.b{npr}"
                    dAo = npr * 512
                    dbu, rB = mp(10 + npr), f"mp{10 + npr}"
                    ch, rC = mp(12 + npr), f"mp{12 + npr}"
                    P.emit("scalar", lambda h, n=n, dc=dc, delta=delta,
                           dAo=dAo: h.activation(
                               pA[:, dAo:dAo + 512], delta, AF.Exp,
                               scale=ncol(s_A, dc * 16 + n)),
                           reads=[rD, "A"], writes=[dA],
                           tag=f"dA{l}.{tcq}.{dc}.{n}")
                    P.emit("gpsimd", lambda h, du=du, n=n, dbu=dbu:
                           h.tensor_tensor(dbu, du, brep_t(n), ALU.mult),
                           reads=[rU, f"brep{n}"], writes=[rB])
                    P.emit("vector", lambda h, dbu=dbu, dc=dc, n=n, tcq=tcq,
                           dAo=dAo: h.tensor_tensor_scan(
                               dbu, pA[:, dAo:dAo + 512], dbu,
                               (0.0 if tcq == 0 else
                                ncol(CARRY, dc * 16 + n)),
                               ALU.mult, ALU.add),
                           reads=([dA, rB] +
                                  ([f"carry.{dc}.{n}!"] if tcq > 0 else [])),
                           writes=[rB],
                           tag=f"scan{l}.{tcq}.{dc}.{n}")
                    P.emit("vector", lambda h, dbu=dbu, dc=dc, n=n:
                           h.tensor_copy(ncol(CARRY, dc * 16 + n),
                                         dbu[:, 511:512]),
                           reads=[rB], writes=[f"carry.{dc}.{n}"])
                    if n > 0:
                        pn = n - 1
                        pdbu, prB = mp(10 + pn % 2), f"mp{10 + pn % 2}"
                        pch, prC = mp(12 + pn % 2), f"mp{12 + pn % 2}"
                        P.emit("gpsimd", lambda h, pdbu=pdbu, pn=pn, pch=pch:
                               h.tensor_tensor(pch, pdbu, crep_t(pn),
                                               ALU.mult),
                               reads=[prB, f"crep{pn}"], writes=[prC])
                        em_yacc(pn)
                P.emit("gpsimd", lambda h: h.tensor_tensor(
                    mp(13), mp(11), crep_t(15), ALU.mult),
                    reads=["mp11", "crep15"], writes=["mp13"])
                em_yacc(15)
                P.emit("vector", lambda h, dc=dc, pyo=pyo: h.tensor_tensor(
                    xc_t(dc), pC[:, pyo:pyo + 512], zs_t(dc), ALU.mult),
                    reads=[py, f"zs{dc}"], writes=[f"ybf{dc}", f"xc{dc}"],
                    tag=f"ygate{l}.{tcq}.{dc}")
            def em_wblk_dma(piece):
                slot = 3 + piece % 2
                P.emit("gpsimd_dma", lambda h, l=l, piece=piece, slot=slot:
                       h.dma_start(war(slot),
                                   d_wblk[l, :, piece * 4096:(piece + 1) * 4096]),
                       writes=[f"war{slot}"], tag=f"dwblk{l}.{tcq}.{piece}")

            em_wblk_dma(0)
            for hc in range(8):
                if hc % 2 == 0 and hc // 2 + 1 < 4:
                    em_wblk_dma(hc // 2 + 1)
                ppr = f"pC.b{2 + hc % 2}"
                poff = 1024 + (hc % 2) * 512
                piece = hc // 2
                slot = 3 + piece % 2
                for dc in range(16):
                    inner = (hc % 2) * 2048 + dc * 128
                    P.emit("tensor", lambda h, hc=hc, dc=dc, slot=slot,
                           inner=inner, poff=poff: h.matmul(
                               pC[:, poff:poff + 512],
                               war(slot)[:, inner:inner + 128],
                               xc_t(dc), start=(dc == 0), stop=(dc == 15)),
                           reads=[f"war{slot}", f"ybf{dc}"], writes=[ppr],
                           tag=f"wblk{l}.{tcq}.{hc}")
                P.emit("vector", lambda h, hc=hc, poff=poff: h.tensor_tensor(
                    hch(hc), hch(hc), pC[:, poff:poff + 512], ALU.add),
                    reads=[ppr, f"hch{hc}"], writes=[f"hch{hc}"],
                    tag=f"hres{l}.{tcq}.{hc}")
            P.emit("sync", lambda h, t0=t0, t1=t1: h.dma_start(
                d_h[:, t0:t1].rearrange("(hc p) t -> p hc t", p=128),
                SCR[:, 0:4096].rearrange("p (hc t) -> p hc t", hc=8)),
                reads=[f"hch{hc}" for hc in range(8)],
                writes=[f"hwb{l}.{tcq}"], tag=f"hstore{l}.{tcq}")

    def wout_phase():
        P.emit("gpsimd_dma", lambda h: h.dma_start(
            war(0), d_wout[:, 0:4096]), writes=["war0"], tag="dwo0")
        P.emit("gpsimd_dma", lambda h: h.dma_start(
            war(1), d_wout[:, 4096:8192]), writes=["war1"], tag="dwo1")
        for tcq in range(NTC):
            t0, t1 = tcq * TC, (tcq + 1) * TC
            P.emit("sync", lambda h, t0=t0, t1=t1: h.dma_start(
                SCR[:, 0:4096].rearrange("p (hc t) -> p hc t", hc=8),
                d_h[:, t0:t1].rearrange("(hc p) t -> p hc t", p=128)),
                reads=[f"hwb{NL - 1}.{tcq}"],
                writes=[f"hch{hc}" for hc in range(8)], tag=f"hload4.{tcq}")
            P.emit("sync", lambda h, t0=t0, t1=t1: h.dma_start(
                R0.rearrange("p (hc t) -> p hc t", hc=8)[:, :, t0:t1],
                d_x1[:, t0:t1].rearrange("(hc p) t -> p hc t", p=128)),
                reads=[f"x1d{tcq}"],
                writes=[f"R0.{hc}" for hc in range(8)], tag=f"x1load{tcq}")
            for hc in range(8):
                P.emit("vector", lambda h, hc=hc: h.tensor_copy(
                    xnm_t(hc), hch(hc)),
                    reads=[f"hch{hc}"], writes=[f"xnm{hc}"])
            for oc in range(8):
                pr = f"pA.b{oc % 2}"
                poff = (oc % 2) * 512
                for hc in range(8):
                    P.emit("tensor", lambda h, oc=oc, hc=hc, poff=poff:
                           h.matmul(
                               pA[:, poff:poff + 512],
                               war(oc // 4)[:, (oc % 4) * 1024 + hc * 128:
                                            (oc % 4) * 1024 + (hc + 1) * 128],
                               xnm_t(hc), start=(hc == 0), stop=(hc == 7)),
                           reads=[f"war{oc // 4}", f"xnm{hc}"], writes=[pr],
                           tag=f"wout{tcq}.{oc}")
                P.emit("scalar", lambda h, oc=oc, poff=poff: h.activation(
                    mp(10), pA[:, poff:poff + 512], AF.Identity,
                    bias=ncol(s_bout, oc)),
                    reads=[pr, "bout"], writes=["mp10"])
                P.emit("vector", lambda h, oc=oc, t0=t0, t1=t1: h.tensor_add(
                    r0x(oc, t0, t1), r0x(oc, t0, t1), mp(10)),
                    reads=["mp10", f"R0.{oc}"], writes=[f"R0.{oc}"],
                    tag=f"x2{tcq}.{oc}")

    def dbg_store():
        pass

    def final_store():
        dbg_store()
        P.emit("sync", lambda h: h.dma_start(
            d_out.rearrange("(hc p) t -> p hc t", p=128),
            R0.rearrange("p (hc t) -> p hc t", hc=8)),
            reads=[f"R0.{hc}" for hc in range(8)], writes=["outd"],
            tag="outstore")

    ffn_prologue(1)
    ffn_sweep(1)
    if stop_stage == "ffn1":
        final_store()
    else:
        P.barrier("postffn1")
        mamba_entry()
        P.barrier("postentry")
        nlayers = NL
        if stop_stage and stop_stage.startswith("mamba"):
            nlayers = int(stop_stage[5:])
        for l in range(nlayers):
            mamba_layer(l)
        if stop_stage and stop_stage.startswith("mamba"):
            for tcq in range(NTC):
                t0, t1 = tcq * TC, (tcq + 1) * TC
                P.emit("sync", lambda h, t0=t0, t1=t1:
                       h.dma_start(d_out[:, t0:t1], d_h[:, t0:t1]),
                       reads=[(f"hwb{nlayers - 1}.{tcq}" if nlayers > 0 else
                               f"hd{tcq}")],
                       writes=["outd"], tag=f"hdump{tcq}")
        else:
            P.barrier("postmamba")
            wout_phase()
            if stop_stage == "wout":
                final_store()
            else:
                P.barrier("postwout")
                ffn_prologue(2)
                ffn_sweep(2)
                final_store()

    P.materialize(block, sems)
    stack.close()
    return nc


def _pack_inputs(inputs):
    f = np.float32
    ins = {k: np.asarray(v) for k, v in inputs.items()}

    def pack_gu(w):
        return np.ascontiguousarray(
            w.reshape(8, 128, 4, 128).transpose(1, 0, 2, 3).reshape(128, 4096))

    def pack_ed(w):
        return np.ascontiguousarray(
            w.reshape(4, 128, 8, 128).transpose(1, 0, 2, 3).reshape(128, 4096))

    sgT = np.ascontiguousarray(ins["L0_ffn_sg_w"].T.astype(f))
    suT = np.ascontiguousarray(ins["L0_ffn_su_w"].T.astype(f))
    sdT = np.ascontiguousarray(ins["L0_ffn_sd_w"].T.astype(f))
    wff = np.empty((NE, 3, 128, 4096), dtype=f)
    wff[0, 0] = pack_gu(sgT[:, 0:512])
    wff[0, 1] = pack_gu(suT[:, 0:512])
    wff[0, 2] = pack_ed(sdT[0:512, :])
    wff[1, 0] = pack_gu(sgT[:, 512:1024])
    wff[1, 1] = pack_gu(suT[:, 512:1024])
    wff[1, 2] = pack_ed(sdT[512:1024, :])
    egT = ins["L0_ffn_eg_w"].transpose(0, 2, 1)
    euT = ins["L0_ffn_eu_w"].transpose(0, 2, 1)
    edT = ins["L0_ffn_ed_w"].transpose(0, 2, 1)
    for eidx in range(E):
        wff[2 + eidx, 0] = pack_gu(egT[eidx])
        wff[2 + eidx, 1] = pack_gu(euT[eidx])
        wff[2 + eidx, 2] = pack_ed(edT[eidx])

    gateT = np.ascontiguousarray(
        ins["L0_ffn_gate_w"].T.reshape(8, 128, E).transpose(1, 0, 2)
        .reshape(128, 8 * E).astype(f))
    gbias = ins["L0_ffn_gate_bias"].reshape(1, E).astype(f)

    Win = ins["L0_attn_Win"].astype(f)
    wwin = np.ascontiguousarray(
        Win.reshape(NL, 8, 128, 32, 128).transpose(0, 2, 3, 1, 4)
        .reshape(NL, 128, 32768))
    Wblk = ins["L0_attn_Wblk"].astype(f)
    wblk = np.ascontiguousarray(
        Wblk.reshape(NL, 16, 128, 8, 128).transpose(0, 2, 3, 1, 4)
        .reshape(NL, 128, 16384))
    Wx = ins["L0_attn_Wx"].astype(f)
    WxP = np.zeros((NL, DI, 128), dtype=f)
    WxP[:, :, 0:16] = Wx[:, :, 80:96]
    WxP[:, :, 32:48] = Wx[:, :, 64:80]
    WxP[:, :, 64:128] = Wx[:, :, 0:64]
    wxp = np.ascontiguousarray(
        WxP.reshape(NL, 16, 128, 128).transpose(2, 0, 1, 3)
        .reshape(128, NL * 2048))
    Wdt = ins["L0_attn_Wdt"].astype(f)
    wdt = np.zeros((128, NL * 2048), dtype=f)
    for l in range(NL):
        wdt[64:128, l * 2048:(l + 1) * 2048] = Wdt[l]
    Wout = ins["L0_attn_Wout"].astype(f)
    woutp = np.ascontiguousarray(
        Wout.reshape(8, 128, 8, 128).transpose(1, 2, 0, 3).reshape(128, 8192))

    def cols(a):
        return np.ascontiguousarray(
            a.reshape(NL, 16, 128).transpose(2, 0, 1).reshape(128, NL * 16)
            .astype(f))

    normw = np.ascontiguousarray(
        ins["L0_attn_norm_w"].reshape(NL, 8, 128).transpose(2, 0, 1)
        .reshape(128, NL * 8).astype(f))
    convw = np.ascontiguousarray(
        ins["L0_attn_conv_w"].reshape(NL, 16, 128, KC).transpose(2, 0, 1, 3)
        .reshape(128, NL * 64).astype(f))
    convb = cols(ins["L0_attn_conv_b"])
    bdt = cols(ins["L0_attn_bdt"])
    dpv = cols(ins["L0_attn_Dp"])
    alog = np.ascontiguousarray(
        ins["L0_attn_A_log"].reshape(NL, 16, 128, DS).transpose(2, 0, 1, 3)
        .reshape(128, NL * 256).astype(f))
    boutc = np.ascontiguousarray(
        ins["L0_attn_bout"].reshape(8, 128).T.astype(f))

    ident = np.eye(128, dtype=f)
    oneh = np.eye(E, dtype=f)
    oneh16 = np.zeros((48, 16), dtype=f)
    oneh16[0:16] = np.eye(16)
    oneh16[32:48] = np.eye(16)

    common = dict(wff=wff, gateT=gateT, gbias=gbias, wwin=wwin, wblk=wblk,
                  wxp=wxp, wdt=wdt, woutp=woutp, normw=normw, convw=convw,
                  convb=convb, bdt=bdt, dpv=dpv, alog=alog, boutc=boutc,
                  ident=ident, oneh=oneh, oneh16=oneh16)
    xs = [np.ascontiguousarray(ins["L0_attn_x"][b].T.astype(f))
          for b in range(B)]
    return common, xs


def kernel(**inputs):
    common, xs = _pack_inputs(inputs)
    nc = build_program()
    in_maps = [dict(common, x=xs[b]) for b in range(B)]
    res = run_bass_kernel_spmd(nc, in_maps, core_ids=list(range(B)))
    out = np.stack([np.asarray(r["out"]).T for r in res.results], axis=0)
    return np.ascontiguousarray(out.astype(np.float32))


if __name__ == "__main__":
    import sys
    nc = build_program(sys.argv[1] if len(sys.argv) > 1 else None)
    print("built ok")
